# version 06
# baseline (speedup 1.0000x reference)
import numpy as np
import ml_dtypes
from contextlib import ExitStack

import concourse.bass as bass
import concourse.mybir as mybir
from concourse.bass import IndirectOffsetOnAxis
from concourse.bass_utils import run_bass_kernel_spmd

F32 = mybir.dt.float32
BF16 = mybir.dt.bfloat16
I32 = mybir.dt.int32
AF = mybir.ActivationFunctionType
ALU = mybir.AluOpType
AX = mybir.AxisListType.X

B, T, D, H, DH = 4, 4096, 1024, 16, 64
TO = T // 2
NT, NTO = T // 128, TO // 128
ND = D // 128
CEXP = 20.0
EPS = 1e-5
FF = 2048
NE = 9
SLOPES = [2.0 ** (-(8.0 / H) * (i + 1)) for i in range(H)]
GW = 4992
GOFF = 896
SKIP_GAP = 110.0
UCOLS = 33024


class Prog:
    def __init__(self, nc, st):
        self.nc = nc
        self.st = st
        self.e = {"pe": nc.tensor, "act": nc.scalar, "dve": nc.vector,
                  "pool": nc.gpsimd, "sync": nc.sync}
        self.sem = {}
        for sname in ("pe", "act", "dve", "pool"):
            self.sem[sname] = st.enter_context(nc.semaphore("sem_" + sname))
        self.cnt = {sname: 0 for sname in self.sem}
        self.waited = {sname: {} for sname in
                       ("pe", "act", "dve", "pool", "sync")}
        self.buf = {}

    def _dma_sem(self, key):
        name = "d_" + key
        if name not in self.sem:
            self.sem[name] = self.st.enter_context(
                self.nc.semaphore("sem_" + name))
            self.cnt[name] = 0
        return name

    def wait(self, stream, tok):
        if tok is None:
            return
        chan, v = tok
        if stream == "pe" and chan == "pe":
            return
        if v <= self.waited[stream].get(chan, 0):
            return
        self.e[stream].wait_ge(self.sem[chan], v)
        self.waited[stream][chan] = v

    def op(self, stream, fn, reads=(), writes=(), dma=False, deps=()):
        for t in deps:
            self.wait(stream, t)
        for k in reads:
            sb_ = self.buf.get(k)
            if sb_ and sb_["w"]:
                self.wait(stream, sb_["w"])
        for k in writes:
            sb_ = self.buf.get(k)
            if sb_:
                if sb_["w"]:
                    self.wait(stream, sb_["w"])
                for t in sb_["r"]:
                    self.wait(stream, t)
        ins = fn(self.e[stream])
        if dma:
            assert writes, "DMA ops must declare a write key"
            chan = self._dma_sem(stream + "_" + writes[0])
            inc = 16
        else:
            chan = stream
            inc = 1
        ins.then_inc(self.sem[chan], inc)
        self.cnt[chan] += inc
        tok = (chan, self.cnt[chan])
        for k in reads:
            self.buf.setdefault(k, {"w": None, "r": []})["r"].append(tok)
        for k in writes:
            self.buf[k] = {"w": tok, "r": []}
        return tok

    def barrier(self, keys, streams=("pe", "act", "dve", "pool", "sync")):
        best = {}
        for k in keys:
            sb_ = self.buf.get(k)
            if not sb_:
                continue
            for t in ([sb_["w"]] if sb_["w"] else []) + sb_["r"]:
                c, v = t
                if best.get(c, 0) < v:
                    best[c] = v
        for stream in streams:
            for c, v in best.items():
                self.wait(stream, (c, v))

    def finish(self):
        for chan in list(self.cnt):
            if self.cnt[chan]:
                self.wait("sync", (chan, self.cnt[chan]))


ATTN_KEYS = ["qkband", "qb", "vb", "gt", "sq", "pt0", "pt1", "pt2", "wqkb0",
             "wqkb1", "wvb", "bvb", "rrow", "rrowb", "rb", "ob0", "ob1",
             "kb", "qstage"]
ASM_KEYS = ["wob", "a0b", "b0b", "hw0", "hw0t", "bpjb", "zrow", "wg0b",
            "xnb", "sq1k", "cw", "gbuf", "eq1", "eq2", "w13_0", "w13_1",
            "w2_0", "w2_1", "s1b", "hwb", "hwtb", "wg1b", "wcols"]


def uv(b, part0, nparts, col0, ncols, dtype=BF16):
    u = b["U"][:]
    ap = bass.AP(u.tensor, part0 * UCOLS + col0, [[UCOLS, nparts], [1, ncols]])
    if dtype == F32:
        ap = ap.bitcast(F32)
    return ap


def alloc_bufs(nc, st, phase):
    b = {}

    def sb(name, shape, dt):
        b[name] = st.enter_context(nc.sbuf_tensor(name, shape, dt))
    sb("xnT", [128, ND * T], BF16)
    if phase == "l1":
        sb("ownT", [128, ND * TO], BF16)
        sb("oT", [128, ND * TO], BF16)
    else:
        sb("oT", [128, ND * T], BF16)
    if phase == "fused":
        ot = b["oT"][:]
        b["ownT"] = bass.AP(ot.tensor, ot.offset + ND * TO,
                            [[ot.ap[0][0], 128], [1, ND * TO]])
        sb("idxTs", [128, 8], I32)
        sb("idxXs", [128, 16], I32)
        b["idxT"], b["idxX"] = b["idxTs"], b["idxXs"]
    sb("U", [128, UCOLS], BF16)
    sb("xt0", [128, 1024], F32)
    sb("xt1", [128, 1024], F32)
    sb("stats", [128, 24], F32)
    sb("mx", [128, 4], F32)
    sb("mxb", [128, 1], BF16)
    sb("mrow", [1, 8], F32)
    sb("mrowb", [1, 1], BF16)
    sb("cm", [128, 1], F32)
    sb("ident", [128, 128], BF16)
    sb("ones128", [128, 1], BF16)
    sb("onesr", [65, 128], BF16)
    sb("bvr", [1, 256], BF16)
    sb("bqk", [128, 16], F32)
    sb("konst", [128, 2], F32)
    nc.const_aps.register(F32, EPS, b["konst"][:, 0:1])
    nc.const_aps.register(F32, 1e-30, b["konst"][:, 1:2])
    b["kb"] = uv(b, 0, 64, 0, 4096)
    b["qstage"] = uv(b, 64, 64, 0, 4096)
    b["qb"] = uv(b, 0, 64, 4096, 4096)
    b["rrow"] = uv(b, 64, 1, 4096, 2048, F32)
    b["rrowb"] = uv(b, 64, 1, 6144, 1024)
    b["vb"] = uv(b, 0, 128, 8192, 8320)
    b["gt"] = uv(b, 0, 128, 16512, GW)
    b["sq"] = uv(b, 0, 64, 21504, 1024)
    b["pt0"] = uv(b, 0, 128, 22528, 1024)
    b["pt1"] = uv(b, 0, 128, 23552, 1024)
    b["pt2"] = uv(b, 0, 128, 24576, 1024)
    b["wqkb0"] = uv(b, 0, 128, 25600, 1024)
    b["wqkb1"] = uv(b, 0, 128, 26624, 1024)
    b["wvb"] = uv(b, 0, 128, 27648, 2048)
    b["bvb"] = uv(b, 0, 128, 29696, 256)
    b["rb"] = uv(b, 0, 64, 29952, 1024)
    b["ob0"] = uv(b, 0, 64, 30976, 1024)
    b["ob1"] = uv(b, 0, 64, 32000, 1024)
    if phase in ("l0", "fused"):
        b["wob"] = uv(b, 0, 128, 0, 8192)
        b["a0b"] = uv(b, 0, 128, 8192, 1024)
        b["b0b"] = uv(b, 0, 128, 9216, 1024)
        b["hw0"] = uv(b, 0, 128, 10240, 128)
        b["hw0t"] = uv(b, 0, 128, 10368, 128)
        b["bpjb"] = uv(b, 0, 1, 10496, 1024)
        b["zrow"] = uv(b, 0, 1, 11520, 512)
        b["wg0b"] = uv(b, 0, 128, 12032, 64)
        b["xnb"] = uv(b, 0, 128, 12096, 1024)
        b["sq1k"] = uv(b, 0, 128, 13120, 1024)
        b["cw"] = uv(b, 0, 128, 14144, 32, F32)
        b["gbuf"] = uv(b, 0, 128, 14176, 32, F32)
        b["eq1"] = uv(b, 0, 128, 14208, 32, F32)
        b["eq2"] = uv(b, 0, 128, 14240, 32, F32)
    if phase in ("l1", "fused"):
        b["wob"] = uv(b, 0, 128, 0, 8192)
        b["w13b0"] = uv(b, 0, 128, 0, 8192)
        b["w13b1"] = uv(b, 0, 128, 8192, 8192)
        b["w2b0"] = uv(b, 0, 128, 16384, 4096)
        b["w2b1"] = uv(b, 0, 128, 20480, 4096)
        b["s1b"] = uv(b, 0, 128, 24576, 512)
        b["hwb"] = uv(b, 0, 128, 25088, 512)
        b["hwtb"] = uv(b, 0, 128, 25600, 512)
        b["wg1b"] = uv(b, 0, 128, 26112, 64)
        b["wcols"] = uv(b, 0, 128, 26176, 288, F32)
        if phase == "l1":
            b["cw"] = uv(b, 0, 128, 28512, 32, F32)
            b["gbuf"] = uv(b, 0, 128, 28544, 32, F32)
            b["eq1"] = uv(b, 0, 128, 28576, 32, F32)
            b["eq2"] = uv(b, 0, 128, 28608, 32, F32)
            b["xnb"] = uv(b, 0, 128, 26464, 1024)
            b["sq1k"] = uv(b, 0, 128, 27488, 1024)
    b["psA"] = st.enter_context(nc.psum_tensor("psA", [128, 1024], F32))
    b["psB"] = st.enter_context(nc.psum_tensor("psB", [128, 1024], F32))
    b["psC"] = st.enter_context(nc.psum_tensor("psC", [128, 1024], F32))
    b["psD"] = st.enter_context(nc.psum_tensor("psD", [128, 512], F32))
    b["psT"] = st.enter_context(nc.psum_tensor("psT", [128, 1024], BF16))
    return b


def emit_consts(p, b):
    import struct
    r0 = p.nc.gpsimd.alloc_register("fill0")
    p.nc.gpsimd.reg_mov(r0, 0)
    rn = p.nc.gpsimd.alloc_register("filln")
    p.nc.gpsimd.reg_mov(rn, struct.unpack("<i", struct.pack("<f", -1e30))[0])
    b["fill0"], b["filln"] = r0, rn
    p.op("pool", lambda e: e.memset(b["ident"][:], 1.0), writes=["ident"])
    p.op("pool", lambda e: e.affine_select(
        b["ident"][:], b["ident"][:], [[-1, 128]], ALU.is_equal, b["fill0"],
        channel_multiplier=1), reads=["ident"], writes=["ident"])
    p.op("pool", lambda e: e.memset(b["ones128"][:], 1.0), writes=["ones128"])
    p.op("pool", lambda e: e.memset(b["onesr"][0:65, :], 1.0), writes=["onesr"])
    p.op("pool", lambda e: e.memset(b["konst"][:, 0:1], EPS), writes=["konst"])
    p.op("pool", lambda e: e.memset(b["konst"][:, 1:2], 1e-30),
         reads=["konst"], writes=["konst"])


def emit_vb_ones(p, b):
    p.op("pool", lambda e: e.memset(b["vb"][:], 0.0), writes=["vb"])
    vb = b["vb"].rearrange("p (k h c) -> p k h c", k=NT, h=4)
    p.op("pool", lambda e: e.memset(vb[:, :, :, 64:65], 1.0),
         reads=["vb"], writes=["vb"])


def _ln_tile(p, b, xt, xk, i, dst, dst_key, dstT):
    st = b["stats"]
    p.op("dve", lambda e: e.reduce_sum(st[:, 0:1], xt[:], axis=AX),
         reads=[xk], writes=["st0"])
    p.op("dve", lambda e: e.scalar_tensor_tensor(
        b["sq1k"], xt[:], 1.0, xt[:], ALU.mult, ALU.mult,
        accum_out=st[:, 1:2]), reads=[xk], writes=["st1", "sq1k"])
    p.op("dve", lambda e: e.tensor_scalar_mul(st[:, 2:3], st[:, 0:1], 1.0 / D),
         reads=["st0"], writes=["st2"])
    p.op("dve", lambda e: e.scalar_tensor_tensor(
        st[:, 3:4], st[:, 2:3], -1.0, st[:, 2:3], ALU.mult, ALU.mult),
        reads=["st2"], writes=["st3"])
    p.op("dve", lambda e: e.scalar_tensor_tensor(
        st[:, 4:5], st[:, 1:2], 1.0 / D, st[:, 3:4], ALU.mult, ALU.add),
        reads=["st1", "st3"], writes=["st4"])
    p.op("act", lambda e: e.activation(st[:, 5:6], st[:, 4:5], AF.Ln,
                                       bias=EPS),
         reads=["st4", "konst"], writes=["st5"])
    p.op("act", lambda e: e.activation(st[:, 6:7], st[:, 5:6], AF.Exp,
                                       scale=-0.5),
         reads=["st5"], writes=["st6"])
    p.op("dve", lambda e: e.tensor_scalar(
        b["xnb"], xt[:], st[:, 2:3], st[:, 6:7], ALU.subtract, ALU.mult),
        reads=[xk, "st2", "st6"], writes=["xnb"])
    for j in range(ND):
        p.op("pe", lambda e, j=j: e.transpose(
            b["psT"][:, j * 128:(j + 1) * 128],
            b["xnb"][:, j * 128:(j + 1) * 128], b["ident"][:]),
            reads=["xnb", "ident"], writes=["psT"])
    dv = dst[:]
    dst_ap = bass.AP(dv.tensor, dv.offset + i * 128,
                     [[dv.ap[0][0], 128], [dstT, ND], [1, 128]])
    p.op("dve", lambda e: e.tensor_copy(
        dst_ap, b["psT"][:].rearrange("p (j c) -> p j c", j=ND)),
        reads=["psT"], writes=[dst_key])


def emit_lnT(p, b, x_dram, ntiles, dst, dst_key, dstT):
    for i in range(ntiles):
        par = i % 2
        xt = b["xt0"] if par == 0 else b["xt1"]
        xk = "xt%d" % par
        p.op("sync", lambda e, i=i, xt=xt: e.dma_start(
            out=xt[:], in_=x_dram[i * 128:(i + 1) * 128, :]),
            writes=[xk], dma=True)
        _ln_tile(p, b, xt, xk, i, dst, dst_key, dstT)


def emit_qk_head(p, b, h, wqk_dram, has_bias, Tq, q_src_key=None):
    wk = "wqkb%d" % (h % 2)
    wb = b[wk]
    wslc = wqk_dram[:, h * 128:(h + 1) * 128]
    p.op("pool", lambda e: e.dma_start(
        out=wb.rearrange("p (ch c) -> p ch c", ch=ND),
        in_=wslc.rearrange("(ch p) c -> p ch c", p=128)),
        writes=[wk], dma=True)
    bk = b["bqk"][0:64, h:h + 1] if has_bias else 0.0
    bq = b["bqk"][64:128, h:h + 1] if has_bias else 0.0
    rds = ["bqk"] if has_bias else []
    nq = Tq // 512
    for tc in range(T // 512):
        bank = "psA" if tc % 2 == 0 else "psB"
        ps = b[bank]
        src = b["xnT"] if q_src_key is None else None
        for ch in range(ND):
            p.op("pe", lambda e, ch=ch, tc=tc, ps=ps: e.matmul(
                ps[:, 0:512], wb[:, ch * 128:(ch + 1) * 128],
                b["xnT"][:, ch * T + tc * 512: ch * T + tc * 512 + 512],
                start=(ch == 0), stop=(ch == ND - 1)),
                reads=[wk, "xnT"], writes=[bank])
        p.op("dve", lambda e, tc=tc, ps=ps: e.tensor_scalar(
            b["kb"][0:64, tc * 512:(tc + 1) * 512], ps[0:64, 0:512],
            bk, None, ALU.add),
            reads=[bank] + rds, writes=["kb"])
        if q_src_key is None and tc < nq:
            p.op("dve", lambda e, tc=tc, ps=ps: e.tensor_scalar(
                b["qstage"][0:64, tc * 512:(tc + 1) * 512], ps[64:128, 0:512],
                bq, 0.125, ALU.add, ALU.mult),
                reads=[bank] + rds, writes=["qstage"])
    if q_src_key is not None:
        qsrc = b[q_src_key]
        for tc in range(nq):
            bank = "psA" if tc % 2 == 0 else "psB"
            ps = b[bank]
            for ch in range(ND):
                p.op("pe", lambda e, ch=ch, tc=tc, ps=ps: e.matmul(
                    ps[:, 0:512], wb[:, ch * 128:(ch + 1) * 128],
                    qsrc[:, ch * TO + tc * 512: ch * TO + tc * 512 + 512],
                    start=(ch == 0), stop=(ch == ND - 1)),
                    reads=[wk, q_src_key], writes=[bank])
            p.op("dve", lambda e, tc=tc, ps=ps: e.tensor_scalar(
                b["qstage"][0:64, tc * 512:(tc + 1) * 512], ps[64:128, 0:512],
                bq, 0.125, ALU.add, ALU.mult),
                reads=[bank] + rds, writes=["qstage"])
    p.op("sync", lambda e: e.dma_start(out=b["qb"][0:64, 0:Tq],
                                       in_=b["qstage"][0:64, 0:Tq]),
         reads=["qstage"], writes=["qb"], dma=True)


def emit_v_group(p, b, hg, wv_dram, bv_dram):
    p.op("pool", lambda e: e.dma_start(
        out=b["wvb"].rearrange("p (ch c) -> p ch c", ch=ND),
        in_=wv_dram[:, hg * 256:(hg + 1) * 256].rearrange(
            "(ch p) c -> p ch c", p=128)),
        writes=["wvb"], dma=True)
    have_bias = bv_dram is not None
    if have_bias:
        p.op("pool", lambda e: e.dma_start(
            out=b["bvr"][0:1, 0:256],
            in_=bv_dram[hg * 4:(hg + 1) * 4, :].rearrange("(o h) c -> o (h c)", o=1)),
            writes=["bvr"], dma=True)
        p.op("pe", lambda e: e.matmul(
            b["psD"][:, 0:256], b["onesr"][0:1, :], b["bvr"][0:1, 0:256],
            start=True, stop=True), reads=["bvr", "onesr"], writes=["psD"])
        p.op("dve", lambda e: e.tensor_copy(b["bvb"][:, 0:256],
                                            b["psD"][:, 0:256]),
             reads=["psD"], writes=["bvb"])
    vv = b["vb"]
    for i in range(NT):
        bank = "psA" if i % 2 == 0 else "psB"
        ps = b[bank]
        for ch in range(ND):
            p.op("pe", lambda e, ch=ch, i=i, ps=ps: e.matmul(
                ps[:, 0:256],
                b["xnT"][:, ch * T + i * 128: ch * T + i * 128 + 128],
                b["wvb"][:, ch * 256:(ch + 1) * 256],
                start=(ch == 0), stop=(ch == ND - 1)),
                reads=["wvb", "xnT"], writes=[bank])
        out_ap = bass.AP(vv.tensor, vv.offset + i * 260,
                         [[UCOLS, 128], [65, 4], [1, 64]])
        src = ps[:, 0:256].rearrange("p (h c) -> p h c", h=4)
        if have_bias:
            p.op("dve", lambda e, src=src, out_ap=out_ap:
                 e.scalar_tensor_tensor(
                     out_ap, src, 1.0,
                     b["bvb"][:, 0:256].rearrange("p (h c) -> p h c", h=4),
                     ALU.mult, ALU.add),
                 reads=[bank, "bvb"], writes=["vb"])
        else:
            p.op("dve", lambda e, src=src, out_ap=out_ap: e.tensor_copy(
                out_ap, src), reads=[bank], writes=["vb"])


def emit_head_M(p, b, Tq):
    for qc in range(Tq // 1024):
        p.op("act", lambda e, qc=qc: e.activation(
            b["sq"][0:64, 0:1024], b["qb"][0:64, qc * 1024:(qc + 1) * 1024],
            AF.Square), reads=["qb"], writes=["sq"])
        for j in range(8):
            p.op("pe", lambda e, qc=qc, j=j: e.matmul(
                b["psD"][:, qc * 8 + j: qc * 8 + j + 1],
                b["sq"][0:64, j * 128:(j + 1) * 128],
                b["ones128"][0:64, :], start=True, stop=True),
                reads=["sq", "ones128"], writes=["psD"])
    p.op("dve", lambda e: e.reduce_max(b["mx"][:, 0:1],
                                       b["psD"][:, 0:Tq // 128], axis=AX),
         reads=["psD"], writes=["mx0"])
    for kc in range(4):
        p.op("act", lambda e, kc=kc: e.activation(
            b["sq"][0:64, 0:1024], b["kb"][0:64, kc * 1024:(kc + 1) * 1024],
            AF.Square), reads=["kb", "sq"], writes=["sq"])
        for j in range(8):
            p.op("pe", lambda e, kc=kc, j=j: e.matmul(
                b["psD"][:, 64 + kc * 8 + j: 64 + kc * 8 + j + 1],
                b["sq"][0:64, j * 128:(j + 1) * 128],
                b["ones128"][0:64, :], start=True, stop=True),
                reads=["sq", "ones128"], writes=["psD"])
    p.op("dve", lambda e: e.reduce_max(b["mx"][:, 1:2],
                                       b["psD"][:, 64:64 + 32], axis=AX),
         reads=["psD"], writes=["mx1"])
    p.op("dve", lambda e: e.tensor_tensor(b["mx"][:, 2:3], b["mx"][:, 0:1],
                                          b["mx"][:, 1:2], ALU.mult),
         reads=["mx0", "mx1"], writes=["mx2"])
    p.op("dve", lambda e: e.tensor_copy(b["mxb"][:], b["mx"][:, 2:3]),
         reads=["mx2"], writes=["mxb"])
    p.op("pe", lambda e: e.transpose(b["psT"][0:1, 0:128], b["mxb"][:],
                                     b["ident"][:]),
         reads=["mxb", "ident"], writes=["psT"])
    p.op("dve", lambda e: e.reduce_max(b["mrow"][0:1, 0:1],
                                       b["psT"][0:1, 0:128], axis=AX),
         reads=["psT"], writes=["mrow0"])
    p.op("act", lambda e: e.activation(b["mrow"][0:1, 1:2],
                                       b["mrow"][0:1, 0:1], AF.Ln,
                                       bias=1e-30),
         reads=["mrow0", "konst"], writes=["mrow1"])
    p.op("act", lambda e: e.activation(b["mrow"][0:1, 2:3],
                                       b["mrow"][0:1, 1:2], AF.Exp,
                                       scale=0.5),
         reads=["mrow1"], writes=["mrow2"])
    p.op("dve", lambda e: e.tensor_scalar(b["mrow"][0:1, 3:4],
                                          b["mrow"][0:1, 2:3], -1.0, CEXP,
                                          ALU.mult, ALU.add),
         reads=["mrow2"], writes=["mrow3"])
    p.op("dve", lambda e: e.tensor_copy(b["mrowb"][0:1, 0:1],
                                        b["mrow"][0:1, 3:4]),
         reads=["mrow3"], writes=["mrowb"])
    p.op("pe", lambda e: e.matmul(b["psD"][:, 200:201], b["onesr"][0:1, :],
                                  b["mrowb"][0:1, 0:1], start=True,
                                  stop=True),
         reads=["mrowb", "onesr"], writes=["psD"])
    p.op("dve", lambda e: e.tensor_copy(b["cm"][:], b["psD"][:, 200:201]),
         reads=["psD"], writes=["cm"])


def emit_G(p, b, slope):
    pos = 0
    while pos < GW:
        wdt = min(1024, GW - pos)
        base = GOFF - pos
        p.op("pool", lambda e, pos=pos, wdt=wdt, base=base: e.iota(
            b["xt0"][:, 0:wdt], [[-1, wdt]], base=base, channel_multiplier=1,
            allow_small_or_imprecise_dtypes=True), writes=["xt0"])
        p.op("pool", lambda e, pos=pos, wdt=wdt, base=base: e.affine_select(
            b["xt0"][:, 0:wdt], b["xt0"][:, 0:wdt], [[1, wdt]], ALU.is_ge,
            b["filln"], base=-base, channel_multiplier=-1),
            reads=["xt0"], writes=["xt0"])
        p.op("act", lambda e, pos=pos, wdt=wdt: e.activation(
            b["gt"][:, pos:pos + wdt], b["xt0"][:, 0:wdt], AF.Exp,
            scale=float(slope)),
            reads=["xt0"], writes=["gt"])
        pos += wdt


def valid_kts(layer, q0, qw, slope):
    if layer == 1:
        return list(range(NT))
    kts = []
    for kt in range(NT):
        ktb = kt * 128
        if ktb >= q0 + qw:
            continue
        if slope * (q0 - ktb - 127) >= SKIP_GAP:
            continue
        kts.append(kt)
    return kts


def emit_attn_head(p, b, layer, h, Tq, oT_cols):
    slope = SLOPES[h] if layer == 0 else 0.0
    QW = 1024
    for qi in range(Tq // QW):
        q0 = qi * QW
        kts = valid_kts(layer, q0, QW, slope)
        L = len(kts)

        def s_mm(n):
            kt = kts[n]
            sb = "psA" if n % 2 == 0 else "psB"
            ps = b[sb]
            for half in range(2):
                p.op("pe", lambda e, kt=kt, ps=ps, half=half, q0=q0: e.matmul(
                    ps[:, half * 512:(half + 1) * 512],
                    b["kb"][0:64, kt * 128:(kt + 1) * 128],
                    b["qb"][0:64, q0 + half * 512: q0 + (half + 1) * 512],
                    start=True, stop=True),
                    reads=["kb", "qb"], writes=[sb])

        def do_exp(n):
            sb = "psA" if n % 2 == 0 else "psB"
            pb = "pt%d" % (n % 3)
            p.op("act", lambda e, sb=sb, pb=pb: e.activation(
                b[pb][:], b[sb][:], AF.Exp, bias=b["cm"][:]),
                reads=[sb, "cm"], writes=[pb])
            if layer == 0:
                off = GOFF + q0 - kts[n] * 128
                p.op("pool", lambda e, off=off, pb=pb: e.tensor_tensor(
                    b[pb][:], b[pb][:], b["gt"][:, off:off + QW], ALU.mult),
                    reads=[pb, "gt"], writes=[pb])

        def do_av(n):
            kt = kts[n]
            pb = "pt%d" % (n % 3)
            first, last = (n == 0), (n == L - 1)
            for half in range(2):
                p.op("pe", lambda e, kt=kt, pb=pb, half=half, first=first,
                     last=last: e.matmul(
                         b["psC"][0:65, half * 512:(half + 1) * 512],
                         b["vb"][:, kt * 260 + (h % 4) * 65:
                                 kt * 260 + (h % 4) * 65 + 65],
                         b[pb][:, half * 512:(half + 1) * 512],
                         start=first, stop=last),
                     reads=[pb, "vb"], writes=["psC"])

        for n in range(L):
            s_mm(n)
            if n >= 2:
                do_av(n - 2)
            do_exp(n)
        if L >= 2:
            do_av(L - 2)
        do_av(L - 1)
        p.op("dve", lambda e: e.reciprocal(b["rrow"][0:1, 0:QW],
                                           b["psC"][64:65, 0:QW]),
             reads=["psC"], writes=["rrow"])
        p.op("dve", lambda e: e.tensor_copy(b["rrowb"][0:1, 0:QW],
                                            b["rrow"][0:1, 0:QW]),
             reads=["rrow"], writes=["rrowb"])
        for half in range(2):
            p.op("pe", lambda e, half=half: e.matmul(
                b["psD"][0:64, 0:512], b["onesr"][64:65, 0:64],
                b["rrowb"][0:1, half * 512:(half + 1) * 512],
                start=True, stop=True),
                reads=["rrowb", "onesr"], writes=["psD"])
            p.op("dve", lambda e, half=half: e.tensor_copy(
                b["rb"][0:64, half * 512:(half + 1) * 512],
                b["psD"][0:64, 0:512]),
                reads=["psD"], writes=["rb"])
        obk = "ob%d" % (qi % 2)
        p.op("dve", lambda e, obk=obk: e.tensor_tensor(
            b[obk][0:64, 0:QW], b["psC"][0:64, 0:QW], b["rb"][0:64, 0:QW],
            ALU.mult),
            reads=["psC", "rb"], writes=[obk])
        ch, sub = h // 2, (h % 2) * 64
        dst = b["oT"][sub:sub + 64,
                      ch * oT_cols + q0: ch * oT_cols + q0 + QW]
        p.op("sync", lambda e, dst=dst, obk=obk: e.dma_start(
            out=dst, in_=b[obk][0:64, 0:QW]),
            reads=[obk], writes=["oT"], dma=True)


def _attn_layer(p, b, layer, t, Tq, oT_cols, q_src_key):
    wqk = t["wqk0"] if layer == 0 else t["wqk1"]
    wv = t["wv0"] if layer == 0 else t["wv1"]
    has_bias = (layer == 0)
    bv = t["bv0"] if layer == 0 else None
    for hg in range(4):
        emit_v_group(p, b, hg, wv, bv)
        for hh in range(4):
            h = hg * 4 + hh
            emit_qk_head(p, b, h, wqk, has_bias, Tq, q_src_key=q_src_key)
            emit_head_M(p, b, Tq)
            if layer == 0:
                emit_G(p, b, SLOPES[h])
            emit_attn_head(p, b, layer, h, Tq, oT_cols)


def _top2_softmax(p, b, sigmoid):
    st = b["stats"]
    p.op("dve", lambda e: e.tensor_copy(b["gbuf"][:, 0:8], b["psD"][:, 0:8]),
         reads=["psD"], writes=["gbuf"])
    p.op("dve", lambda e: e.reduce_max(st[:, 8:9], b["gbuf"][:, 0:8],
                                       axis=AX),
         reads=["gbuf"], writes=["m1"])
    p.op("dve", lambda e: e.tensor_scalar(b["eq1"][:, 0:8], b["gbuf"][:, 0:8],
                                          st[:, 8:9], None, ALU.is_equal),
         reads=["gbuf", "m1"], writes=["eq1"])
    p.op("dve", lambda e: e.scalar_tensor_tensor(
        b["eq2"][:, 0:8], b["eq1"][:, 0:8], -1e30, b["gbuf"][:, 0:8],
        ALU.mult, ALU.add), reads=["eq1", "gbuf"], writes=["eq2"])
    p.op("dve", lambda e: e.reduce_max(st[:, 9:10], b["eq2"][:, 0:8],
                                       axis=AX),
         reads=["eq2"], writes=["m2"])
    p.op("dve", lambda e: e.tensor_scalar(b["eq2"][:, 0:8], b["gbuf"][:, 0:8],
                                          st[:, 9:10], None, ALU.is_equal),
         reads=["gbuf", "m2"], writes=["eq2"])
    if not sigmoid:
        p.op("dve", lambda e: e.tensor_sub(st[:, 10:11], st[:, 9:10],
                                           st[:, 8:9]),
             reads=["m1", "m2"], writes=["d21"])
        p.op("act", lambda e: e.activation(st[:, 11:12], st[:, 10:11],
                                           AF.Exp),
             reads=["d21"], writes=["e21"])
        p.op("dve", lambda e: e.tensor_scalar_add(st[:, 12:13],
                                                  st[:, 11:12], 1.0),
             reads=["e21"], writes=["s21"])
        p.op("dve", lambda e: e.reciprocal(st[:, 13:14], st[:, 12:13]),
             reads=["s21"], writes=["wA"])
        p.op("dve", lambda e: e.tensor_tensor(st[:, 14:15], st[:, 11:12],
                                              st[:, 13:14], ALU.mult),
             reads=["e21", "wA"], writes=["wB"])
        wa, wb_ = 13, 14
    else:
        for k, src, kk in ((10, 8, "m1"), (11, 9, "m2")):
            p.op("act", lambda e, k=k, src=src: e.activation(
                st[:, k:k + 1], st[:, src:src + 1], AF.Exp, scale=-1.0),
                reads=[kk], writes=["sg%d" % k])
            p.op("dve", lambda e, k=k: e.tensor_scalar_add(
                st[:, k:k + 1], st[:, k:k + 1], 1.0),
                reads=["sg%d" % k], writes=["sg%d" % k])
            p.op("dve", lambda e, k=k: e.reciprocal(st[:, k + 2:k + 3],
                                                    st[:, k:k + 1]),
                 reads=["sg%d" % k], writes=["v%d" % k])
        p.op("dve", lambda e: e.tensor_add(st[:, 14:15], st[:, 12:13],
                                           st[:, 13:14]),
             reads=["v10", "v11"], writes=["vs"])
        p.op("dve", lambda e: e.reciprocal(st[:, 15:16], st[:, 14:15]),
             reads=["vs"], writes=["vsr"])
        p.op("dve", lambda e: e.tensor_scalar_mul(st[:, 15:16],
                                                  st[:, 15:16], 1.0 / 3.0),
             reads=["vsr"], writes=["vsr"])
        p.op("dve", lambda e: e.tensor_tensor(st[:, 10:11], st[:, 12:13],
                                              st[:, 15:16], ALU.mult),
             reads=["v10", "vsr"], writes=["wA"])
        p.op("dve", lambda e: e.tensor_tensor(st[:, 11:12], st[:, 13:14],
                                              st[:, 15:16], ALU.mult),
             reads=["v11", "vsr"], writes=["wB"])
        wa, wb_ = 10, 11
    p.op("dve", lambda e, wa=wa: e.tensor_scalar(
        b["cw"][:, 0:8], b["eq1"][:, 0:8], st[:, wa:wa + 1], None, ALU.mult),
        reads=["eq1", "wA"], writes=["cw"])
    p.op("dve", lambda e, wb_=wb_: e.scalar_tensor_tensor(
        b["cw"][:, 0:8], b["eq2"][:, 0:8], st[:, wb_:wb_ + 1],
        b["cw"][:, 0:8], ALU.mult, ALU.add),
        reads=["eq2", "wB", "cw"], writes=["cw"])


def emit_gate0_logits(p, b, i):
    for ch in range(ND):
        p.op("pe", lambda e, ch=ch: e.matmul(
            b["psD"][:, 0:8],
            b["xnT"][:, ch * T + i * 128: ch * T + i * 128 + 128],
            b["wg0b"][:, ch * 8:(ch + 1) * 8],
            start=(ch == 0), stop=(ch == ND - 1)),
            reads=["wg0b", "xnT"], writes=["psD"])


def emit_gate0_rest(p, b, i):
    _top2_softmax(p, b, sigmoid=False)
    for ch in range(ND):
        p.op("pe", lambda e, ch=ch: e.matmul(
            b["psB"][:, 0:128],
            b["xnT"][:, ch * T + i * 128: ch * T + i * 128 + 128],
            b["a0b"][:, ch * 128:(ch + 1) * 128],
            start=(ch == 0), stop=(ch == ND - 1)),
            reads=["a0b", "xnT"], writes=["psB"])
    cw8 = b["cw"][:, 0:8]
    cwb = bass.AP(cw8.tensor, cw8.offset,
                  [list(d) for d in cw8.ap] + [[0, 16]])
    p.op("dve", lambda e: e.tensor_tensor(
        b["hw0"].rearrange("p (g r) -> p g r", g=8),
        b["psB"][:, 0:128].rearrange("p (g r) -> p g r", g=8), cwb,
        ALU.mult),
        reads=["psB", "cw"], writes=["hw0"])
    p.op("pe", lambda e: e.transpose(b["psT"][:, 0:128], b["hw0"],
                                     b["ident"][:]),
         reads=["hw0", "ident"], writes=["psT"])
    p.op("dve", lambda e: e.tensor_copy(b["hw0t"], b["psT"][:, 0:128]),
         reads=["psT"], writes=["hw0t"])
    for half in range(2):
        p.op("pe", lambda e, half=half: e.matmul(
            b["psC"][:, half * 512:(half + 1) * 512], b["hw0t"],
            b["b0b"][:, half * 512:(half + 1) * 512],
            start=False, stop=False), reads=["hw0t", "b0b"], writes=["psC"])


def load_asm0_weights(p, b, t):
    p.op("pool", lambda e: e.memset(b["zrow"][0:1, :], 0.0), writes=["zrow"])
    p.op("pool", lambda e: e.dma_start(
        out=b["wob"].rearrange("p (ch c) -> p ch c", ch=ND),
        in_=t["wo0"].rearrange("(ch p) c -> p ch c", p=128)),
        writes=["wob"], dma=True)
    p.op("pool", lambda e: e.dma_start(
        out=b["wg0b"].rearrange("p (ch c) -> p ch c", ch=ND),
        in_=t["wg0"].rearrange("(ch p) c -> p ch c", p=128)),
        writes=["wg0b"], dma=True)
    p.op("pool", lambda e: e.dma_start(
        out=b["a0b"].rearrange("p (ch c) -> p ch c", ch=ND),
        in_=t["a0"].rearrange("(ch p) c -> p ch c", p=128)),
        writes=["a0b"], dma=True)
    p.op("pool", lambda e: e.dma_start(out=b["b0b"], in_=t["b0"]),
         writes=["b0b"], dma=True)
    p.op("pool", lambda e: e.dma_start(out=b["bpjb"][0:1, :], in_=t["bproj"]),
         writes=["bpjb"], dma=True)


def emit_assembly_l0(p, b, x0_dram, ln1, x1_dram, x1_out):
    for i in range(NT):
        psy = b["psC"]
        emit_gate0_logits(p, b, i)
        for half in range(2):
            p.op("pe", lambda e, half=half: e.matmul(
                psy[:, half * 512:(half + 1) * 512], b["onesr"][0:1, :],
                b["bpjb"][0:1, half * 512:(half + 1) * 512],
                start=True, stop=False),
                reads=["bpjb", "onesr"], writes=["psC"])
        for ch in range(ND):
            for half in range(2):
                p.op("pe", lambda e, ch=ch, i=i, half=half: e.matmul(
                    psy[:, half * 512:(half + 1) * 512],
                    b["oT"][:, ch * T + i * 128: ch * T + i * 128 + 128],
                    b["wob"][:, ch * 1024 + half * 512:
                             ch * 1024 + (half + 1) * 512],
                    start=False, stop=False), reads=["oT", "wob"],
                    writes=["psC"])
        emit_gate0_rest(p, b, i)
        for half in range(2):
            p.op("pe", lambda e, half=half: e.matmul(
                psy[:, half * 512:(half + 1) * 512], b["onesr"][0:1, :],
                b["zrow"][0:1, 0:512],
                start=False, stop=True),
                reads=["zrow", "onesr"], writes=["psC"])
        par = i % 2
        xt = b["xt0"] if par == 0 else b["xt1"]
        xk = "xt%d" % par
        p.op("sync", lambda e, i=i, xt=xt: e.dma_start(
            out=xt[:], in_=x0_dram[i * 128:(i + 1) * 128, :]),
            writes=[xk], dma=True)
        p.op("dve", lambda e, xt=xt: e.scalar_tensor_tensor(
            xt[:], psy[:], 1.0, xt[:], ALU.mult, ALU.add),
            reads=["psC"], writes=[xk])
        if x1_dram is not None:
            p.op("sync", lambda e, i=i, xt=xt: e.dma_start(
                out=x1_dram[i * 128:(i + 1) * 128, :], in_=xt[:]),
                reads=[xk], writes=["x1scr"], dma=True)
        if x1_out is not None:
            p.op("sync", lambda e, i=i, xt=xt: e.dma_start(
                out=x1_out[i * 128:(i + 1) * 128, :], in_=xt[:]),
                reads=[xk], writes=["x1out"], dma=True)
        if ln1:
            _ln_tile(p, b, xt, xk, i, b["xnT"], "xnT", T)


def yacc_slice(b, i):
    return b["xnT"][:, i * 2048:(i + 1) * 2048].bitcast(F32)


def emit_gating1_proj1(p, b, t):
    p.op("pool", lambda e: e.dma_start(
        out=b["wg1b"].rearrange("p (ch c) -> p ch c", ch=ND),
        in_=t["wg1"].rearrange("(ch p) c -> p ch c", p=128)),
        writes=["wg1b"], dma=True)
    p.op("pool", lambda e: e.dma_start(
        out=b["wob"].rearrange("p (ch c) -> p ch c", ch=ND),
        in_=t["wo1"].rearrange("(ch p) c -> p ch c", p=128)),
        writes=["wob"], dma=True)
    sh = b["wcols"].rearrange("p (i e) -> p i e", i=NTO)
    p.op("pool", lambda e: e.memset(sh[:, :, 0:1], 1.0 / 3.0),
         writes=["wcols"])
    for i in range(NTO):
        for ch in range(ND):
            p.op("pe", lambda e, ch=ch, i=i: e.matmul(
                b["psD"][:, 0:8],
                b["ownT"][:, ch * TO + i * 128: ch * TO + i * 128 + 128],
                b["wg1b"][:, ch * 8:(ch + 1) * 8],
                start=(ch == 0), stop=(ch == ND - 1)),
                reads=["wg1b", "ownT"], writes=["psD"])
        for ch in range(ND):
            for half in range(2):
                p.op("pe", lambda e, ch=ch, i=i, half=half: e.matmul(
                    b["psC"][:, half * 512:(half + 1) * 512],
                    b["oT"][:, ch * TO + i * 128: ch * TO + i * 128 + 128],
                    b["wob"][:, ch * 1024 + half * 512:
                             ch * 1024 + (half + 1) * 512],
                    start=(ch == 0), stop=(ch == ND - 1)),
                    reads=["oT", "wob"], writes=["psC"])
        _top2_softmax(p, b, sigmoid=True)
        p.op("dve", lambda e, i=i: e.tensor_copy(
            b["wcols"][:, i * NE + 1: i * NE + 9], b["cw"][:, 0:8]),
            reads=["cw"], writes=["wcols"])
        p.op("dve", lambda e, i=i: e.scalar_tensor_tensor(
            yacc_slice(b, i), b["psC"][:], 1.0, yacc_slice(b, i),
            ALU.mult, ALU.add),
            reads=["psC", "yacc"], writes=["yacc"])


def emit_ffn1(p, b, we13_dram, we2_dram):
    for e_i in range(NE):
        for fq in range(4):
            par = (e_i * 4 + fq) % 2
            w13k, w2k = "w13_%d" % par, "w2_%d" % par
            w13 = b["w13b0"] if par == 0 else b["w13b1"]
            w2 = b["w2b0"] if par == 0 else b["w2b1"]
            p.op("pool", lambda e, e_i=e_i, fq=fq, w13=w13: e.dma_start(
                out=w13.rearrange("p (ch c) -> p ch c", ch=ND),
                in_=we13_dram[e_i][:, fq * 1024:(fq + 1) * 1024].rearrange(
                    "(ch p) c -> p ch c", p=128)),
                writes=[w13k], dma=True)
            p.op("pool", lambda e, e_i=e_i, fq=fq, w2=w2: e.dma_start(
                out=w2.rearrange("p (ch c) -> p ch c", ch=4),
                in_=we2_dram[e_i][fq * 512:(fq + 1) * 512, :].rearrange(
                    "(ch p) c -> p ch c", p=128)),
                writes=[w2k], dma=True)
            for i in range(NTO):
                for ch in range(ND):
                    for hh in range(2):
                        p.op("pe", lambda e, ch=ch, i=i, hh=hh, w13=w13:
                             e.matmul(
                                 b["psA"][:, hh * 512:(hh + 1) * 512],
                                 b["ownT"][:, ch * TO + i * 128:
                                           ch * TO + i * 128 + 128],
                                 w13[:, ch * 1024 + hh * 512:
                                     ch * 1024 + (hh + 1) * 512],
                                 start=(ch == 0), stop=(ch == ND - 1)),
                             reads=[w13k, "ownT"], writes=["psA"])
                p.op("act", lambda e: e.activation(
                    b["s1b"][:, 0:512], b["psA"][:, 0:512], AF.Silu),
                    reads=["psA"], writes=["s1b"])
                wcol = b["wcols"][:, i * NE + e_i: i * NE + e_i + 1]
                p.op("dve", lambda e, wcol=wcol: e.scalar_tensor_tensor(
                    b["hwb"][:, 0:512], b["psA"][:, 512:1024], wcol,
                    b["s1b"][:, 0:512], ALU.mult, ALU.mult),
                    reads=["psA", "s1b", "wcols"], writes=["hwb"])
                for j in range(4):
                    p.op("pe", lambda e, j=j: e.transpose(
                        b["psT"][:, j * 128:(j + 1) * 128],
                        b["hwb"][:, j * 128:(j + 1) * 128], b["ident"][:]),
                        reads=["hwb", "ident"], writes=["psT"])
                p.op("dve", lambda e: e.tensor_copy(b["hwtb"][:, 0:512],
                                                    b["psT"][:, 0:512]),
                     reads=["psT"], writes=["hwtb"])
                for j in range(4):
                    for half in range(2):
                        p.op("pe", lambda e, j=j, half=half, w2=w2: e.matmul(
                            b["psB"][:, half * 512:(half + 1) * 512],
                            b["hwtb"][:, j * 128:(j + 1) * 128],
                            w2[:, j * 1024 + half * 512:
                               j * 1024 + (half + 1) * 512],
                            start=(j == 0), stop=(j == 3)),
                            reads=["hwtb", w2k], writes=["psB"])
                p.op("dve", lambda e, i=i: e.scalar_tensor_tensor(
                    yacc_slice(b, i), b["psB"][:], 1.0, yacc_slice(b, i),
                    ALU.mult, ALU.add),
                    reads=["psB", "yacc"], writes=["yacc"])


def emit_assembly_l1(p, b, x1own_src, out_dram):
    for i in range(NTO):
        par = i % 2
        xt = b["xt0"] if par == 0 else b["xt1"]
        xk = "xt%d" % par
        if x1own_src[0] == "dram":
            p.op("sync", lambda e, i=i, xt=xt: e.dma_start(
                out=xt[:], in_=x1own_src[1][i * 128:(i + 1) * 128, :]),
                writes=[xk], dma=True)
        else:
            idxs, x1d = x1own_src[1], x1own_src[2]
            p.op("pool", lambda e, i=i, xt=xt: e.indirect_dma_start(
                out=xt[:], out_offset=None, in_=x1d,
                in_offset=IndirectOffsetOnAxis(idxs[:, i:i + 1], 0)),
                reads=["idxX", "x1scr"], writes=[xk], dma=True)
        p.op("dve", lambda e, i=i, xt=xt: e.scalar_tensor_tensor(
            xt[:], yacc_slice(b, i), 1.0, xt[:], ALU.mult, ALU.add),
            reads=["yacc"], writes=[xk])
        p.op("sync", lambda e, i=i, xt=xt: e.dma_start(
            out=out_dram[i * 128:(i + 1) * 128, :], in_=xt[:]),
            reads=[xk], writes=["x2out"], dma=True)


def build_l0(x1_as_output):
    nc = bass.Bass("TRN2", target_bir_lowering=False)
    t = {}
    t["x0"] = nc.dram_tensor("x0", [T, D], F32, kind="ExternalInput").ap()
    t["wqk0"] = nc.dram_tensor("wqk0", [D, H * 128], F32,
                               kind="ExternalInput").ap()
    t["wv0"] = nc.dram_tensor("wv0", [D, H * 64], F32,
                              kind="ExternalInput").ap()
    t["bqk0"] = nc.dram_tensor("bqk0", [128, 16], F32,
                               kind="ExternalInput").ap()
    t["bv0"] = nc.dram_tensor("bv0", [16, 64], F32,
                              kind="ExternalInput").ap()
    t["wo0"] = nc.dram_tensor("wo0", [D, D], F32, kind="ExternalInput").ap()
    t["bproj"] = nc.dram_tensor("bproj", [1, D], F32,
                                kind="ExternalInput").ap()
    t["wg0"] = nc.dram_tensor("wg0", [D, 8], F32, kind="ExternalInput").ap()
    t["a0"] = nc.dram_tensor("a0", [D, 128], F32, kind="ExternalInput").ap()
    t["b0"] = nc.dram_tensor("b0", [128, D], F32, kind="ExternalInput").ap()
    x1_out = nc.dram_tensor("x1", [T, D], F32, kind="ExternalOutput").ap() \
        if x1_as_output else None

    with ExitStack() as st:
        b = alloc_bufs(nc, st, "l0")
        p = Prog(nc, st)
        emit_consts(p, b)
        p.op("sync", lambda e: e.dma_start(out=b["bqk"][:], in_=t["bqk0"]),
             writes=["bqk"], dma=True)
        emit_lnT(p, b, t["x0"], NT, b["xnT"], "xnT", T)
        p.barrier(["xnb", "sq1k"])
        emit_vb_ones(p, b)
        _attn_layer(p, b, 0, t, T, T, None)
        p.barrier(ATTN_KEYS)
        load_asm0_weights(p, b, t)
        emit_assembly_l0(p, b, t["x0"], ln1=False, x1_dram=None,
                         x1_out=x1_out)
        p.finish()
    return nc


def build_l1():
    nc = bass.Bass("TRN2", target_bir_lowering=False)
    t = {}
    t["x1f"] = nc.dram_tensor("x1f", [T, D], F32, kind="ExternalInput").ap()
    t["x1o"] = nc.dram_tensor("x1o", [TO, D], F32, kind="ExternalInput").ap()
    t["wqk1"] = nc.dram_tensor("wqk1", [D, H * 128], F32,
                               kind="ExternalInput").ap()
    t["wv1"] = nc.dram_tensor("wv1", [D, H * 64], F32,
                              kind="ExternalInput").ap()
    t["wo1"] = nc.dram_tensor("wo1", [D, D], F32, kind="ExternalInput").ap()
    t["wg1"] = nc.dram_tensor("wg1", [D, 8], F32, kind="ExternalInput").ap()
    we13 = [nc.dram_tensor("we13_%d" % i, [D, 2 * FF], F32,
                           kind="ExternalInput").ap() for i in range(NE)]
    we2 = [nc.dram_tensor("we2_%d" % i, [FF, D], F32,
                          kind="ExternalInput").ap() for i in range(NE)]
    out = nc.dram_tensor("x2", [TO, D], F32, kind="ExternalOutput").ap()

    with ExitStack() as st:
        b = alloc_bufs(nc, st, "l1")
        p = Prog(nc, st)
        emit_consts(p, b)
        emit_lnT(p, b, t["x1f"], NT, b["xnT"], "xnT", T)
        emit_lnT(p, b, t["x1o"], NTO, b["ownT"], "ownT", TO)
        p.barrier(["xnb", "sq1k"])
        emit_vb_ones(p, b)
        _attn_layer(p, b, 1, t, TO, TO, "ownT")
        p.barrier(ATTN_KEYS)
        p.op("pool", lambda e: e.memset(b["xnT"][:], 0.0),
             reads=["xnT"], writes=["xnT", "yacc"])
        emit_gating1_proj1(p, b, t)
        p.barrier(["wob"])
        emit_ffn1(p, b, we13, we2)
        emit_assembly_l1(p, b, ("dram", t["x1o"]), out)
        p.finish()
    return nc




ASM0_KEYS = ["wob", "a0b", "b0b", "hw0", "hw0t", "bpjb", "zrow", "wg0b",
             "xnb", "sq1k", "cw", "gbuf", "eq1", "eq2"]


def build_fused():
    nc = bass.Bass("TRN2", target_bir_lowering=False)
    t = {}
    t["x0"] = nc.dram_tensor("x0", [T, D], F32, kind="ExternalInput").ap()
    t["wqk0"] = nc.dram_tensor("wqk0", [D, H * 128], F32,
                               kind="ExternalInput").ap()
    t["wv0"] = nc.dram_tensor("wv0", [D, H * 64], F32,
                              kind="ExternalInput").ap()
    t["bqk0"] = nc.dram_tensor("bqk0", [128, 16], F32,
                               kind="ExternalInput").ap()
    t["bv0"] = nc.dram_tensor("bv0", [16, 64], F32,
                              kind="ExternalInput").ap()
    t["wo0"] = nc.dram_tensor("wo0", [D, D], F32, kind="ExternalInput").ap()
    t["bproj"] = nc.dram_tensor("bproj", [1, D], F32,
                                kind="ExternalInput").ap()
    t["wg0"] = nc.dram_tensor("wg0", [D, 8], F32, kind="ExternalInput").ap()
    t["a0"] = nc.dram_tensor("a0", [D, 128], F32, kind="ExternalInput").ap()
    t["b0"] = nc.dram_tensor("b0", [128, D], F32, kind="ExternalInput").ap()
    t["wqk1"] = nc.dram_tensor("wqk1", [D, H * 128], F32,
                               kind="ExternalInput").ap()
    t["wv1"] = nc.dram_tensor("wv1", [D, H * 64], F32,
                              kind="ExternalInput").ap()
    t["wo1"] = nc.dram_tensor("wo1", [D, D], F32, kind="ExternalInput").ap()
    t["wg1"] = nc.dram_tensor("wg1", [D, 8], F32, kind="ExternalInput").ap()
    we13 = [nc.dram_tensor("we13_%d" % i, [D, 2 * FF], F32,
                           kind="ExternalInput").ap() for i in range(NE)]
    we2 = [nc.dram_tensor("we2_%d" % i, [FF, D], F32,
                          kind="ExternalInput").ap() for i in range(NE)]
    idxT = nc.dram_tensor("idxT", [128, 8], I32, kind="ExternalInput").ap()
    idxX = nc.dram_tensor("idxX", [128, 16], I32, kind="ExternalInput").ap()
    x1scr = nc.dram_tensor("x1scr", [T, D], F32, kind="Internal").ap()
    xn1scr = nc.dram_tensor("xn1scr", [D, T], BF16, kind="Internal").ap()
    out = nc.dram_tensor("x2", [TO, D], F32, kind="ExternalOutput").ap()

    with ExitStack() as st:
        b = alloc_bufs(nc, st, "fused")
        p = Prog(nc, st)
        emit_consts(p, b)
        p.op("sync", lambda e: e.dma_start(out=b["bqk"][:], in_=t["bqk0"]),
             writes=["bqk"], dma=True)
        p.op("sync", lambda e: e.dma_start(out=b["idxT"][:], in_=idxT),
             writes=["idxT"], dma=True)
        p.op("sync", lambda e: e.dma_start(out=b["idxX"][:], in_=idxX),
             writes=["idxX"], dma=True)
        emit_lnT(p, b, t["x0"], NT, b["xnT"], "xnT", T)
        p.barrier(["xnb", "sq1k"])
        emit_vb_ones(p, b)
        _attn_layer(p, b, 0, t, T, T, None)
        p.barrier(ATTN_KEYS)
        load_asm0_weights(p, b, t)
        emit_assembly_l0(p, b, t["x0"], ln1=True, x1_dram=x1scr, x1_out=None)
        for ch in range(ND):
            p.op("sync", lambda e, ch=ch: e.dma_start(
                out=xn1scr[ch * 128:(ch + 1) * 128, :],
                in_=b["xnT"][:, ch * T:(ch + 1) * T]),
                reads=["xnT"], writes=["xn1scr"], dma=True)
        p.barrier(ASM0_KEYS + ["oT"])
        xn1flat = bass.AP(xn1scr.tensor, 0, [[1, D * T], [1, 1]])
        for ch in range(ND):
            p.op("pool", lambda e, ch=ch: e.indirect_dma_start(
                out=b["ownT"][:, ch * TO:(ch + 1) * TO], out_offset=None,
                in_=xn1flat,
                in_offset=IndirectOffsetOnAxis(b["idxT"][:, ch:ch + 1], 0)),
                reads=["xn1scr", "idxT"], writes=["ownT"], dma=True)
        emit_vb_ones(p, b)
        _attn_layer(p, b, 1, t, TO, TO, "ownT")
        p.barrier(ATTN_KEYS)
        p.op("pool", lambda e: e.memset(b["xnT"][:], 0.0),
             reads=["xnT"], writes=["xnT", "yacc"])
        emit_gating1_proj1(p, b, t)
        p.barrier(["wob", "cw", "gbuf", "eq1", "eq2"])
        emit_ffn1(p, b, we13, we2)
        emit_assembly_l1(p, b, ("gather", b["idxX"], x1scr), out)
        p.finish()
    return nc



def _prep_shared(w):
    g = {}
    wa = np.asarray(w["L0_attn_w_attn"], np.float32)
    qk = np.empty((D, H * 128), np.float32)
    for h in range(H):
        qk[:, h * 128:h * 128 + 64] = wa[:, D + h * 64:D + (h + 1) * 64]
        qk[:, h * 128 + 64:h * 128 + 128] = wa[:, h * 64:(h + 1) * 64]
    g["wqk0"] = qk
    g["wv0"] = np.ascontiguousarray(wa[:, 2 * D:3 * D])
    ba = np.asarray(w["L0_attn_b_attn"], np.float32)
    bqk = np.empty((128, 16), np.float32)
    for h in range(H):
        bqk[0:64, h] = ba[D + h * 64:D + (h + 1) * 64]
        bqk[64:128, h] = ba[h * 64:(h + 1) * 64]
    g["bqk0"] = bqk
    g["bv0"] = np.ascontiguousarray(ba[2 * D:].reshape(16, 64))
    g["wo0"] = np.ascontiguousarray(np.asarray(w["L0_attn_w_proj"],
                                               np.float32))
    g["bproj"] = np.ascontiguousarray(
        np.asarray(w["L0_attn_b_proj"], np.float32)[None, :])
    g["wg0"] = np.ascontiguousarray(np.asarray(w["L0_ffn_Wg"], np.float32))
    g["a0"] = np.ascontiguousarray(
        np.asarray(w["L0_ffn_A"], np.float32).transpose(1, 0, 2).reshape(
            D, 128))
    g["b0"] = np.ascontiguousarray(
        np.asarray(w["L0_ffn_B"], np.float32).reshape(128, D))
    wq = np.asarray(w["L1_attn_W_q"], np.float32)
    wk2 = np.asarray(w["L1_attn_W_k"], np.float32)
    wv2 = np.asarray(w["L1_attn_W_v"], np.float32)
    qk1 = np.empty((D, H * 128), np.float32)
    for h in range(H):
        qk1[:, h * 128:h * 128 + 64] = wk2[h]
        qk1[:, h * 128 + 64:h * 128 + 128] = wq[h]
    g["wqk1"] = qk1
    g["wv1"] = np.ascontiguousarray(wv2.transpose(1, 0, 2).reshape(D, H * 64))
    g["wo1"] = np.ascontiguousarray(np.asarray(w["L1_attn_W_o"], np.float32))
    g["wg1"] = np.ascontiguousarray(np.asarray(w["L1_ffn_Wg"], np.float32))
    s1 = np.asarray(w["L1_ffn_Ws1"], np.float32)
    s3 = np.asarray(w["L1_ffn_Ws3"], np.float32)
    s2 = np.asarray(w["L1_ffn_Ws2"], np.float32)
    e1 = np.asarray(w["L1_ffn_We1"], np.float32)
    e3 = np.asarray(w["L1_ffn_We3"], np.float32)
    e2 = np.asarray(w["L1_ffn_We2"], np.float32)

    def inter(a, bm):
        out = np.empty((D, 2 * FF), np.float32)
        for fq in range(4):
            out[:, fq * 1024:fq * 1024 + 512] = a[:, fq * 512:(fq + 1) * 512]
            out[:, fq * 1024 + 512:(fq + 1) * 1024] = \
                bm[:, fq * 512:(fq + 1) * 512]
        return out
    for i in range(NE):
        if i == 0:
            g["we13_%d" % i] = inter(s1, s3)
            g["we2_%d" % i] = np.ascontiguousarray(s2)
        else:
            g["we13_%d" % i] = inter(e1[i - 1], e3[i - 1])
            g["we2_%d" % i] = np.ascontiguousarray(e2[i - 1])
    return g


FUSED = True


def kernel(**inputs):
    g = _prep_shared(inputs)
    x0 = np.ascontiguousarray(np.asarray(inputs["L0_attn_x"], np.float32))
    if FUSED:
        return kernel_fused(inputs, g, x0)

    nc0 = build_l0(x1_as_output=True)
    l0_keys = ["wqk0", "wv0", "bqk0", "bv0", "wo0", "bproj", "wg0", "a0",
               "b0"]
    in_maps0 = []
    for c in range(8):
        m = {"x0": x0[c // 2]}
        for k in l0_keys:
            m[k] = g[k]
        in_maps0.append(m)
    r0 = run_bass_kernel_spmd(nc0, in_maps0, core_ids=list(range(8)))
    x1 = [np.asarray(r0.results[2 * bb]["x1"]) for bb in range(B)]

    nc1 = build_l1()
    l1_keys = (["wqk1", "wv1", "wo1", "wg1"] +
               ["we13_%d" % i for i in range(NE)] +
               ["we2_%d" % i for i in range(NE)])
    in_maps1 = []
    for c in range(8):
        bb, hf = c // 2, c % 2
        m = {"x1f": x1[bb],
             "x1o": np.ascontiguousarray(x1[bb][hf * TO:(hf + 1) * TO])}
        for k in l1_keys:
            m[k] = g[k]
        in_maps1.append(m)
    r1 = run_bass_kernel_spmd(nc1, in_maps1, core_ids=list(range(8)))

    out = np.empty((B, T, D), np.float32)
    for c in range(8):
        bb, hf = c // 2, c % 2
        out[bb, hf * TO:(hf + 1) * TO] = np.asarray(r1.results[c]["x2"])
    return out


def kernel_fused(inputs, g, x0):
    nc = build_fused()
    keys = (["wqk0", "wv0", "bqk0", "bv0", "wo0", "bproj", "wg0", "a0", "b0",
             "wqk1", "wv1", "wo1", "wg1"] +
            ["we13_%d" % i for i in range(NE)] +
            ["we2_%d" % i for i in range(NE)])
    pidx = np.arange(128)
    in_maps = []
    for c in range(8):
        bb, hf = c // 2, c % 2
        own0 = hf * TO
        idxT_np = np.empty((128, 8), np.int32)
        for ch in range(ND):
            idxT_np[:, ch] = (ch * 128 + pidx) * T + own0
        idxX_np = np.empty((128, 16), np.int32)
        for i in range(NTO):
            idxX_np[:, i] = own0 + i * 128 + pidx
        m = {"x0": x0[bb], "idxT": idxT_np, "idxX": idxX_np}
        for k in keys:
            m[k] = g[k]
        in_maps.append(m)
    r = run_bass_kernel_spmd(nc, in_maps, core_ids=list(range(8)))
    out = np.empty((B, T, D), np.float32)
    for c in range(8):
        bb, hf = c // 2, c % 2
        out[bb, hf * TO:(hf + 1) * TO] = np.asarray(r.results[c]["x2"])
    return out


# version 08
# speedup vs baseline: 1.0774x; 1.0774x over previous
import numpy as np
import ml_dtypes
from contextlib import ExitStack

import concourse.bass as bass
import concourse.mybir as mybir
from concourse.bass import IndirectOffsetOnAxis
from concourse.bass_utils import run_bass_kernel_spmd

F32 = mybir.dt.float32
BF16 = mybir.dt.bfloat16
I32 = mybir.dt.int32
AF = mybir.ActivationFunctionType
ALU = mybir.AluOpType
AX = mybir.AxisListType.X

B, T, D, H, DH = 4, 4096, 1024, 16, 64
TO = T // 2
NT, NTO = T // 128, TO // 128
ND = D // 128
CEXP = 20.0
EPS = 1e-5
FF = 2048
NE = 9
SLOPES = [2.0 ** (-(8.0 / H) * (i + 1)) for i in range(H)]
GW = 4480
GOFF = 384
SKIP_GAP = 25.0
QW = 512
UCOLS = 33024


class Prog:
    def __init__(self, nc, st):
        self.nc = nc
        self.st = st
        self.e = {"pe": nc.tensor, "act": nc.scalar, "dve": nc.vector,
                  "pool": nc.gpsimd, "sync": nc.sync}
        self.sem = {}
        for sname in ("pe", "act", "dve", "pool"):
            self.sem[sname] = st.enter_context(nc.semaphore("sem_" + sname))
        self.cnt = {sname: 0 for sname in self.sem}
        self.waited = {sname: {} for sname in
                       ("pe", "act", "dve", "pool", "sync")}
        self.buf = {}

    def _dma_sem(self, key):
        name = "d_" + key
        if name not in self.sem:
            self.sem[name] = self.st.enter_context(
                self.nc.semaphore("sem_" + name))
            self.cnt[name] = 0
        return name

    def wait(self, stream, tok):
        if tok is None:
            return
        chan, v = tok
        if stream == "pe" and chan == "pe":
            return
        if v <= self.waited[stream].get(chan, 0):
            return
        self.e[stream].wait_ge(self.sem[chan], v)
        self.waited[stream][chan] = v

    def op(self, stream, fn, reads=(), writes=(), dma=False, deps=()):
        for t in deps:
            self.wait(stream, t)
        for k in reads:
            sb_ = self.buf.get(k)
            if sb_ and sb_["w"]:
                self.wait(stream, sb_["w"])
        for k in writes:
            sb_ = self.buf.get(k)
            if sb_:
                if sb_["w"]:
                    self.wait(stream, sb_["w"])
                for t in sb_["r"]:
                    self.wait(stream, t)
        ins = fn(self.e[stream])
        if dma:
            assert writes, "DMA ops must declare a write key"
            chan = self._dma_sem(writes[0])
            inc = 16
        else:
            chan = stream
            inc = 1
        ins.then_inc(self.sem[chan], inc)
        self.cnt[chan] += inc
        tok = (chan, self.cnt[chan])
        for k in reads:
            self.buf.setdefault(k, {"w": None, "r": []})["r"].append(tok)
        for k in writes:
            self.buf[k] = {"w": tok, "r": []}
        return tok

    def barrier(self, keys, streams=("pe", "act", "dve", "pool", "sync")):
        best = {}
        for k in keys:
            sb_ = self.buf.get(k)
            if not sb_:
                continue
            for t in ([sb_["w"]] if sb_["w"] else []) + sb_["r"]:
                c, v = t
                if best.get(c, 0) < v:
                    best[c] = v
        for stream in streams:
            for c, v in best.items():
                self.wait(stream, (c, v))

    def finish(self):
        for chan in list(self.cnt):
            if self.cnt[chan]:
                self.wait("sync", (chan, self.cnt[chan]))


PS_KEYS = ["psA0", "psA1", "psB0", "psB1", "psC0", "psC1", "psD", "psT"]
ATTN_KEYS = ["qkband", "qb", "vb", "gt", "sq", "pt00", "pt01", "pt10",
             "pt11", "wqkb0", "wqkb1", "wvb", "bvb", "rrowb", "rb0", "rb1",
             "ob0", "ob1", "kb", "qstage", "bk0", "bk1", "bk2", "bk3",
             "spc", "pcol"] + PS_KEYS
ASM_KEYS = ["wob", "a0b", "b0b", "hw0", "hw0t", "bpjb", "zrow", "wg0b",
            "xnb", "sq1k", "cw", "gbuf", "eq1", "eq2", "w13_0", "w13_1",
            "w2_0", "w2_1", "s1b", "hwb", "hwtb", "wg1b", "wcols"]


def uv(b, part0, nparts, col0, ncols, dtype=BF16):
    u = b["U"][:]
    ap = bass.AP(u.tensor, part0 * UCOLS + col0, [[UCOLS, nparts], [1, ncols]])
    if dtype == F32:
        ap = ap.bitcast(F32)
    return ap


def alloc_bufs(nc, st, phase):
    b = {}

    def sb(name, shape, dt):
        b[name] = st.enter_context(nc.sbuf_tensor(name, shape, dt))
    sb("xnT", [128, ND * T], BF16)
    if phase == "l1":
        sb("ownT", [128, ND * TO], BF16)
        sb("oT", [128, ND * TO], BF16)
    else:
        sb("oT", [128, ND * T], BF16)
    if phase == "fused":
        ot = b["oT"][:]
        b["ownT"] = bass.AP(ot.tensor, ot.offset + ND * TO,
                            [[ot.ap[0][0], 128], [1, ND * TO]])
        sb("idxTs", [128, 8], I32)
        sb("idxXs", [128, 16], I32)
        b["idxT"], b["idxX"] = b["idxTs"], b["idxXs"]
    sb("U", [128, UCOLS], BF16)
    sb("xt0", [128, 1024], F32)
    sb("xt1", [128, 1024], F32)
    sb("stats", [128, 24], F32)
    sb("mx", [128, 4], F32)
    sb("mxb", [128, 1], BF16)
    sb("mrow", [1, 8], F32)
    sb("mrowb", [1, 1], BF16)
    sb("cm", [128, 1], F32)
    sb("ident", [128, 128], BF16)
    sb("ones128", [128, 1], BF16)
    sb("onesr", [65, 128], BF16)
    sb("bvr", [1, 256], BF16)
    sb("bqk", [128, 16], F32)
    sb("konst", [128, 2], F32)
    nc.const_aps.register(F32, EPS, b["konst"][:, 0:1])
    nc.const_aps.register(F32, 1e-30, b["konst"][:, 1:2])
    b["kb"] = uv(b, 0, 64, 0, 4096)
    b["qstage"] = uv(b, 64, 64, 0, 4096)
    b["qb"] = uv(b, 0, 64, 4096, 4096)
    b["rrow"] = uv(b, 64, 1, 4096, 2048, F32)
    b["rrowb"] = uv(b, 64, 1, 6144, 1024)
    b["vb"] = uv(b, 0, 128, 8192, 8320)
    b["gt"] = uv(b, 0, 128, 16512, GW)
    b["sq"] = uv(b, 0, 64, 21504, 1024)
    b["pt0"] = uv(b, 0, 128, 22528, 1024)
    b["pt1"] = uv(b, 0, 128, 23552, 1024)
    b["rbA"] = uv(b, 0, 64, 29952, 1024, F32)
    b["rbB"] = uv(b, 0, 64, 24576, 1024, F32)
    b["wqkb0"] = uv(b, 0, 128, 25600, 1024)
    b["wqkb1"] = uv(b, 0, 128, 26624, 1024)
    b["wvb"] = uv(b, 0, 128, 27648, 2048)
    b["bvb"] = uv(b, 0, 128, 29696, 256)
    b["rb"] = uv(b, 0, 64, 29952, 1024)
    b["ob0"] = uv(b, 0, 64, 30976, 1024)
    b["ob1"] = uv(b, 0, 64, 32000, 1024)
    if phase in ("l0", "fused"):
        b["wob"] = uv(b, 0, 128, 0, 8192)
        b["a0b"] = uv(b, 0, 128, 8192, 1024)
        b["b0b"] = uv(b, 0, 128, 9216, 1024)
        b["hw0"] = uv(b, 0, 128, 10240, 128)
        b["hw0t"] = uv(b, 0, 128, 10368, 128)
        b["bpjb"] = uv(b, 0, 1, 10496, 1024)
        b["zrow"] = uv(b, 0, 1, 11520, 512)
        b["wg0b"] = uv(b, 0, 128, 12032, 64)
        b["xnb"] = uv(b, 0, 128, 12096, 1024)
        b["sq1k"] = uv(b, 0, 128, 13120, 1024)
        b["cw"] = uv(b, 0, 128, 14144, 32, F32)
        b["gbuf"] = uv(b, 0, 128, 14176, 32, F32)
        b["eq1"] = uv(b, 0, 128, 14208, 32, F32)
        b["eq2"] = uv(b, 0, 128, 14240, 32, F32)
    if phase in ("l1", "fused"):
        b["wob"] = uv(b, 0, 128, 0, 8192)
        b["w13b0"] = uv(b, 0, 128, 0, 8192)
        b["w13b1"] = uv(b, 0, 128, 8192, 8192)
        b["w2b0"] = uv(b, 0, 128, 16384, 4096)
        b["w2b1"] = uv(b, 0, 128, 20480, 4096)
        b["s1b"] = uv(b, 0, 128, 24576, 512)
        b["hwb"] = uv(b, 0, 128, 25088, 512)
        b["hwtb"] = uv(b, 0, 128, 25600, 512)
        b["wg1b"] = uv(b, 0, 128, 26112, 64)
        b["wcols"] = uv(b, 0, 128, 26176, 288, F32)
        if phase == "l1":
            b["cw"] = uv(b, 0, 128, 28512, 32, F32)
            b["gbuf"] = uv(b, 0, 128, 28544, 32, F32)
            b["eq1"] = uv(b, 0, 128, 28576, 32, F32)
            b["eq2"] = uv(b, 0, 128, 28608, 32, F32)
            b["xnb"] = uv(b, 0, 128, 26464, 1024)
            b["sq1k"] = uv(b, 0, 128, 27488, 1024)
    b["psA"] = st.enter_context(nc.psum_tensor("psA", [128, 1024], F32))
    b["psB"] = st.enter_context(nc.psum_tensor("psB", [128, 1024], F32))
    b["psC"] = st.enter_context(nc.psum_tensor("psC", [128, 1024], F32))
    b["psD"] = st.enter_context(nc.psum_tensor("psD", [128, 512], F32))
    b["psT"] = st.enter_context(nc.psum_tensor("psT", [128, 1024], BF16))
    return b


def emit_consts(p, b):
    import struct
    r0 = p.nc.gpsimd.alloc_register("fill0")
    p.nc.gpsimd.reg_mov(r0, 0)
    rn = p.nc.gpsimd.alloc_register("filln")
    p.nc.gpsimd.reg_mov(rn, struct.unpack("<i", struct.pack("<f", -1e30))[0])
    b["fill0"], b["filln"] = r0, rn
    p.op("pool", lambda e: e.memset(b["ident"][:], 1.0), writes=["ident"])
    p.op("pool", lambda e: e.affine_select(
        b["ident"][:], b["ident"][:], [[-1, 128]], ALU.is_equal, b["fill0"],
        channel_multiplier=1), reads=["ident"], writes=["ident"])
    p.op("pool", lambda e: e.memset(b["ones128"][:], 1.0), writes=["ones128"])
    p.op("pool", lambda e: e.memset(b["onesr"][0:65, :], 1.0), writes=["onesr"])
    p.op("pool", lambda e: e.memset(b["konst"][:, 0:1], EPS), writes=["konst"])
    p.op("pool", lambda e: e.memset(b["konst"][:, 1:2], 1e-30),
         reads=["konst"], writes=["konst"])
    p.op("pool", lambda e: e.iota(
        b["stats"][:, 16:17], [[0, 1]], base=0, channel_multiplier=1,
        allow_small_or_imprecise_dtypes=True), writes=["pcol"])


def emit_vb_ones(p, b):
    p.op("pool", lambda e: e.memset(b["vb"][:], 0.0), writes=["vb"])
    vb = b["vb"].rearrange("p (k h c) -> p k h c", k=NT, h=4)
    p.op("pool", lambda e: e.memset(vb[:, :, :, 64:65], 1.0),
         reads=["vb"], writes=["vb"])


def _ln_tile(p, b, xt, xk, i, dst, dst_key, dstT):
    st = b["stats"]
    p.op("dve", lambda e: e.reduce_sum(st[:, 0:1], xt[:], axis=AX),
         reads=[xk], writes=["st0"])
    p.op("dve", lambda e: e.scalar_tensor_tensor(
        b["sq1k"], xt[:], 1.0, xt[:], ALU.mult, ALU.mult,
        accum_out=st[:, 1:2]), reads=[xk], writes=["st1", "sq1k"])
    p.op("dve", lambda e: e.tensor_scalar_mul(st[:, 2:3], st[:, 0:1], 1.0 / D),
         reads=["st0"], writes=["st2"])
    p.op("dve", lambda e: e.scalar_tensor_tensor(
        st[:, 3:4], st[:, 2:3], -1.0, st[:, 2:3], ALU.mult, ALU.mult),
        reads=["st2"], writes=["st3"])
    p.op("dve", lambda e: e.scalar_tensor_tensor(
        st[:, 4:5], st[:, 1:2], 1.0 / D, st[:, 3:4], ALU.mult, ALU.add),
        reads=["st1", "st3"], writes=["st4"])
    p.op("act", lambda e: e.activation(st[:, 5:6], st[:, 4:5], AF.Ln,
                                       bias=EPS),
         reads=["st4", "konst"], writes=["st5"])
    p.op("act", lambda e: e.activation(st[:, 6:7], st[:, 5:6], AF.Exp,
                                       scale=-0.5),
         reads=["st5"], writes=["st6"])
    p.op("dve", lambda e: e.tensor_scalar(
        b["xnb"], xt[:], st[:, 2:3], st[:, 6:7], ALU.subtract, ALU.mult),
        reads=[xk, "st2", "st6"], writes=["xnb"])
    for j in range(ND):
        p.op("pe", lambda e, j=j: e.transpose(
            b["psT"][:, j * 128:(j + 1) * 128],
            b["xnb"][:, j * 128:(j + 1) * 128], b["ident"][:]),
            reads=["xnb", "ident"], writes=["psT"])
    dv = dst[:]
    dst_ap = bass.AP(dv.tensor, dv.offset + i * 128,
                     [[dv.ap[0][0], 128], [dstT, ND], [1, 128]])
    p.op("dve", lambda e: e.tensor_copy(
        dst_ap, b["psT"][:].rearrange("p (j c) -> p j c", j=ND)),
        reads=["psT"], writes=[dst_key])


def emit_lnT(p, b, x_dram, ntiles, dst, dst_key, dstT):
    for i in range(ntiles):
        par = i % 2
        xt = b["xt0"] if par == 0 else b["xt1"]
        xk = "xt%d" % par
        p.op("sync", lambda e, i=i, xt=xt: e.dma_start(
            out=xt[:], in_=x_dram[i * 128:(i + 1) * 128, :]),
            writes=[xk], dma=True)
        _ln_tile(p, b, xt, xk, i, dst, dst_key, dstT)


def emit_qk_head(p, b, h, wqk_dram, has_bias, Tq, q_src_key=None):
    wk = "wqkb%d" % (h % 2)
    wb = b[wk]
    wslc = wqk_dram[:, h * 128:(h + 1) * 128]
    p.op("pool", lambda e: e.dma_start(
        out=wb.rearrange("p (ch c) -> p ch c", ch=ND),
        in_=wslc.rearrange("(ch p) c -> p ch c", p=128)),
        writes=[wk], dma=True)
    bk = b["bqk"][0:64, h:h + 1] if has_bias else 0.0
    bq = b["bqk"][64:128, h:h + 1] if has_bias else 0.0
    rds = ["bqk"] if has_bias else []
    nq = Tq // 512
    for tc in range(T // 512):
        bank = "psA0" if tc % 2 == 0 else "psB0"
        ps = b["psA"] if tc % 2 == 0 else b["psB"]
        for ch in range(ND):
            p.op("pe", lambda e, ch=ch, tc=tc, ps=ps: e.matmul(
                ps[:, 0:512], wb[:, ch * 128:(ch + 1) * 128],
                b["xnT"][:, ch * T + tc * 512: ch * T + tc * 512 + 512],
                start=(ch == 0), stop=(ch == ND - 1)),
                reads=[wk, "xnT"], writes=[bank])
        p.op("dve", lambda e, tc=tc, ps=ps: e.tensor_scalar(
            b["kb"][0:64, tc * 512:(tc + 1) * 512], ps[0:64, 0:512],
            bk, None, ALU.add),
            reads=[bank] + rds, writes=["kb"])
        if q_src_key is None and tc < nq:
            p.op("dve", lambda e, tc=tc, ps=ps: e.tensor_scalar(
                b["qstage"][0:64, tc * 512:(tc + 1) * 512], ps[64:128, 0:512],
                bq, 0.125, ALU.add, ALU.mult),
                reads=[bank] + rds, writes=["qstage"])
    if q_src_key is not None:
        qsrc = b[q_src_key]
        for tc in range(nq):
            bank = "psA0" if tc % 2 == 0 else "psB0"
            ps = b["psA"] if tc % 2 == 0 else b["psB"]
            for ch in range(ND):
                p.op("pe", lambda e, ch=ch, tc=tc, ps=ps: e.matmul(
                    ps[:, 0:512], wb[:, ch * 128:(ch + 1) * 128],
                    qsrc[:, ch * TO + tc * 512: ch * TO + tc * 512 + 512],
                    start=(ch == 0), stop=(ch == ND - 1)),
                    reads=[wk, q_src_key], writes=[bank])
            p.op("dve", lambda e, tc=tc, ps=ps: e.tensor_scalar(
                b["qstage"][0:64, tc * 512:(tc + 1) * 512], ps[64:128, 0:512],
                bq, 0.125, ALU.add, ALU.mult),
                reads=[bank] + rds, writes=["qstage"])
    p.op("sync", lambda e: e.dma_start(out=b["qb"][0:64, 0:Tq],
                                       in_=b["qstage"][0:64, 0:Tq]),
         reads=["qstage"], writes=["qb"], dma=True)


def emit_v_group(p, b, hg, wv_dram, bv_dram):
    p.op("pool", lambda e: e.dma_start(
        out=b["wvb"].rearrange("p (ch c) -> p ch c", ch=ND),
        in_=wv_dram[:, hg * 256:(hg + 1) * 256].rearrange(
            "(ch p) c -> p ch c", p=128)),
        writes=["wvb"], dma=True)
    have_bias = bv_dram is not None
    if have_bias:
        p.op("pool", lambda e: e.dma_start(
            out=b["bvr"][0:1, 0:256],
            in_=bv_dram[hg * 4:(hg + 1) * 4, :].rearrange("(o h) c -> o (h c)", o=1)),
            writes=["bvr"], dma=True)
        p.op("pe", lambda e: e.matmul(
            b["psD"][:, 0:256], b["onesr"][0:1, :], b["bvr"][0:1, 0:256],
            start=True, stop=True), reads=["bvr", "onesr"], writes=["psD"])
        p.op("dve", lambda e: e.tensor_copy(b["bvb"][:, 0:256],
                                            b["psD"][:, 0:256]),
             reads=["psD"], writes=["bvb"])
    vv = b["vb"]
    for i in range(NT):
        bank = "psA0" if i % 2 == 0 else "psB0"
        ps = b["psA"] if i % 2 == 0 else b["psB"]
        for ch in range(ND):
            p.op("pe", lambda e, ch=ch, i=i, ps=ps: e.matmul(
                ps[:, 0:256],
                b["xnT"][:, ch * T + i * 128: ch * T + i * 128 + 128],
                b["wvb"][:, ch * 256:(ch + 1) * 256],
                start=(ch == 0), stop=(ch == ND - 1)),
                reads=["wvb", "xnT"], writes=[bank])
        out_ap = bass.AP(vv.tensor, vv.offset + i * 260,
                         [[UCOLS, 128], [65, 4], [1, 64]])
        src = ps[:, 0:256].rearrange("p (h c) -> p h c", h=4)
        if have_bias:
            p.op("dve", lambda e, src=src, out_ap=out_ap:
                 e.scalar_tensor_tensor(
                     out_ap, src, 1.0,
                     b["bvb"][:, 0:256].rearrange("p (h c) -> p h c", h=4),
                     ALU.mult, ALU.add),
                 reads=[bank, "bvb"], writes=["vb"])
        else:
            p.op("dve", lambda e, src=src, out_ap=out_ap: e.tensor_copy(
                out_ap, src), reads=[bank], writes=["vb"])


def emit_head_M(p, b, Tq):
    for qc in range(Tq // 1024):
        p.op("act", lambda e, qc=qc: e.activation(
            b["sq"][0:64, 0:1024], b["qb"][0:64, qc * 1024:(qc + 1) * 1024],
            AF.Square), reads=["qb"], writes=["sq"])
        for j in range(8):
            p.op("pe", lambda e, qc=qc, j=j: e.matmul(
                b["psD"][:, qc * 8 + j: qc * 8 + j + 1],
                b["sq"][0:64, j * 128:(j + 1) * 128],
                b["ones128"][0:64, :], start=True, stop=True),
                reads=["sq", "ones128"], writes=["psD"])
    p.op("dve", lambda e: e.reduce_max(b["mx"][:, 0:1],
                                       b["psD"][:, 0:Tq // 128], axis=AX),
         reads=["psD"], writes=["mx0"])
    for kc in range(4):
        p.op("act", lambda e, kc=kc: e.activation(
            b["sq"][0:64, 0:1024], b["kb"][0:64, kc * 1024:(kc + 1) * 1024],
            AF.Square), reads=["kb", "sq"], writes=["sq"])
        for j in range(8):
            p.op("pe", lambda e, kc=kc, j=j: e.matmul(
                b["psD"][:, 64 + kc * 8 + j: 64 + kc * 8 + j + 1],
                b["sq"][0:64, j * 128:(j + 1) * 128],
                b["ones128"][0:64, :], start=True, stop=True),
                reads=["sq", "ones128"], writes=["psD"])
    p.op("dve", lambda e: e.reduce_max(b["mx"][:, 1:2],
                                       b["psD"][:, 64:64 + 32], axis=AX),
         reads=["psD"], writes=["mx1"])
    p.op("dve", lambda e: e.tensor_tensor(b["mx"][:, 2:3], b["mx"][:, 0:1],
                                          b["mx"][:, 1:2], ALU.mult),
         reads=["mx0", "mx1"], writes=["mx2"])
    p.op("dve", lambda e: e.tensor_copy(b["mxb"][:], b["mx"][:, 2:3]),
         reads=["mx2"], writes=["mxb"])
    p.op("pe", lambda e: e.transpose(b["psT"][0:1, 0:128], b["mxb"][:],
                                     b["ident"][:]),
         reads=["mxb", "ident"], writes=["psT"])
    p.op("dve", lambda e: e.reduce_max(b["mrow"][0:1, 0:1],
                                       b["psT"][0:1, 0:128], axis=AX),
         reads=["psT"], writes=["mrow0"])
    p.op("act", lambda e: e.activation(b["mrow"][0:1, 1:2],
                                       b["mrow"][0:1, 0:1], AF.Ln,
                                       bias=1e-30),
         reads=["mrow0", "konst"], writes=["mrow1"])
    p.op("act", lambda e: e.activation(b["mrow"][0:1, 2:3],
                                       b["mrow"][0:1, 1:2], AF.Exp,
                                       scale=0.5),
         reads=["mrow1"], writes=["mrow2"])
    p.op("dve", lambda e: e.tensor_scalar(b["mrow"][0:1, 3:4],
                                          b["mrow"][0:1, 2:3], -1.0, CEXP,
                                          ALU.mult, ALU.add),
         reads=["mrow2"], writes=["mrow3"])
    p.op("dve", lambda e: e.tensor_copy(b["mrowb"][0:1, 0:1],
                                        b["mrow"][0:1, 3:4]),
         reads=["mrow3"], writes=["mrowb"])
    p.op("pe", lambda e: e.matmul(b["psD"][:, 200:201], b["onesr"][0:1, :],
                                  b["mrowb"][0:1, 0:1], start=True,
                                  stop=True),
         reads=["mrowb", "onesr"], writes=["psD"])
    p.op("dve", lambda e: e.tensor_copy(b["cm"][:], b["psD"][:, 200:201]),
         reads=["psD"], writes=["cm"])


def emit_G(p, b, slope):
    pos = 0
    while pos < GW:
        wdt = min(1024, GW - pos)
        base = GOFF - pos
        p.op("pool", lambda e, pos=pos, wdt=wdt, base=base: e.iota(
            b["xt0"][:, 0:wdt], [[-1, wdt]], base=base, channel_multiplier=1,
            allow_small_or_imprecise_dtypes=True), writes=["xt0"])
        p.op("pool", lambda e, pos=pos, wdt=wdt, base=base: e.affine_select(
            b["xt0"][:, 0:wdt], b["xt0"][:, 0:wdt], [[1, wdt]], ALU.is_ge,
            b["filln"], base=-base, channel_multiplier=-1),
            reads=["xt0"], writes=["xt0"])
        p.op("act", lambda e, pos=pos, wdt=wdt: e.activation(
            b["gt"][:, pos:pos + wdt], b["xt0"][:, 0:wdt], AF.Exp,
            scale=float(slope)),
            reads=["xt0"], writes=["gt"])
        pos += wdt


def valid_kts(layer, q0, qw, slope):
    if layer == 1:
        return list(range(NT))
    kts = []
    for kt in range(NT):
        ktb = kt * 128
        if ktb >= q0 + qw:
            continue
        if slope * (q0 - ktb - 127) >= SKIP_GAP:
            continue
        kts.append(kt)
    return kts


def emit_attn_head(p, b, layer, h, Tq, oT_cols):
    slope = SLOPES[h] if layer == 0 else 0.0
    nch = Tq // QW
    psn = {0: "psA", 1: "psB"}
    ptn = {0: "pt0", 1: "pt1"}
    for cp in range(nch // 2):
        q0s = [(2 * cp) * QW, (2 * cp + 1) * QW]
        ktss = [valid_kts(layer, q0, QW, slope) for q0 in q0s]
        L = max(len(k) for k in ktss)

        def s_mm(s, n):
            kt = ktss[s][n]
            key = psn[s] + str(n % 2)
            p.op("pe", lambda e, kt=kt, s=s, n=n: e.matmul(
                b[psn[s]][:, (n % 2) * 512:(n % 2) * 512 + 512],
                b["kb"][0:64, kt * 128:(kt + 1) * 128],
                b["qb"][0:64, q0s[s]:q0s[s] + QW],
                start=True, stop=True),
                reads=["kb", "qb"], writes=[key])

        def do_exp(s, n):
            kt = ktss[s][n]
            ktb = kt * 128
            pskey = psn[s] + str(n % 2)
            pb = ptn[s]
            ptkey = pb + str(n % 2)
            ptv = b[pb][:, (n % 2) * 512:(n % 2) * 512 + 512]
            p.op("act", lambda e, pskey=pskey, ptv=ptv, s=s, n=n:
                 e.activation(ptv, b[psn[s]][:, (n % 2) * 512:
                                             (n % 2) * 512 + 512],
                              AF.Exp, bias=b["cm"][:]),
                 reads=[pskey, "cm"], writes=[ptkey])
            if layer == 0:
                off = GOFF + q0s[s] - ktb
                p.op("dve", lambda e, ptv=ptv, off=off: e.tensor_tensor(
                    ptv, ptv, b["gt"][:, off:off + QW], ALU.mult),
                    reads=[ptkey, "gt"], writes=[ptkey])

        def do_av(s, n):
            kt = ktss[s][n]
            ptkey = ptn[s] + str(n % 2)
            first, last = (n == 0), (n == len(ktss[s]) - 1)
            p.op("pe", lambda e, kt=kt, s=s, n=n, first=first, last=last:
                 e.matmul(
                     b["psC"][0:65, s * 512:s * 512 + 512],
                     b["vb"][:, kt * 260 + (h % 4) * 65:
                             kt * 260 + (h % 4) * 65 + 65],
                     b[ptn[s]][:, (n % 2) * 512:(n % 2) * 512 + 512],
                     start=first, stop=last),
                 reads=[ptkey, "vb"], writes=["psC%d" % s])

        for n in range(L + 1):
            for s in range(2):
                if n < len(ktss[s]):
                    s_mm(s, n)
                if 1 <= n <= len(ktss[s]):
                    do_av(s, n - 1)
                if n < len(ktss[s]):
                    do_exp(s, n)
        ch, sub = h // 2, (h % 2) * 64
        for s in range(2):
            rbv = b["rbA"] if s == 0 else b["rbB"]
            p.op("dve", lambda e, s=s: e.tensor_copy(
                b["rrowb"][0:1, 0:QW], b["psC"][64:65, s * 512:s * 512 + 512]),
                reads=["psC%d" % s], writes=["rrowb"])
            p.op("pe", lambda e: e.matmul(
                b["psD"][0:64, 0:512], b["onesr"][64:65, 0:64],
                b["rrowb"][0:1, 0:QW], start=True, stop=True),
                reads=["rrowb", "onesr"], writes=["psD"])
            p.op("dve", lambda e, rbv=rbv: e.reciprocal(
                rbv[0:64, 0:512], b["psD"][0:64, 0:512]),
                reads=["psD"], writes=["rb%d" % s])
            obk = "ob%d" % s
            p.op("dve", lambda e, s=s, obk=obk, rbv=rbv: e.tensor_tensor(
                b[obk][0:64, 0:QW], b["psC"][0:64, s * 512:s * 512 + 512],
                rbv[0:64, 0:512], ALU.mult),
                reads=["psC%d" % s, "rb%d" % s], writes=[obk])
            dst = b["oT"][sub:sub + 64,
                          ch * oT_cols + q0s[s]: ch * oT_cols + q0s[s] + QW]
            p.op("sync", lambda e, dst=dst, obk=obk: e.dma_start(
                out=dst, in_=b[obk][0:64, 0:QW]),
                reads=[obk], writes=["oT"], dma=True)


def _attn_layer(p, b, layer, t, Tq, oT_cols, q_src_key):
    wqk = t["wqk0"] if layer == 0 else t["wqk1"]
    wv = t["wv0"] if layer == 0 else t["wv1"]
    has_bias = (layer == 0)
    bv = t["bv0"] if layer == 0 else None
    for hg in range(4):
        emit_v_group(p, b, hg, wv, bv)
        for hh in range(4):
            h = hg * 4 + hh
            emit_qk_head(p, b, h, wqk, has_bias, Tq, q_src_key=q_src_key)
            emit_head_M(p, b, Tq)
            if layer == 0:
                emit_G(p, b, SLOPES[h])
            emit_attn_head(p, b, layer, h, Tq, oT_cols)


def _top2_softmax(p, b, sigmoid):
    st = b["stats"]
    p.op("dve", lambda e: e.tensor_copy(b["gbuf"][:, 0:8], b["psD"][:, 0:8]),
         reads=["psD"], writes=["gbuf"])
    p.op("dve", lambda e: e.reduce_max(st[:, 8:9], b["gbuf"][:, 0:8],
                                       axis=AX),
         reads=["gbuf"], writes=["m1"])
    p.op("dve", lambda e: e.tensor_scalar(b["eq1"][:, 0:8], b["gbuf"][:, 0:8],
                                          st[:, 8:9], None, ALU.is_equal),
         reads=["gbuf", "m1"], writes=["eq1"])
    p.op("dve", lambda e: e.scalar_tensor_tensor(
        b["eq2"][:, 0:8], b["eq1"][:, 0:8], -1e30, b["gbuf"][:, 0:8],
        ALU.mult, ALU.add), reads=["eq1", "gbuf"], writes=["eq2"])
    p.op("dve", lambda e: e.reduce_max(st[:, 9:10], b["eq2"][:, 0:8],
                                       axis=AX),
         reads=["eq2"], writes=["m2"])
    p.op("dve", lambda e: e.tensor_scalar(b["eq2"][:, 0:8], b["gbuf"][:, 0:8],
                                          st[:, 9:10], None, ALU.is_equal),
         reads=["gbuf", "m2"], writes=["eq2"])
    if not sigmoid:
        p.op("dve", lambda e: e.tensor_sub(st[:, 10:11], st[:, 9:10],
                                           st[:, 8:9]),
             reads=["m1", "m2"], writes=["d21"])
        p.op("act", lambda e: e.activation(st[:, 11:12], st[:, 10:11],
                                           AF.Exp),
             reads=["d21"], writes=["e21"])
        p.op("dve", lambda e: e.tensor_scalar_add(st[:, 12:13],
                                                  st[:, 11:12], 1.0),
             reads=["e21"], writes=["s21"])
        p.op("dve", lambda e: e.reciprocal(st[:, 13:14], st[:, 12:13]),
             reads=["s21"], writes=["wA"])
        p.op("dve", lambda e: e.tensor_tensor(st[:, 14:15], st[:, 11:12],
                                              st[:, 13:14], ALU.mult),
             reads=["e21", "wA"], writes=["wB"])
        wa, wb_ = 13, 14
    else:
        for k, src, kk in ((10, 8, "m1"), (11, 9, "m2")):
            p.op("act", lambda e, k=k, src=src: e.activation(
                st[:, k:k + 1], st[:, src:src + 1], AF.Exp, scale=-1.0),
                reads=[kk], writes=["sg%d" % k])
            p.op("dve", lambda e, k=k: e.tensor_scalar_add(
                st[:, k:k + 1], st[:, k:k + 1], 1.0),
                reads=["sg%d" % k], writes=["sg%d" % k])
            p.op("dve", lambda e, k=k: e.reciprocal(st[:, k + 2:k + 3],
                                                    st[:, k:k + 1]),
                 reads=["sg%d" % k], writes=["v%d" % k])
        p.op("dve", lambda e: e.tensor_add(st[:, 14:15], st[:, 12:13],
                                           st[:, 13:14]),
             reads=["v10", "v11"], writes=["vs"])
        p.op("dve", lambda e: e.reciprocal(st[:, 15:16], st[:, 14:15]),
             reads=["vs"], writes=["vsr"])
        p.op("dve", lambda e: e.tensor_scalar_mul(st[:, 15:16],
                                                  st[:, 15:16], 1.0 / 3.0),
             reads=["vsr"], writes=["vsr"])
        p.op("dve", lambda e: e.tensor_tensor(st[:, 10:11], st[:, 12:13],
                                              st[:, 15:16], ALU.mult),
             reads=["v10", "vsr"], writes=["wA"])
        p.op("dve", lambda e: e.tensor_tensor(st[:, 11:12], st[:, 13:14],
                                              st[:, 15:16], ALU.mult),
             reads=["v11", "vsr"], writes=["wB"])
        wa, wb_ = 10, 11
    p.op("dve", lambda e, wa=wa: e.tensor_scalar(
        b["cw"][:, 0:8], b["eq1"][:, 0:8], st[:, wa:wa + 1], None, ALU.mult),
        reads=["eq1", "wA"], writes=["cw"])
    p.op("dve", lambda e, wb_=wb_: e.scalar_tensor_tensor(
        b["cw"][:, 0:8], b["eq2"][:, 0:8], st[:, wb_:wb_ + 1],
        b["cw"][:, 0:8], ALU.mult, ALU.add),
        reads=["eq2", "wB", "cw"], writes=["cw"])


def emit_gate0_logits(p, b, i):
    for ch in range(ND):
        p.op("pe", lambda e, ch=ch: e.matmul(
            b["psD"][:, 0:8],
            b["xnT"][:, ch * T + i * 128: ch * T + i * 128 + 128],
            b["wg0b"][:, ch * 8:(ch + 1) * 8],
            start=(ch == 0), stop=(ch == ND - 1)),
            reads=["wg0b", "xnT"], writes=["psD"])


def emit_gate0_rest(p, b, i):
    _top2_softmax(p, b, sigmoid=False)
    for ch in range(ND):
        p.op("pe", lambda e, ch=ch: e.matmul(
            b["psB"][:, 0:128],
            b["xnT"][:, ch * T + i * 128: ch * T + i * 128 + 128],
            b["a0b"][:, ch * 128:(ch + 1) * 128],
            start=(ch == 0), stop=(ch == ND - 1)),
            reads=["a0b", "xnT"], writes=["psB0"])
    cw8 = b["cw"][:, 0:8]
    cwb = bass.AP(cw8.tensor, cw8.offset,
                  [list(d) for d in cw8.ap] + [[0, 16]])
    p.op("dve", lambda e: e.tensor_tensor(
        b["hw0"].rearrange("p (g r) -> p g r", g=8),
        b["psB"][:, 0:128].rearrange("p (g r) -> p g r", g=8), cwb,
        ALU.mult),
        reads=["psB0", "cw"], writes=["hw0"])
    p.op("pe", lambda e: e.transpose(b["psT"][:, 0:128], b["hw0"],
                                     b["ident"][:]),
         reads=["hw0", "ident"], writes=["psT"])
    p.op("dve", lambda e: e.tensor_copy(b["hw0t"], b["psT"][:, 0:128]),
         reads=["psT"], writes=["hw0t"])
    for half in range(2):
        p.op("pe", lambda e, half=half: e.matmul(
            b["psC"][:, half * 512:(half + 1) * 512], b["hw0t"],
            b["b0b"][:, half * 512:(half + 1) * 512],
            start=False, stop=False), reads=["hw0t", "b0b"],
            writes=["psC%d" % half])


def load_asm0_weights(p, b, t):
    p.op("pool", lambda e: e.memset(b["zrow"][0:1, :], 0.0), writes=["zrow"])
    p.op("pool", lambda e: e.dma_start(
        out=b["wob"].rearrange("p (ch c) -> p ch c", ch=ND),
        in_=t["wo0"].rearrange("(ch p) c -> p ch c", p=128)),
        writes=["wob"], dma=True)
    p.op("pool", lambda e: e.dma_start(
        out=b["wg0b"].rearrange("p (ch c) -> p ch c", ch=ND),
        in_=t["wg0"].rearrange("(ch p) c -> p ch c", p=128)),
        writes=["wg0b"], dma=True)
    p.op("pool", lambda e: e.dma_start(
        out=b["a0b"].rearrange("p (ch c) -> p ch c", ch=ND),
        in_=t["a0"].rearrange("(ch p) c -> p ch c", p=128)),
        writes=["a0b"], dma=True)
    p.op("pool", lambda e: e.dma_start(out=b["b0b"], in_=t["b0"]),
         writes=["b0b"], dma=True)
    p.op("pool", lambda e: e.dma_start(out=b["bpjb"][0:1, :], in_=t["bproj"]),
         writes=["bpjb"], dma=True)


def emit_assembly_l0(p, b, x0_dram, ln1, x1_dram, x1_out):
    for i in range(NT):
        psy = b["psC"]
        emit_gate0_logits(p, b, i)
        for half in range(2):
            p.op("pe", lambda e, half=half: e.matmul(
                psy[:, half * 512:(half + 1) * 512], b["onesr"][0:1, :],
                b["bpjb"][0:1, half * 512:(half + 1) * 512],
                start=True, stop=False),
                reads=["bpjb", "onesr"], writes=["psC%d" % half])
        for ch in range(ND):
            for half in range(2):
                p.op("pe", lambda e, ch=ch, i=i, half=half: e.matmul(
                    psy[:, half * 512:(half + 1) * 512],
                    b["oT"][:, ch * T + i * 128: ch * T + i * 128 + 128],
                    b["wob"][:, ch * 1024 + half * 512:
                             ch * 1024 + (half + 1) * 512],
                    start=False, stop=False), reads=["oT", "wob"],
                    writes=["psC%d" % half])
        emit_gate0_rest(p, b, i)
        for half in range(2):
            p.op("pe", lambda e, half=half: e.matmul(
                psy[:, half * 512:(half + 1) * 512], b["onesr"][0:1, :],
                b["zrow"][0:1, 0:512],
                start=False, stop=True),
                reads=["zrow", "onesr"], writes=["psC%d" % half])
        par = i % 2
        xt = b["xt0"] if par == 0 else b["xt1"]
        xk = "xt%d" % par
        p.op("sync", lambda e, i=i, xt=xt: e.dma_start(
            out=xt[:], in_=x0_dram[i * 128:(i + 1) * 128, :]),
            writes=[xk], dma=True)
        p.op("dve", lambda e, xt=xt: e.scalar_tensor_tensor(
            xt[:], psy[:], 1.0, xt[:], ALU.mult, ALU.add),
            reads=["psC0", "psC1"], writes=[xk])
        if x1_dram is not None:
            p.op("sync", lambda e, i=i, xt=xt: e.dma_start(
                out=x1_dram[i * 128:(i + 1) * 128, :], in_=xt[:]),
                reads=[xk], writes=["x1scr"], dma=True)
        if x1_out is not None:
            p.op("sync", lambda e, i=i, xt=xt: e.dma_start(
                out=x1_out[i * 128:(i + 1) * 128, :], in_=xt[:]),
                reads=[xk], writes=["x1out"], dma=True)
        if ln1:
            _ln_tile(p, b, xt, xk, i, b["xnT"], "xnT", T)


def yacc_slice(b, i):
    return b["xnT"][:, i * 2048:(i + 1) * 2048].bitcast(F32)


def emit_gating1_proj1(p, b, t):
    p.op("pool", lambda e: e.dma_start(
        out=b["wg1b"].rearrange("p (ch c) -> p ch c", ch=ND),
        in_=t["wg1"].rearrange("(ch p) c -> p ch c", p=128)),
        writes=["wg1b"], dma=True)
    p.op("pool", lambda e: e.dma_start(
        out=b["wob"].rearrange("p (ch c) -> p ch c", ch=ND),
        in_=t["wo1"].rearrange("(ch p) c -> p ch c", p=128)),
        writes=["wob"], dma=True)
    sh = b["wcols"].rearrange("p (i e) -> p i e", i=NTO)
    p.op("pool", lambda e: e.memset(sh[:, :, 0:1], 1.0 / 3.0),
         writes=["wcols"])
    for i in range(NTO):
        for ch in range(ND):
            p.op("pe", lambda e, ch=ch, i=i: e.matmul(
                b["psD"][:, 0:8],
                b["ownT"][:, ch * TO + i * 128: ch * TO + i * 128 + 128],
                b["wg1b"][:, ch * 8:(ch + 1) * 8],
                start=(ch == 0), stop=(ch == ND - 1)),
                reads=["wg1b", "ownT"], writes=["psD"])
        for ch in range(ND):
            for half in range(2):
                p.op("pe", lambda e, ch=ch, i=i, half=half: e.matmul(
                    b["psC"][:, half * 512:(half + 1) * 512],
                    b["oT"][:, ch * TO + i * 128: ch * TO + i * 128 + 128],
                    b["wob"][:, ch * 1024 + half * 512:
                             ch * 1024 + (half + 1) * 512],
                    start=(ch == 0), stop=(ch == ND - 1)),
                    reads=["oT", "wob"], writes=["psC%d" % half])
        _top2_softmax(p, b, sigmoid=True)
        p.op("dve", lambda e, i=i: e.tensor_copy(
            b["wcols"][:, i * NE + 1: i * NE + 9], b["cw"][:, 0:8]),
            reads=["cw"], writes=["wcols"])
        p.op("dve", lambda e, i=i: e.scalar_tensor_tensor(
            yacc_slice(b, i), b["psC"][:], 1.0, yacc_slice(b, i),
            ALU.mult, ALU.add),
            reads=["psC0", "psC1", "yacc"], writes=["yacc"])


def emit_ffn1(p, b, we13_dram, we2_dram):
    for e_i in range(NE):
        for fq in range(4):
            par = (e_i * 4 + fq) % 2
            w13k, w2k = "w13_%d" % par, "w2_%d" % par
            w13 = b["w13b0"] if par == 0 else b["w13b1"]
            w2 = b["w2b0"] if par == 0 else b["w2b1"]
            p.op("pool", lambda e, e_i=e_i, fq=fq, w13=w13: e.dma_start(
                out=w13.rearrange("p (ch c) -> p ch c", ch=ND),
                in_=we13_dram[e_i][:, fq * 1024:(fq + 1) * 1024].rearrange(
                    "(ch p) c -> p ch c", p=128)),
                writes=[w13k], dma=True)
            p.op("pool", lambda e, e_i=e_i, fq=fq, w2=w2: e.dma_start(
                out=w2.rearrange("p (ch c) -> p ch c", ch=4),
                in_=we2_dram[e_i][fq * 512:(fq + 1) * 512, :].rearrange(
                    "(ch p) c -> p ch c", p=128)),
                writes=[w2k], dma=True)
            for i in range(NTO):
                for ch in range(ND):
                    for hh in range(2):
                        p.op("pe", lambda e, ch=ch, i=i, hh=hh, w13=w13:
                             e.matmul(
                                 b["psA"][:, hh * 512:(hh + 1) * 512],
                                 b["ownT"][:, ch * TO + i * 128:
                                           ch * TO + i * 128 + 128],
                                 w13[:, ch * 1024 + hh * 512:
                                     ch * 1024 + (hh + 1) * 512],
                                 start=(ch == 0), stop=(ch == ND - 1)),
                             reads=[w13k, "ownT"], writes=["psA%d" % hh])
                p.op("act", lambda e: e.activation(
                    b["s1b"][:, 0:512], b["psA"][:, 0:512], AF.Silu),
                    reads=["psA0"], writes=["s1b"])
                wcol = b["wcols"][:, i * NE + e_i: i * NE + e_i + 1]
                p.op("dve", lambda e, wcol=wcol: e.scalar_tensor_tensor(
                    b["hwb"][:, 0:512], b["psA"][:, 512:1024], wcol,
                    b["s1b"][:, 0:512], ALU.mult, ALU.mult),
                    reads=["psA1", "s1b", "wcols"], writes=["hwb"])
                for j in range(4):
                    p.op("pe", lambda e, j=j: e.transpose(
                        b["psT"][:, j * 128:(j + 1) * 128],
                        b["hwb"][:, j * 128:(j + 1) * 128], b["ident"][:]),
                        reads=["hwb", "ident"], writes=["psT"])
                p.op("dve", lambda e: e.tensor_copy(b["hwtb"][:, 0:512],
                                                    b["psT"][:, 0:512]),
                     reads=["psT"], writes=["hwtb"])
                for j in range(4):
                    for half in range(2):
                        p.op("pe", lambda e, j=j, half=half, w2=w2: e.matmul(
                            b["psB"][:, half * 512:(half + 1) * 512],
                            b["hwtb"][:, j * 128:(j + 1) * 128],
                            w2[:, j * 1024 + half * 512:
                               j * 1024 + (half + 1) * 512],
                            start=(j == 0), stop=(j == 3)),
                            reads=["hwtb", w2k], writes=["psB%d" % half])
                p.op("dve", lambda e, i=i: e.scalar_tensor_tensor(
                    yacc_slice(b, i), b["psB"][:], 1.0, yacc_slice(b, i),
                    ALU.mult, ALU.add),
                    reads=["psB0", "psB1", "yacc"], writes=["yacc"])


def emit_assembly_l1(p, b, x1own_src, out_dram):
    for i in range(NTO):
        par = i % 2
        xt = b["xt0"] if par == 0 else b["xt1"]
        xk = "xt%d" % par
        if x1own_src[0] == "dram":
            p.op("sync", lambda e, i=i, xt=xt: e.dma_start(
                out=xt[:], in_=x1own_src[1][i * 128:(i + 1) * 128, :]),
                writes=[xk], dma=True)
        else:
            idxs, x1d = x1own_src[1], x1own_src[2]
            p.op("pool", lambda e, i=i, xt=xt: e.indirect_dma_start(
                out=xt[:], out_offset=None, in_=x1d,
                in_offset=IndirectOffsetOnAxis(idxs[:, i:i + 1], 0)),
                reads=["idxX", "x1scr"], writes=[xk], dma=True)
        p.op("dve", lambda e, i=i, xt=xt: e.scalar_tensor_tensor(
            xt[:], yacc_slice(b, i), 1.0, xt[:], ALU.mult, ALU.add),
            reads=["yacc"], writes=[xk])
        p.op("sync", lambda e, i=i, xt=xt: e.dma_start(
            out=out_dram[i * 128:(i + 1) * 128, :], in_=xt[:]),
            reads=[xk], writes=["x2out"], dma=True)


def build_l0(x1_as_output):
    nc = bass.Bass("TRN2", target_bir_lowering=False)
    t = {}
    t["x0"] = nc.dram_tensor("x0", [T, D], F32, kind="ExternalInput").ap()
    t["wqk0"] = nc.dram_tensor("wqk0", [D, H * 128], F32,
                               kind="ExternalInput").ap()
    t["wv0"] = nc.dram_tensor("wv0", [D, H * 64], F32,
                              kind="ExternalInput").ap()
    t["bqk0"] = nc.dram_tensor("bqk0", [128, 16], F32,
                               kind="ExternalInput").ap()
    t["bv0"] = nc.dram_tensor("bv0", [16, 64], F32,
                              kind="ExternalInput").ap()
    t["wo0"] = nc.dram_tensor("wo0", [D, D], F32, kind="ExternalInput").ap()
    t["bproj"] = nc.dram_tensor("bproj", [1, D], F32,
                                kind="ExternalInput").ap()
    t["wg0"] = nc.dram_tensor("wg0", [D, 8], F32, kind="ExternalInput").ap()
    t["a0"] = nc.dram_tensor("a0", [D, 128], F32, kind="ExternalInput").ap()
    t["b0"] = nc.dram_tensor("b0", [128, D], F32, kind="ExternalInput").ap()
    x1_out = nc.dram_tensor("x1", [T, D], F32, kind="ExternalOutput").ap() \
        if x1_as_output else None

    with ExitStack() as st:
        b = alloc_bufs(nc, st, "l0")
        p = Prog(nc, st)
        emit_consts(p, b)
        p.op("sync", lambda e: e.dma_start(out=b["bqk"][:], in_=t["bqk0"]),
             writes=["bqk"], dma=True)
        emit_lnT(p, b, t["x0"], NT, b["xnT"], "xnT", T)
        p.barrier(["xnb", "sq1k"])
        emit_vb_ones(p, b)
        _attn_layer(p, b, 0, t, T, T, None)
        p.barrier(ATTN_KEYS)
        load_asm0_weights(p, b, t)
        emit_assembly_l0(p, b, t["x0"], ln1=False, x1_dram=None,
                         x1_out=x1_out)
        p.finish()
    return nc


def build_l1():
    nc = bass.Bass("TRN2", target_bir_lowering=False)
    t = {}
    t["x1f"] = nc.dram_tensor("x1f", [T, D], F32, kind="ExternalInput").ap()
    t["x1o"] = nc.dram_tensor("x1o", [TO, D], F32, kind="ExternalInput").ap()
    t["wqk1"] = nc.dram_tensor("wqk1", [D, H * 128], F32,
                               kind="ExternalInput").ap()
    t["wv1"] = nc.dram_tensor("wv1", [D, H * 64], F32,
                              kind="ExternalInput").ap()
    t["wo1"] = nc.dram_tensor("wo1", [D, D], F32, kind="ExternalInput").ap()
    t["wg1"] = nc.dram_tensor("wg1", [D, 8], F32, kind="ExternalInput").ap()
    we13 = [nc.dram_tensor("we13_%d" % i, [D, 2 * FF], F32,
                           kind="ExternalInput").ap() for i in range(NE)]
    we2 = [nc.dram_tensor("we2_%d" % i, [FF, D], F32,
                          kind="ExternalInput").ap() for i in range(NE)]
    out = nc.dram_tensor("x2", [TO, D], F32, kind="ExternalOutput").ap()

    with ExitStack() as st:
        b = alloc_bufs(nc, st, "l1")
        p = Prog(nc, st)
        emit_consts(p, b)
        emit_lnT(p, b, t["x1f"], NT, b["xnT"], "xnT", T)
        emit_lnT(p, b, t["x1o"], NTO, b["ownT"], "ownT", TO)
        p.barrier(["xnb", "sq1k"])
        emit_vb_ones(p, b)
        _attn_layer(p, b, 1, t, TO, TO, "ownT")
        p.barrier(ATTN_KEYS)
        p.op("pool", lambda e: e.memset(b["xnT"][:], 0.0),
             reads=["xnT"], writes=["xnT", "yacc"])
        emit_gating1_proj1(p, b, t)
        p.barrier(["wob"])
        emit_ffn1(p, b, we13, we2)
        emit_assembly_l1(p, b, ("dram", t["x1o"]), out)
        p.finish()
    return nc




ASM0_KEYS = ["wob", "a0b", "b0b", "hw0", "hw0t", "bpjb", "zrow", "wg0b",
             "xnb", "sq1k", "cw", "gbuf", "eq1", "eq2"]


def build_fused():
    nc = bass.Bass("TRN2", target_bir_lowering=False)
    t = {}
    t["x0"] = nc.dram_tensor("x0", [T, D], F32, kind="ExternalInput").ap()
    t["wqk0"] = nc.dram_tensor("wqk0", [D, H * 128], F32,
                               kind="ExternalInput").ap()
    t["wv0"] = nc.dram_tensor("wv0", [D, H * 64], F32,
                              kind="ExternalInput").ap()
    t["bqk0"] = nc.dram_tensor("bqk0", [128, 16], F32,
                               kind="ExternalInput").ap()
    t["bv0"] = nc.dram_tensor("bv0", [16, 64], F32,
                              kind="ExternalInput").ap()
    t["wo0"] = nc.dram_tensor("wo0", [D, D], F32, kind="ExternalInput").ap()
    t["bproj"] = nc.dram_tensor("bproj", [1, D], F32,
                                kind="ExternalInput").ap()
    t["wg0"] = nc.dram_tensor("wg0", [D, 8], F32, kind="ExternalInput").ap()
    t["a0"] = nc.dram_tensor("a0", [D, 128], F32, kind="ExternalInput").ap()
    t["b0"] = nc.dram_tensor("b0", [128, D], F32, kind="ExternalInput").ap()
    t["wqk1"] = nc.dram_tensor("wqk1", [D, H * 128], F32,
                               kind="ExternalInput").ap()
    t["wv1"] = nc.dram_tensor("wv1", [D, H * 64], F32,
                              kind="ExternalInput").ap()
    t["wo1"] = nc.dram_tensor("wo1", [D, D], F32, kind="ExternalInput").ap()
    t["wg1"] = nc.dram_tensor("wg1", [D, 8], F32, kind="ExternalInput").ap()
    we13 = [nc.dram_tensor("we13_%d" % i, [D, 2 * FF], F32,
                           kind="ExternalInput").ap() for i in range(NE)]
    we2 = [nc.dram_tensor("we2_%d" % i, [FF, D], F32,
                          kind="ExternalInput").ap() for i in range(NE)]
    idxT = nc.dram_tensor("idxT", [128, 8], I32, kind="ExternalInput").ap()
    idxX = nc.dram_tensor("idxX", [128, 16], I32, kind="ExternalInput").ap()
    x1scr = nc.dram_tensor("x1scr", [T, D], F32, kind="Internal").ap()
    xn1scr = nc.dram_tensor("xn1scr", [D, T], BF16, kind="Internal").ap()
    out = nc.dram_tensor("x2", [TO, D], F32, kind="ExternalOutput").ap()

    with ExitStack() as st:
        b = alloc_bufs(nc, st, "fused")
        p = Prog(nc, st)
        emit_consts(p, b)
        p.op("sync", lambda e: e.dma_start(out=b["bqk"][:], in_=t["bqk0"]),
             writes=["bqk"], dma=True)
        p.op("sync", lambda e: e.dma_start(out=b["idxT"][:], in_=idxT),
             writes=["idxT"], dma=True)
        p.op("sync", lambda e: e.dma_start(out=b["idxX"][:], in_=idxX),
             writes=["idxX"], dma=True)
        emit_lnT(p, b, t["x0"], NT, b["xnT"], "xnT", T)
        p.barrier(["xnb", "sq1k"])
        emit_vb_ones(p, b)
        _attn_layer(p, b, 0, t, T, T, None)
        p.barrier(ATTN_KEYS)
        load_asm0_weights(p, b, t)
        emit_assembly_l0(p, b, t["x0"], ln1=True, x1_dram=x1scr, x1_out=None)
        for ch in range(ND):
            p.op("sync", lambda e, ch=ch: e.dma_start(
                out=xn1scr[ch * 128:(ch + 1) * 128, :],
                in_=b["xnT"][:, ch * T:(ch + 1) * T]),
                reads=["xnT"], writes=["xn1scr"], dma=True)
        p.barrier(ASM0_KEYS + ["oT"])
        xn1flat = bass.AP(xn1scr.tensor, 0, [[1, D * T], [1, 1]])
        for ch in range(ND):
            p.op("pool", lambda e, ch=ch: e.indirect_dma_start(
                out=b["ownT"][:, ch * TO:(ch + 1) * TO], out_offset=None,
                in_=xn1flat,
                in_offset=IndirectOffsetOnAxis(b["idxT"][:, ch:ch + 1], 0)),
                reads=["xn1scr", "idxT"], writes=["ownT"], dma=True)
        emit_vb_ones(p, b)
        _attn_layer(p, b, 1, t, TO, TO, "ownT")
        p.barrier(ATTN_KEYS)
        p.op("pool", lambda e: e.memset(b["xnT"][:], 0.0),
             reads=["xnT"], writes=["xnT", "yacc"])
        emit_gating1_proj1(p, b, t)
        p.barrier(["wob", "cw", "gbuf", "eq1", "eq2"])
        emit_ffn1(p, b, we13, we2)
        emit_assembly_l1(p, b, ("gather", b["idxX"], x1scr), out)
        p.finish()
    return nc



def _prep_shared(w):
    g = {}
    wa = np.asarray(w["L0_attn_w_attn"], np.float32)
    qk = np.empty((D, H * 128), np.float32)
    for h in range(H):
        qk[:, h * 128:h * 128 + 64] = wa[:, D + h * 64:D + (h + 1) * 64]
        qk[:, h * 128 + 64:h * 128 + 128] = wa[:, h * 64:(h + 1) * 64]
    g["wqk0"] = qk
    g["wv0"] = np.ascontiguousarray(wa[:, 2 * D:3 * D])
    ba = np.asarray(w["L0_attn_b_attn"], np.float32)
    bqk = np.empty((128, 16), np.float32)
    for h in range(H):
        bqk[0:64, h] = ba[D + h * 64:D + (h + 1) * 64]
        bqk[64:128, h] = ba[h * 64:(h + 1) * 64]
    g["bqk0"] = bqk
    g["bv0"] = np.ascontiguousarray(ba[2 * D:].reshape(16, 64))
    g["wo0"] = np.ascontiguousarray(np.asarray(w["L0_attn_w_proj"],
                                               np.float32))
    g["bproj"] = np.ascontiguousarray(
        np.asarray(w["L0_attn_b_proj"], np.float32)[None, :])
    g["wg0"] = np.ascontiguousarray(np.asarray(w["L0_ffn_Wg"], np.float32))
    g["a0"] = np.ascontiguousarray(
        np.asarray(w["L0_ffn_A"], np.float32).transpose(1, 0, 2).reshape(
            D, 128))
    g["b0"] = np.ascontiguousarray(
        np.asarray(w["L0_ffn_B"], np.float32).reshape(128, D))
    wq = np.asarray(w["L1_attn_W_q"], np.float32)
    wk2 = np.asarray(w["L1_attn_W_k"], np.float32)
    wv2 = np.asarray(w["L1_attn_W_v"], np.float32)
    qk1 = np.empty((D, H * 128), np.float32)
    for h in range(H):
        qk1[:, h * 128:h * 128 + 64] = wk2[h]
        qk1[:, h * 128 + 64:h * 128 + 128] = wq[h]
    g["wqk1"] = qk1
    g["wv1"] = np.ascontiguousarray(wv2.transpose(1, 0, 2).reshape(D, H * 64))
    g["wo1"] = np.ascontiguousarray(np.asarray(w["L1_attn_W_o"], np.float32))
    g["wg1"] = np.ascontiguousarray(np.asarray(w["L1_ffn_Wg"], np.float32))
    s1 = np.asarray(w["L1_ffn_Ws1"], np.float32)
    s3 = np.asarray(w["L1_ffn_Ws3"], np.float32)
    s2 = np.asarray(w["L1_ffn_Ws2"], np.float32)
    e1 = np.asarray(w["L1_ffn_We1"], np.float32)
    e3 = np.asarray(w["L1_ffn_We3"], np.float32)
    e2 = np.asarray(w["L1_ffn_We2"], np.float32)

    def inter(a, bm):
        out = np.empty((D, 2 * FF), np.float32)
        for fq in range(4):
            out[:, fq * 1024:fq * 1024 + 512] = a[:, fq * 512:(fq + 1) * 512]
            out[:, fq * 1024 + 512:(fq + 1) * 1024] = \
                bm[:, fq * 512:(fq + 1) * 512]
        return out
    for i in range(NE):
        if i == 0:
            g["we13_%d" % i] = inter(s1, s3)
            g["we2_%d" % i] = np.ascontiguousarray(s2)
        else:
            g["we13_%d" % i] = inter(e1[i - 1], e3[i - 1])
            g["we2_%d" % i] = np.ascontiguousarray(e2[i - 1])
    return g


FUSED = True


def kernel(**inputs):
    g = _prep_shared(inputs)
    x0 = np.ascontiguousarray(np.asarray(inputs["L0_attn_x"], np.float32))
    if FUSED:
        return kernel_fused(inputs, g, x0)

    nc0 = build_l0(x1_as_output=True)
    l0_keys = ["wqk0", "wv0", "bqk0", "bv0", "wo0", "bproj", "wg0", "a0",
               "b0"]
    in_maps0 = []
    for c in range(8):
        m = {"x0": x0[c // 2]}
        for k in l0_keys:
            m[k] = g[k]
        in_maps0.append(m)
    r0 = run_bass_kernel_spmd(nc0, in_maps0, core_ids=list(range(8)))
    x1 = [np.asarray(r0.results[2 * bb]["x1"]) for bb in range(B)]

    nc1 = build_l1()
    l1_keys = (["wqk1", "wv1", "wo1", "wg1"] +
               ["we13_%d" % i for i in range(NE)] +
               ["we2_%d" % i for i in range(NE)])
    in_maps1 = []
    for c in range(8):
        bb, hf = c // 2, c % 2
        m = {"x1f": x1[bb],
             "x1o": np.ascontiguousarray(x1[bb][hf * TO:(hf + 1) * TO])}
        for k in l1_keys:
            m[k] = g[k]
        in_maps1.append(m)
    r1 = run_bass_kernel_spmd(nc1, in_maps1, core_ids=list(range(8)))

    out = np.empty((B, T, D), np.float32)
    for c in range(8):
        bb, hf = c // 2, c % 2
        out[bb, hf * TO:(hf + 1) * TO] = np.asarray(r1.results[c]["x2"])
    return out


def kernel_fused(inputs, g, x0):
    nc = build_fused()
    keys = (["wqk0", "wv0", "bqk0", "bv0", "wo0", "bproj", "wg0", "a0", "b0",
             "wqk1", "wv1", "wo1", "wg1"] +
            ["we13_%d" % i for i in range(NE)] +
            ["we2_%d" % i for i in range(NE)])
    pidx = np.arange(128)
    in_maps = []
    for c in range(8):
        bb, hf = c // 2, c % 2
        own0 = hf * TO
        idxT_np = np.empty((128, 8), np.int32)
        for ch in range(ND):
            idxT_np[:, ch] = (ch * 128 + pidx) * T + own0
        idxX_np = np.empty((128, 16), np.int32)
        for i in range(NTO):
            idxX_np[:, i] = own0 + i * 128 + pidx
        m = {"x0": x0[bb], "idxT": idxT_np, "idxX": idxX_np}
        for k in keys:
            m[k] = g[k]
        in_maps.append(m)
    r = run_bass_kernel_spmd(nc, in_maps, core_ids=list(range(8)))
    out = np.empty((B, T, D), np.float32)
    for c in range(8):
        bb, hf = c // 2, c % 2
        out[bb, hf * TO:(hf + 1) * TO] = np.asarray(r.results[c]["x2"])
    return out


# version 09
# speedup vs baseline: 1.3532x; 1.2559x over previous
import numpy as np
import ml_dtypes
from contextlib import ExitStack

import concourse.bass as bass
import concourse.mybir as mybir
from concourse.bass import IndirectOffsetOnAxis
from concourse.bass_utils import run_bass_kernel_spmd

F32 = mybir.dt.float32
BF16 = mybir.dt.bfloat16
I32 = mybir.dt.int32
AF = mybir.ActivationFunctionType
ALU = mybir.AluOpType
AX = mybir.AxisListType.X

B, T, D, H, DH = 4, 4096, 1024, 16, 64
TO = T // 2
NT, NTO = T // 128, TO // 128
ND = D // 128
CEXP = 20.0
EPS = 1e-5
FF = 2048
NE = 9
SLOPES = [2.0 ** (-(8.0 / H) * (i + 1)) for i in range(H)]
GW = 4480
GOFF = 384
SKIP_GAP = 25.0
QW = 512
UCOLS = 33024


class Prog:
    def __init__(self, nc, st):
        self.nc = nc
        self.st = st
        self.e = {"pe": nc.tensor, "act": nc.scalar, "dve": nc.vector,
                  "pool": nc.gpsimd, "sync": nc.sync}
        self.sem = {}
        for sname in ("pe", "act", "dve", "pool"):
            self.sem[sname] = st.enter_context(nc.semaphore("sem_" + sname))
        self.cnt = {sname: 0 for sname in self.sem}
        self.waited = {sname: {} for sname in
                       ("pe", "act", "dve", "pool", "sync")}
        self.buf = {}

    def _dma_sem(self, key):
        name = "d_" + key
        if name not in self.sem:
            self.sem[name] = self.st.enter_context(
                self.nc.semaphore("sem_" + name))
            self.cnt[name] = 0
        return name

    def wait(self, stream, tok):
        if tok is None:
            return
        chan, v = tok
        if stream == "pe" and chan == "pe":
            return
        if v <= self.waited[stream].get(chan, 0):
            return
        self.e[stream].wait_ge(self.sem[chan], v)
        self.waited[stream][chan] = v

    def op(self, stream, fn, reads=(), writes=(), dma=False, deps=()):
        for t in deps:
            self.wait(stream, t)
        for k in reads:
            sb_ = self.buf.get(k)
            if sb_ and sb_["w"]:
                self.wait(stream, sb_["w"])
        for k in writes:
            sb_ = self.buf.get(k)
            if sb_:
                if sb_["w"]:
                    self.wait(stream, sb_["w"])
                for t in sb_["r"]:
                    self.wait(stream, t)
        ins = fn(self.e[stream])
        if dma:
            assert writes, "DMA ops must declare a write key"
            chan = self._dma_sem(writes[0])
            inc = 16
        else:
            chan = stream
            inc = 1
        ins.then_inc(self.sem[chan], inc)
        self.cnt[chan] += inc
        tok = (chan, self.cnt[chan])
        for k in reads:
            self.buf.setdefault(k, {"w": None, "r": []})["r"].append(tok)
        for k in writes:
            self.buf[k] = {"w": tok, "r": []}
        return tok

    def barrier(self, keys, streams=("pe", "act", "dve", "pool", "sync")):
        best = {}
        for k in keys:
            sb_ = self.buf.get(k)
            if not sb_:
                continue
            for t in ([sb_["w"]] if sb_["w"] else []) + sb_["r"]:
                c, v = t
                if best.get(c, 0) < v:
                    best[c] = v
        for stream in streams:
            for c, v in best.items():
                self.wait(stream, (c, v))

    def finish(self):
        for chan in list(self.cnt):
            if self.cnt[chan]:
                self.wait("sync", (chan, self.cnt[chan]))


PS_KEYS = ["psA0", "psA1", "psB0", "psB1", "psC0", "psC1", "psD", "psT"]
ATTN_KEYS = ["qkband", "qb", "vb", "gt", "sq", "pt00", "pt01", "pt10",
             "pt11", "wqkb0", "wqkb1", "wvb", "bvb", "rrowb", "rb0", "rb1",
             "ob0", "ob1", "kb", "qstage", "bk0", "bk1", "bk2", "bk3",
             "spc", "pcol"] + PS_KEYS
ASM_KEYS = ["wob", "a0b", "b0b", "hw0", "hw0t", "bpjb", "zrow", "wg0b",
            "xnb", "sq1k", "cw", "gbuf", "eq1", "eq2", "w13_0", "w13_1",
            "w2_0", "w2_1", "s1b", "hwb", "hwtb", "wg1b", "wcols"]


def uv(b, part0, nparts, col0, ncols, dtype=BF16):
    u = b["U"][:]
    ap = bass.AP(u.tensor, part0 * UCOLS + col0, [[UCOLS, nparts], [1, ncols]])
    if dtype == F32:
        ap = ap.bitcast(F32)
    return ap


def alloc_bufs(nc, st, phase):
    b = {}

    def sb(name, shape, dt):
        b[name] = st.enter_context(nc.sbuf_tensor(name, shape, dt))
    sb("xnT", [128, ND * T], BF16)
    if phase == "l1":
        sb("ownT", [128, ND * TO], BF16)
        sb("oT", [128, ND * TO], BF16)
    else:
        sb("oT", [128, ND * T], BF16)
    if phase == "fused":
        ot = b["oT"][:]
        b["ownT"] = bass.AP(ot.tensor, ot.offset + ND * TO,
                            [[ot.ap[0][0], 128], [1, ND * TO]])
        sb("idxTs", [128, 8], I32)
        sb("idxXs", [128, 16], I32)
        b["idxT"], b["idxX"] = b["idxTs"], b["idxXs"]
    sb("U", [128, UCOLS], BF16)
    sb("xt0", [128, 1024], F32)
    sb("xt1", [128, 1024], F32)
    sb("stats", [128, 24], F32)
    sb("mx", [128, 4], F32)
    sb("mxb", [128, 1], BF16)
    sb("mrow", [1, 8], F32)
    sb("mrowb", [1, 1], BF16)
    sb("cm", [128, 1], F32)
    sb("ident", [128, 128], BF16)
    sb("ones128", [128, 1], BF16)
    sb("onesr", [65, 128], BF16)
    sb("bvr", [1, 256], BF16)
    sb("bqk", [128, 16], F32)
    sb("konst", [128, 2], F32)
    nc.const_aps.register(F32, EPS, b["konst"][:, 0:1])
    nc.const_aps.register(F32, 1e-30, b["konst"][:, 1:2])
    b["kb"] = uv(b, 0, 64, 0, 4096)
    b["qstage"] = uv(b, 64, 64, 0, 4096)
    b["qb"] = uv(b, 0, 64, 4096, 4096)
    b["rrow"] = uv(b, 64, 1, 4096, 2048, F32)
    b["rrowb"] = uv(b, 64, 1, 6144, 1024)
    b["vb"] = uv(b, 0, 128, 8192, 8320)
    b["gt"] = uv(b, 0, 128, 16512, GW)
    b["sq"] = uv(b, 0, 64, 21504, 1024)
    b["pt0"] = uv(b, 0, 128, 22528, 1024)
    b["pt1"] = uv(b, 0, 128, 23552, 1024)
    b["rbA"] = uv(b, 0, 64, 29952, 1024, F32)
    b["rbB"] = uv(b, 0, 64, 24576, 1024, F32)
    b["wqkb0"] = uv(b, 0, 128, 25600, 1024)
    b["wqkb1"] = uv(b, 0, 128, 26624, 1024)
    b["wvb"] = uv(b, 0, 128, 27648, 2048)
    b["bvb"] = uv(b, 0, 128, 29696, 256)
    b["rb"] = uv(b, 0, 64, 29952, 1024)
    b["ob0"] = uv(b, 0, 64, 30976, 1024)
    b["ob1"] = uv(b, 0, 64, 32000, 1024)
    if phase in ("l0", "fused"):
        b["wob"] = uv(b, 0, 128, 0, 8192)
        b["a0b"] = uv(b, 0, 128, 8192, 1024)
        b["b0b"] = uv(b, 0, 128, 9216, 1024)
        b["hw0"] = uv(b, 0, 128, 10240, 128)
        b["hw0t"] = uv(b, 0, 128, 10368, 128)
        b["bpjb"] = uv(b, 0, 1, 10496, 1024)
        b["zrow"] = uv(b, 0, 1, 11520, 512)
        b["wg0b"] = uv(b, 0, 128, 12032, 64)
        b["xnb"] = uv(b, 0, 128, 12096, 1024)
        b["sq1k"] = uv(b, 0, 128, 13120, 1024)
        b["cw"] = uv(b, 0, 128, 14144, 32, F32)
        b["gbuf"] = uv(b, 0, 128, 14176, 32, F32)
        b["eq1"] = uv(b, 0, 128, 14208, 32, F32)
        b["eq2"] = uv(b, 0, 128, 14240, 32, F32)
    if phase in ("l1", "fused"):
        b["wob"] = uv(b, 0, 128, 0, 8192)
        b["w13b0"] = uv(b, 0, 128, 0, 8192)
        b["w13b1"] = uv(b, 0, 128, 8192, 8192)
        b["w2b0"] = uv(b, 0, 128, 16384, 4096)
        b["w2b1"] = uv(b, 0, 128, 20480, 4096)
        b["s1b"] = uv(b, 0, 128, 24576, 1024)
        b["wg1b"] = uv(b, 0, 128, 25600, 64)
        b["wcols"] = uv(b, 0, 128, 25664, 288, F32)
        b["hwT0"] = uv(b, 0, 128, 26240, 2048)
        b["hwT1"] = uv(b, 0, 128, 28288, 2048)
        if phase == "l1":
            b["cw"] = uv(b, 0, 128, 30336, 32, F32)
            b["gbuf"] = uv(b, 0, 128, 30368, 32, F32)
            b["eq1"] = uv(b, 0, 128, 30400, 32, F32)
            b["eq2"] = uv(b, 0, 128, 30432, 32, F32)
            b["xnb"] = uv(b, 0, 128, 30464, 1024)
            b["sq1k"] = uv(b, 0, 128, 31488, 1024)
    b["psA"] = st.enter_context(nc.psum_tensor("psA", [128, 1024], F32))
    b["psB"] = st.enter_context(nc.psum_tensor("psB", [128, 1024], F32))
    b["psC"] = st.enter_context(nc.psum_tensor("psC", [128, 1024], F32))
    b["psD"] = st.enter_context(nc.psum_tensor("psD", [128, 512], F32))
    b["psT"] = st.enter_context(nc.psum_tensor("psT", [128, 1024], BF16))
    return b


def emit_consts(p, b):
    import struct
    r0 = p.nc.gpsimd.alloc_register("fill0")
    p.nc.gpsimd.reg_mov(r0, 0)
    rn = p.nc.gpsimd.alloc_register("filln")
    p.nc.gpsimd.reg_mov(rn, struct.unpack("<i", struct.pack("<f", -1e30))[0])
    b["fill0"], b["filln"] = r0, rn
    p.op("pool", lambda e: e.memset(b["ident"][:], 1.0), writes=["ident"])
    p.op("pool", lambda e: e.affine_select(
        b["ident"][:], b["ident"][:], [[-1, 128]], ALU.is_equal, b["fill0"],
        channel_multiplier=1), reads=["ident"], writes=["ident"])
    p.op("pool", lambda e: e.memset(b["ones128"][:], 1.0), writes=["ones128"])
    p.op("pool", lambda e: e.memset(b["onesr"][0:65, :], 1.0), writes=["onesr"])
    p.op("pool", lambda e: e.memset(b["konst"][:, 0:1], EPS), writes=["konst"])
    p.op("pool", lambda e: e.memset(b["konst"][:, 1:2], 1e-30),
         reads=["konst"], writes=["konst"])
    p.op("pool", lambda e: e.iota(
        b["stats"][:, 16:17], [[0, 1]], base=0, channel_multiplier=1,
        allow_small_or_imprecise_dtypes=True), writes=["pcol"])


def emit_vb_ones(p, b):
    p.op("pool", lambda e: e.memset(b["vb"][:], 0.0), writes=["vb"])
    vb = b["vb"].rearrange("p (k h c) -> p k h c", k=NT, h=4)
    p.op("pool", lambda e: e.memset(vb[:, :, :, 64:65], 1.0),
         reads=["vb"], writes=["vb"])


def _ln_tile(p, b, xt, xk, i, dst, dst_key, dstT):
    st = b["stats"]
    p.op("dve", lambda e: e.reduce_sum(st[:, 0:1], xt[:], axis=AX),
         reads=[xk], writes=["st0"])
    p.op("dve", lambda e: e.scalar_tensor_tensor(
        b["sq1k"], xt[:], 1.0, xt[:], ALU.mult, ALU.mult,
        accum_out=st[:, 1:2]), reads=[xk], writes=["st1", "sq1k"])
    p.op("dve", lambda e: e.tensor_scalar_mul(st[:, 2:3], st[:, 0:1], 1.0 / D),
         reads=["st0"], writes=["st2"])
    p.op("dve", lambda e: e.scalar_tensor_tensor(
        st[:, 3:4], st[:, 2:3], -1.0, st[:, 2:3], ALU.mult, ALU.mult),
        reads=["st2"], writes=["st3"])
    p.op("dve", lambda e: e.scalar_tensor_tensor(
        st[:, 4:5], st[:, 1:2], 1.0 / D, st[:, 3:4], ALU.mult, ALU.add),
        reads=["st1", "st3"], writes=["st4"])
    p.op("act", lambda e: e.activation(st[:, 5:6], st[:, 4:5], AF.Ln,
                                       bias=EPS),
         reads=["st4", "konst"], writes=["st5"])
    p.op("act", lambda e: e.activation(st[:, 6:7], st[:, 5:6], AF.Exp,
                                       scale=-0.5),
         reads=["st5"], writes=["st6"])
    p.op("dve", lambda e: e.tensor_scalar(
        b["xnb"], xt[:], st[:, 2:3], st[:, 6:7], ALU.subtract, ALU.mult),
        reads=[xk, "st2", "st6"], writes=["xnb"])
    for j in range(ND):
        p.op("pe", lambda e, j=j: e.transpose(
            b["psT"][:, j * 128:(j + 1) * 128],
            b["xnb"][:, j * 128:(j + 1) * 128], b["ident"][:]),
            reads=["xnb", "ident"], writes=["psT"])
    dv = dst[:]
    dst_ap = bass.AP(dv.tensor, dv.offset + i * 128,
                     [[dv.ap[0][0], 128], [dstT, ND], [1, 128]])
    p.op("dve", lambda e: e.tensor_copy(
        dst_ap, b["psT"][:].rearrange("p (j c) -> p j c", j=ND)),
        reads=["psT"], writes=[dst_key])


def emit_lnT(p, b, x_dram, ntiles, dst, dst_key, dstT):
    for i in range(ntiles):
        par = i % 2
        xt = b["xt0"] if par == 0 else b["xt1"]
        xk = "xt%d" % par
        p.op("sync", lambda e, i=i, xt=xt: e.dma_start(
            out=xt[:], in_=x_dram[i * 128:(i + 1) * 128, :]),
            writes=[xk], dma=True)
        _ln_tile(p, b, xt, xk, i, dst, dst_key, dstT)


def emit_qk_head(p, b, h, wqk_dram, has_bias, Tq, q_src_key=None):
    wk = "wqkb%d" % (h % 2)
    wb = b[wk]
    wslc = wqk_dram[:, h * 128:(h + 1) * 128]
    p.op("pool", lambda e: e.dma_start(
        out=wb.rearrange("p (ch c) -> p ch c", ch=ND),
        in_=wslc.rearrange("(ch p) c -> p ch c", p=128)),
        writes=[wk], dma=True)
    bk = b["bqk"][0:64, h:h + 1] if has_bias else 0.0
    bq = b["bqk"][64:128, h:h + 1] if has_bias else 0.0
    rds = ["bqk"] if has_bias else []
    nq = Tq // 512
    for tc in range(T // 512):
        bank = "psA0" if tc % 2 == 0 else "psB0"
        ps = b["psA"] if tc % 2 == 0 else b["psB"]
        for ch in range(ND):
            p.op("pe", lambda e, ch=ch, tc=tc, ps=ps: e.matmul(
                ps[:, 0:512], wb[:, ch * 128:(ch + 1) * 128],
                b["xnT"][:, ch * T + tc * 512: ch * T + tc * 512 + 512],
                start=(ch == 0), stop=(ch == ND - 1)),
                reads=[wk, "xnT"], writes=[bank])
        p.op("dve", lambda e, tc=tc, ps=ps: e.tensor_scalar(
            b["kb"][0:64, tc * 512:(tc + 1) * 512], ps[0:64, 0:512],
            bk, None, ALU.add),
            reads=[bank] + rds, writes=["kb"])
        if q_src_key is None and tc < nq:
            p.op("dve", lambda e, tc=tc, ps=ps: e.tensor_scalar(
                b["qstage"][0:64, tc * 512:(tc + 1) * 512], ps[64:128, 0:512],
                bq, 0.125, ALU.add, ALU.mult),
                reads=[bank] + rds, writes=["qstage"])
    if q_src_key is not None:
        qsrc = b[q_src_key]
        for tc in range(nq):
            bank = "psA0" if tc % 2 == 0 else "psB0"
            ps = b["psA"] if tc % 2 == 0 else b["psB"]
            for ch in range(ND):
                p.op("pe", lambda e, ch=ch, tc=tc, ps=ps: e.matmul(
                    ps[:, 0:512], wb[:, ch * 128:(ch + 1) * 128],
                    qsrc[:, ch * TO + tc * 512: ch * TO + tc * 512 + 512],
                    start=(ch == 0), stop=(ch == ND - 1)),
                    reads=[wk, q_src_key], writes=[bank])
            p.op("dve", lambda e, tc=tc, ps=ps: e.tensor_scalar(
                b["qstage"][0:64, tc * 512:(tc + 1) * 512], ps[64:128, 0:512],
                bq, 0.125, ALU.add, ALU.mult),
                reads=[bank] + rds, writes=["qstage"])
    p.op("sync", lambda e: e.dma_start(out=b["qb"][0:64, 0:Tq],
                                       in_=b["qstage"][0:64, 0:Tq]),
         reads=["qstage"], writes=["qb"], dma=True)


def emit_v_group(p, b, hg, wv_dram, bv_dram):
    p.op("pool", lambda e: e.dma_start(
        out=b["wvb"].rearrange("p (ch c) -> p ch c", ch=ND),
        in_=wv_dram[:, hg * 256:(hg + 1) * 256].rearrange(
            "(ch p) c -> p ch c", p=128)),
        writes=["wvb"], dma=True)
    have_bias = bv_dram is not None
    if have_bias:
        p.op("pool", lambda e: e.dma_start(
            out=b["bvr"][0:1, 0:256],
            in_=bv_dram[hg * 4:(hg + 1) * 4, :].rearrange("(o h) c -> o (h c)", o=1)),
            writes=["bvr"], dma=True)
        p.op("pe", lambda e: e.matmul(
            b["psD"][:, 0:256], b["onesr"][0:1, :], b["bvr"][0:1, 0:256],
            start=True, stop=True), reads=["bvr", "onesr"], writes=["psD"])
        p.op("dve", lambda e: e.tensor_copy(b["bvb"][:, 0:256],
                                            b["psD"][:, 0:256]),
             reads=["psD"], writes=["bvb"])
    vv = b["vb"]
    for i in range(NT):
        bank = "psA0" if i % 2 == 0 else "psB0"
        ps = b["psA"] if i % 2 == 0 else b["psB"]
        for ch in range(ND):
            p.op("pe", lambda e, ch=ch, i=i, ps=ps: e.matmul(
                ps[:, 0:256],
                b["xnT"][:, ch * T + i * 128: ch * T + i * 128 + 128],
                b["wvb"][:, ch * 256:(ch + 1) * 256],
                start=(ch == 0), stop=(ch == ND - 1)),
                reads=["wvb", "xnT"], writes=[bank])
        out_ap = bass.AP(vv.tensor, vv.offset + i * 260,
                         [[UCOLS, 128], [65, 4], [1, 64]])
        src = ps[:, 0:256].rearrange("p (h c) -> p h c", h=4)
        if have_bias:
            p.op("dve", lambda e, src=src, out_ap=out_ap:
                 e.scalar_tensor_tensor(
                     out_ap, src, 1.0,
                     b["bvb"][:, 0:256].rearrange("p (h c) -> p h c", h=4),
                     ALU.mult, ALU.add),
                 reads=[bank, "bvb"], writes=["vb"])
        else:
            p.op("dve", lambda e, src=src, out_ap=out_ap: e.tensor_copy(
                out_ap, src), reads=[bank], writes=["vb"])


def emit_head_M(p, b, Tq):
    for qc in range(Tq // 1024):
        p.op("act", lambda e, qc=qc: e.activation(
            b["sq"][0:64, 0:1024], b["qb"][0:64, qc * 1024:(qc + 1) * 1024],
            AF.Square), reads=["qb"], writes=["sq"])
        for j in range(8):
            p.op("pe", lambda e, qc=qc, j=j: e.matmul(
                b["psD"][:, qc * 8 + j: qc * 8 + j + 1],
                b["sq"][0:64, j * 128:(j + 1) * 128],
                b["ones128"][0:64, :], start=True, stop=True),
                reads=["sq", "ones128"], writes=["psD"])
    p.op("dve", lambda e: e.reduce_max(b["mx"][:, 0:1],
                                       b["psD"][:, 0:Tq // 128], axis=AX),
         reads=["psD"], writes=["mx0"])
    for kc in range(4):
        p.op("act", lambda e, kc=kc: e.activation(
            b["sq"][0:64, 0:1024], b["kb"][0:64, kc * 1024:(kc + 1) * 1024],
            AF.Square), reads=["kb", "sq"], writes=["sq"])
        for j in range(8):
            p.op("pe", lambda e, kc=kc, j=j: e.matmul(
                b["psD"][:, 64 + kc * 8 + j: 64 + kc * 8 + j + 1],
                b["sq"][0:64, j * 128:(j + 1) * 128],
                b["ones128"][0:64, :], start=True, stop=True),
                reads=["sq", "ones128"], writes=["psD"])
    p.op("dve", lambda e: e.reduce_max(b["mx"][:, 1:2],
                                       b["psD"][:, 64:64 + 32], axis=AX),
         reads=["psD"], writes=["mx1"])
    p.op("dve", lambda e: e.tensor_tensor(b["mx"][:, 2:3], b["mx"][:, 0:1],
                                          b["mx"][:, 1:2], ALU.mult),
         reads=["mx0", "mx1"], writes=["mx2"])
    p.op("dve", lambda e: e.tensor_copy(b["mxb"][:], b["mx"][:, 2:3]),
         reads=["mx2"], writes=["mxb"])
    p.op("pe", lambda e: e.transpose(b["psT"][0:1, 0:128], b["mxb"][:],
                                     b["ident"][:]),
         reads=["mxb", "ident"], writes=["psT"])
    p.op("dve", lambda e: e.reduce_max(b["mrow"][0:1, 0:1],
                                       b["psT"][0:1, 0:128], axis=AX),
         reads=["psT"], writes=["mrow0"])
    p.op("act", lambda e: e.activation(b["mrow"][0:1, 1:2],
                                       b["mrow"][0:1, 0:1], AF.Ln,
                                       bias=1e-30),
         reads=["mrow0", "konst"], writes=["mrow1"])
    p.op("act", lambda e: e.activation(b["mrow"][0:1, 2:3],
                                       b["mrow"][0:1, 1:2], AF.Exp,
                                       scale=0.5),
         reads=["mrow1"], writes=["mrow2"])
    p.op("dve", lambda e: e.tensor_scalar(b["mrow"][0:1, 3:4],
                                          b["mrow"][0:1, 2:3], -1.0, CEXP,
                                          ALU.mult, ALU.add),
         reads=["mrow2"], writes=["mrow3"])
    p.op("dve", lambda e: e.tensor_copy(b["mrowb"][0:1, 0:1],
                                        b["mrow"][0:1, 3:4]),
         reads=["mrow3"], writes=["mrowb"])
    p.op("pe", lambda e: e.matmul(b["psD"][:, 200:201], b["onesr"][0:1, :],
                                  b["mrowb"][0:1, 0:1], start=True,
                                  stop=True),
         reads=["mrowb", "onesr"], writes=["psD"])
    p.op("dve", lambda e: e.tensor_copy(b["cm"][:], b["psD"][:, 200:201]),
         reads=["psD"], writes=["cm"])


def emit_G(p, b, slope):
    pos = 0
    while pos < GW:
        wdt = min(1024, GW - pos)
        base = GOFF - pos
        p.op("pool", lambda e, pos=pos, wdt=wdt, base=base: e.iota(
            b["xt0"][:, 0:wdt], [[-1, wdt]], base=base, channel_multiplier=1,
            allow_small_or_imprecise_dtypes=True), writes=["xt0"])
        p.op("pool", lambda e, pos=pos, wdt=wdt, base=base: e.affine_select(
            b["xt0"][:, 0:wdt], b["xt0"][:, 0:wdt], [[1, wdt]], ALU.is_ge,
            b["filln"], base=-base, channel_multiplier=-1),
            reads=["xt0"], writes=["xt0"])
        p.op("act", lambda e, pos=pos, wdt=wdt: e.activation(
            b["gt"][:, pos:pos + wdt], b["xt0"][:, 0:wdt], AF.Exp,
            scale=float(slope)),
            reads=["xt0"], writes=["gt"])
        pos += wdt


def valid_kts(layer, q0, qw, slope):
    if layer == 1:
        return list(range(NT))
    kts = []
    for kt in range(NT):
        ktb = kt * 128
        if ktb >= q0 + qw:
            continue
        if slope * (q0 - ktb - 127) >= SKIP_GAP:
            continue
        kts.append(kt)
    return kts


def emit_attn_head(p, b, layer, h, Tq, oT_cols):
    slope = SLOPES[h] if layer == 0 else 0.0
    nch = Tq // QW
    psn = {0: "psA", 1: "psB"}
    ptn = {0: "pt0", 1: "pt1"}
    for cp in range(nch // 2):
        q0s = [(2 * cp) * QW, (2 * cp + 1) * QW]
        ktss = [valid_kts(layer, q0, QW, slope) for q0 in q0s]
        L = max(len(k) for k in ktss)

        def s_mm(s, n):
            kt = ktss[s][n]
            key = psn[s] + str(n % 2)
            p.op("pe", lambda e, kt=kt, s=s, n=n: e.matmul(
                b[psn[s]][:, (n % 2) * 512:(n % 2) * 512 + 512],
                b["kb"][0:64, kt * 128:(kt + 1) * 128],
                b["qb"][0:64, q0s[s]:q0s[s] + QW],
                start=True, stop=True),
                reads=["kb", "qb"], writes=[key])

        def do_exp(s, n):
            kt = ktss[s][n]
            ktb = kt * 128
            pskey = psn[s] + str(n % 2)
            pb = ptn[s]
            ptkey = pb + str(n % 2)
            ptv = b[pb][:, (n % 2) * 512:(n % 2) * 512 + 512]
            p.op("act", lambda e, pskey=pskey, ptv=ptv, s=s, n=n:
                 e.activation(ptv, b[psn[s]][:, (n % 2) * 512:
                                             (n % 2) * 512 + 512],
                              AF.Exp, bias=b["cm"][:]),
                 reads=[pskey, "cm"], writes=[ptkey])
            if layer == 0:
                off = GOFF + q0s[s] - ktb
                p.op("dve", lambda e, ptv=ptv, off=off: e.tensor_tensor(
                    ptv, ptv, b["gt"][:, off:off + QW], ALU.mult),
                    reads=[ptkey, "gt"], writes=[ptkey])

        def do_av(s, n):
            kt = ktss[s][n]
            ptkey = ptn[s] + str(n % 2)
            first, last = (n == 0), (n == len(ktss[s]) - 1)
            p.op("pe", lambda e, kt=kt, s=s, n=n, first=first, last=last:
                 e.matmul(
                     b["psC"][0:65, s * 512:s * 512 + 512],
                     b["vb"][:, kt * 260 + (h % 4) * 65:
                             kt * 260 + (h % 4) * 65 + 65],
                     b[ptn[s]][:, (n % 2) * 512:(n % 2) * 512 + 512],
                     start=first, stop=last),
                 reads=[ptkey, "vb"], writes=["psC%d" % s])

        for n in range(L + 1):
            for s in range(2):
                if n < len(ktss[s]):
                    s_mm(s, n)
                if 1 <= n <= len(ktss[s]):
                    do_av(s, n - 1)
                if n < len(ktss[s]):
                    do_exp(s, n)
        ch, sub = h // 2, (h % 2) * 64
        for s in range(2):
            rbv = b["rbA"] if s == 0 else b["rbB"]
            p.op("dve", lambda e, s=s: e.tensor_copy(
                b["rrowb"][0:1, 0:QW], b["psC"][64:65, s * 512:s * 512 + 512]),
                reads=["psC%d" % s], writes=["rrowb"])
            p.op("pe", lambda e: e.matmul(
                b["psD"][0:64, 0:512], b["onesr"][64:65, 0:64],
                b["rrowb"][0:1, 0:QW], start=True, stop=True),
                reads=["rrowb", "onesr"], writes=["psD"])
            p.op("dve", lambda e, rbv=rbv: e.reciprocal(
                rbv[0:64, 0:512], b["psD"][0:64, 0:512]),
                reads=["psD"], writes=["rb%d" % s])
            obk = "ob%d" % s
            p.op("dve", lambda e, s=s, obk=obk, rbv=rbv: e.tensor_tensor(
                b[obk][0:64, 0:QW], b["psC"][0:64, s * 512:s * 512 + 512],
                rbv[0:64, 0:512], ALU.mult),
                reads=["psC%d" % s, "rb%d" % s], writes=[obk])
            dst = b["oT"][sub:sub + 64,
                          ch * oT_cols + q0s[s]: ch * oT_cols + q0s[s] + QW]
            p.op("sync", lambda e, dst=dst, obk=obk: e.dma_start(
                out=dst, in_=b[obk][0:64, 0:QW]),
                reads=[obk], writes=["oT"], dma=True)


def _attn_layer(p, b, layer, t, Tq, oT_cols, q_src_key):
    wqk = t["wqk0"] if layer == 0 else t["wqk1"]
    wv = t["wv0"] if layer == 0 else t["wv1"]
    has_bias = (layer == 0)
    bv = t["bv0"] if layer == 0 else None
    for hg in range(4):
        emit_v_group(p, b, hg, wv, bv)
        for hh in range(4):
            h = hg * 4 + hh
            emit_qk_head(p, b, h, wqk, has_bias, Tq, q_src_key=q_src_key)
            emit_head_M(p, b, Tq)
            if layer == 0:
                emit_G(p, b, SLOPES[h])
            emit_attn_head(p, b, layer, h, Tq, oT_cols)


def _top2_softmax(p, b, sigmoid):
    st = b["stats"]
    p.op("dve", lambda e: e.tensor_copy(b["gbuf"][:, 0:8], b["psD"][:, 0:8]),
         reads=["psD"], writes=["gbuf"])
    p.op("dve", lambda e: e.reduce_max(st[:, 8:9], b["gbuf"][:, 0:8],
                                       axis=AX),
         reads=["gbuf"], writes=["m1"])
    p.op("dve", lambda e: e.tensor_scalar(b["eq1"][:, 0:8], b["gbuf"][:, 0:8],
                                          st[:, 8:9], None, ALU.is_equal),
         reads=["gbuf", "m1"], writes=["eq1"])
    p.op("dve", lambda e: e.scalar_tensor_tensor(
        b["eq2"][:, 0:8], b["eq1"][:, 0:8], -1e30, b["gbuf"][:, 0:8],
        ALU.mult, ALU.add), reads=["eq1", "gbuf"], writes=["eq2"])
    p.op("dve", lambda e: e.reduce_max(st[:, 9:10], b["eq2"][:, 0:8],
                                       axis=AX),
         reads=["eq2"], writes=["m2"])
    p.op("dve", lambda e: e.tensor_scalar(b["eq2"][:, 0:8], b["gbuf"][:, 0:8],
                                          st[:, 9:10], None, ALU.is_equal),
         reads=["gbuf", "m2"], writes=["eq2"])
    if not sigmoid:
        p.op("dve", lambda e: e.tensor_sub(st[:, 10:11], st[:, 9:10],
                                           st[:, 8:9]),
             reads=["m1", "m2"], writes=["d21"])
        p.op("act", lambda e: e.activation(st[:, 11:12], st[:, 10:11],
                                           AF.Exp),
             reads=["d21"], writes=["e21"])
        p.op("dve", lambda e: e.tensor_scalar_add(st[:, 12:13],
                                                  st[:, 11:12], 1.0),
             reads=["e21"], writes=["s21"])
        p.op("dve", lambda e: e.reciprocal(st[:, 13:14], st[:, 12:13]),
             reads=["s21"], writes=["wA"])
        p.op("dve", lambda e: e.tensor_tensor(st[:, 14:15], st[:, 11:12],
                                              st[:, 13:14], ALU.mult),
             reads=["e21", "wA"], writes=["wB"])
        wa, wb_ = 13, 14
    else:
        for k, src, kk in ((10, 8, "m1"), (11, 9, "m2")):
            p.op("act", lambda e, k=k, src=src: e.activation(
                st[:, k:k + 1], st[:, src:src + 1], AF.Exp, scale=-1.0),
                reads=[kk], writes=["sg%d" % k])
            p.op("dve", lambda e, k=k: e.tensor_scalar_add(
                st[:, k:k + 1], st[:, k:k + 1], 1.0),
                reads=["sg%d" % k], writes=["sg%d" % k])
            p.op("dve", lambda e, k=k: e.reciprocal(st[:, k + 2:k + 3],
                                                    st[:, k:k + 1]),
                 reads=["sg%d" % k], writes=["v%d" % k])
        p.op("dve", lambda e: e.tensor_add(st[:, 14:15], st[:, 12:13],
                                           st[:, 13:14]),
             reads=["v10", "v11"], writes=["vs"])
        p.op("dve", lambda e: e.reciprocal(st[:, 15:16], st[:, 14:15]),
             reads=["vs"], writes=["vsr"])
        p.op("dve", lambda e: e.tensor_scalar_mul(st[:, 15:16],
                                                  st[:, 15:16], 1.0 / 3.0),
             reads=["vsr"], writes=["vsr"])
        p.op("dve", lambda e: e.tensor_tensor(st[:, 10:11], st[:, 12:13],
                                              st[:, 15:16], ALU.mult),
             reads=["v10", "vsr"], writes=["wA"])
        p.op("dve", lambda e: e.tensor_tensor(st[:, 11:12], st[:, 13:14],
                                              st[:, 15:16], ALU.mult),
             reads=["v11", "vsr"], writes=["wB"])
        wa, wb_ = 10, 11
    p.op("dve", lambda e, wa=wa: e.tensor_scalar(
        b["cw"][:, 0:8], b["eq1"][:, 0:8], st[:, wa:wa + 1], None, ALU.mult),
        reads=["eq1", "wA"], writes=["cw"])
    p.op("dve", lambda e, wb_=wb_: e.scalar_tensor_tensor(
        b["cw"][:, 0:8], b["eq2"][:, 0:8], st[:, wb_:wb_ + 1],
        b["cw"][:, 0:8], ALU.mult, ALU.add),
        reads=["eq2", "wB", "cw"], writes=["cw"])


def emit_gate0_logits(p, b, i):
    for ch in range(ND):
        p.op("pe", lambda e, ch=ch: e.matmul(
            b["psD"][:, 0:8],
            b["xnT"][:, ch * T + i * 128: ch * T + i * 128 + 128],
            b["wg0b"][:, ch * 8:(ch + 1) * 8],
            start=(ch == 0), stop=(ch == ND - 1)),
            reads=["wg0b", "xnT"], writes=["psD"])


def emit_gate0_rest(p, b, i):
    _top2_softmax(p, b, sigmoid=False)
    for ch in range(ND):
        p.op("pe", lambda e, ch=ch: e.matmul(
            b["psB"][:, 0:128],
            b["xnT"][:, ch * T + i * 128: ch * T + i * 128 + 128],
            b["a0b"][:, ch * 128:(ch + 1) * 128],
            start=(ch == 0), stop=(ch == ND - 1)),
            reads=["a0b", "xnT"], writes=["psB0"])
    cw8 = b["cw"][:, 0:8]
    cwb = bass.AP(cw8.tensor, cw8.offset,
                  [list(d) for d in cw8.ap] + [[0, 16]])
    p.op("dve", lambda e: e.tensor_tensor(
        b["hw0"].rearrange("p (g r) -> p g r", g=8),
        b["psB"][:, 0:128].rearrange("p (g r) -> p g r", g=8), cwb,
        ALU.mult),
        reads=["psB0", "cw"], writes=["hw0"])
    p.op("pe", lambda e: e.transpose(b["psT"][:, 0:128], b["hw0"],
                                     b["ident"][:]),
         reads=["hw0", "ident"], writes=["psT"])
    p.op("dve", lambda e: e.tensor_copy(b["hw0t"], b["psT"][:, 0:128]),
         reads=["psT"], writes=["hw0t"])
    for half in range(2):
        p.op("pe", lambda e, half=half: e.matmul(
            b["psC"][:, half * 512:(half + 1) * 512], b["hw0t"],
            b["b0b"][:, half * 512:(half + 1) * 512],
            start=False, stop=False), reads=["hw0t", "b0b"],
            writes=["psC%d" % half])


def load_asm0_weights(p, b, t):
    p.op("pool", lambda e: e.memset(b["zrow"][0:1, :], 0.0), writes=["zrow"])
    p.op("pool", lambda e: e.dma_start(
        out=b["wob"].rearrange("p (ch c) -> p ch c", ch=ND),
        in_=t["wo0"].rearrange("(ch p) c -> p ch c", p=128)),
        writes=["wob"], dma=True)
    p.op("pool", lambda e: e.dma_start(
        out=b["wg0b"].rearrange("p (ch c) -> p ch c", ch=ND),
        in_=t["wg0"].rearrange("(ch p) c -> p ch c", p=128)),
        writes=["wg0b"], dma=True)
    p.op("pool", lambda e: e.dma_start(
        out=b["a0b"].rearrange("p (ch c) -> p ch c", ch=ND),
        in_=t["a0"].rearrange("(ch p) c -> p ch c", p=128)),
        writes=["a0b"], dma=True)
    p.op("pool", lambda e: e.dma_start(out=b["b0b"], in_=t["b0"]),
         writes=["b0b"], dma=True)
    p.op("pool", lambda e: e.dma_start(out=b["bpjb"][0:1, :], in_=t["bproj"]),
         writes=["bpjb"], dma=True)


def emit_assembly_l0(p, b, x0_dram, ln1, x1_dram, x1_out):
    for i in range(NT):
        psy = b["psC"]
        emit_gate0_logits(p, b, i)
        for half in range(2):
            p.op("pe", lambda e, half=half: e.matmul(
                psy[:, half * 512:(half + 1) * 512], b["onesr"][0:1, :],
                b["bpjb"][0:1, half * 512:(half + 1) * 512],
                start=True, stop=False),
                reads=["bpjb", "onesr"], writes=["psC%d" % half])
        for ch in range(ND):
            for half in range(2):
                p.op("pe", lambda e, ch=ch, i=i, half=half: e.matmul(
                    psy[:, half * 512:(half + 1) * 512],
                    b["oT"][:, ch * T + i * 128: ch * T + i * 128 + 128],
                    b["wob"][:, ch * 1024 + half * 512:
                             ch * 1024 + (half + 1) * 512],
                    start=False, stop=False), reads=["oT", "wob"],
                    writes=["psC%d" % half])
        emit_gate0_rest(p, b, i)
        for half in range(2):
            p.op("pe", lambda e, half=half: e.matmul(
                psy[:, half * 512:(half + 1) * 512], b["onesr"][0:1, :],
                b["zrow"][0:1, 0:512],
                start=False, stop=True),
                reads=["zrow", "onesr"], writes=["psC%d" % half])
        par = i % 2
        xt = b["xt0"] if par == 0 else b["xt1"]
        xk = "xt%d" % par
        p.op("sync", lambda e, i=i, xt=xt: e.dma_start(
            out=xt[:], in_=x0_dram[i * 128:(i + 1) * 128, :]),
            writes=[xk], dma=True)
        p.op("dve", lambda e, xt=xt: e.scalar_tensor_tensor(
            xt[:], psy[:], 1.0, xt[:], ALU.mult, ALU.add),
            reads=["psC0", "psC1"], writes=[xk])
        if x1_dram is not None:
            p.op("sync", lambda e, i=i, xt=xt: e.dma_start(
                out=x1_dram[i * 128:(i + 1) * 128, :], in_=xt[:]),
                reads=[xk], writes=["x1scr"], dma=True)
        if x1_out is not None:
            p.op("sync", lambda e, i=i, xt=xt: e.dma_start(
                out=x1_out[i * 128:(i + 1) * 128, :], in_=xt[:]),
                reads=[xk], writes=["x1out"], dma=True)
        if ln1:
            _ln_tile(p, b, xt, xk, i, b["xnT"], "xnT", T)


def yacc_slice(b, i):
    return b["xnT"][:, i * 2048:(i + 1) * 2048].bitcast(F32)


def emit_gating1_proj1(p, b, t):
    p.op("pool", lambda e: e.dma_start(
        out=b["wg1b"].rearrange("p (ch c) -> p ch c", ch=ND),
        in_=t["wg1"].rearrange("(ch p) c -> p ch c", p=128)),
        writes=["wg1b"], dma=True)
    p.op("pool", lambda e: e.dma_start(
        out=b["wob"].rearrange("p (ch c) -> p ch c", ch=ND),
        in_=t["wo1"].rearrange("(ch p) c -> p ch c", p=128)),
        writes=["wob"], dma=True)
    sh = b["wcols"].rearrange("p (i e) -> p i e", i=NTO)
    p.op("pool", lambda e: e.memset(sh[:, :, 0:1], 1.0 / 3.0),
         writes=["wcols"])
    for i in range(NTO):
        for ch in range(ND):
            p.op("pe", lambda e, ch=ch, i=i: e.matmul(
                b["psD"][:, 0:8],
                b["ownT"][:, ch * TO + i * 128: ch * TO + i * 128 + 128],
                b["wg1b"][:, ch * 8:(ch + 1) * 8],
                start=(ch == 0), stop=(ch == ND - 1)),
                reads=["wg1b", "ownT"], writes=["psD"])
        for ch in range(ND):
            for half in range(2):
                p.op("pe", lambda e, ch=ch, i=i, half=half: e.matmul(
                    b["psC"][:, half * 512:(half + 1) * 512],
                    b["oT"][:, ch * TO + i * 128: ch * TO + i * 128 + 128],
                    b["wob"][:, ch * 1024 + half * 512:
                             ch * 1024 + (half + 1) * 512],
                    start=(ch == 0), stop=(ch == ND - 1)),
                    reads=["oT", "wob"], writes=["psC%d" % half])
        _top2_softmax(p, b, sigmoid=True)
        p.op("dve", lambda e, i=i: e.tensor_copy(
            b["wcols"][:, i * NE + 1: i * NE + 9], b["cw"][:, 0:8]),
            reads=["cw"], writes=["wcols"])
        p.op("dve", lambda e, i=i: e.scalar_tensor_tensor(
            yacc_slice(b, i), b["psC"][:], 1.0, yacc_slice(b, i),
            ALU.mult, ALU.add),
            reads=["psC0", "psC1", "yacc"], writes=["yacc"])


def emit_ffn1(p, b, we13_dram, we2_dram):
    dq = []

    def pop_d():
        if dq:
            dq.pop(0)()

    for e_i in range(NE):
        for fq in range(4):
            par = (e_i * 4 + fq) % 2
            w13k, w2k = "w13_%d" % par, "w2_%d" % par
            w13 = b["w13b0"] if par == 0 else b["w13b1"]
            w2 = b["w2b0"] if par == 0 else b["w2b1"]
            p.op("pool", lambda e, e_i=e_i, fq=fq, w13=w13: e.dma_start(
                out=w13.rearrange("p (ch c) -> p ch c", ch=ND),
                in_=we13_dram[e_i][:, fq * 1024:(fq + 1) * 1024].rearrange(
                    "(ch p) c -> p ch c", p=128)),
                writes=[w13k], dma=True)
            p.op("pool", lambda e, e_i=e_i, fq=fq, w2=w2: e.dma_start(
                out=w2.rearrange("p (ch c) -> p ch c", ch=4),
                in_=we2_dram[e_i][fq * 512:(fq + 1) * 512, :].rearrange(
                    "(ch p) c -> p ch c", p=128)),
                writes=[w2k], dma=True)
            for tc in range(4):
                gpar = (e_i * 16 + fq * 4 + tc) % 2
                hw = b["hwT0"] if gpar == 0 else b["hwT1"]
                hwk = "hwT%d" % gpar
                for j in range(4):
                    up = j % 2
                    uk1, uk3 = "psA%d" % up, "psB%d" % up
                    for ch in range(ND):
                        p.op("pe", lambda e, ch=ch, j=j, tc=tc, up=up,
                             w13=w13: e.matmul(
                                 b["psA"][:, up * 512:up * 512 + 512],
                                 w13[:, ch * 1024 + j * 128:
                                     ch * 1024 + j * 128 + 128],
                                 b["ownT"][:, ch * TO + tc * 512:
                                           ch * TO + tc * 512 + 512],
                                 start=(ch == 0), stop=(ch == ND - 1)),
                             reads=[w13k, "ownT"], writes=[uk1])
                    for ch in range(ND):
                        p.op("pe", lambda e, ch=ch, j=j, tc=tc, up=up,
                             w13=w13: e.matmul(
                                 b["psB"][:, up * 512:up * 512 + 512],
                                 w13[:, ch * 1024 + 512 + j * 128:
                                     ch * 1024 + 512 + j * 128 + 128],
                                 b["ownT"][:, ch * TO + tc * 512:
                                           ch * TO + tc * 512 + 512],
                                 start=(ch == 0), stop=(ch == ND - 1)),
                             reads=[w13k, "ownT"], writes=[uk3])
                    p.op("act", lambda e, up=up, j=j: e.activation(
                        b["s1b"][:, up * 512:up * 512 + 512],
                        b["psA"][:, up * 512:up * 512 + 512], AF.Silu),
                        reads=[uk1], writes=["s1b%d" % up])
                    p.op("dve", lambda e, up=up, j=j, hw=hw: e.tensor_tensor(
                        hw[:, j * 512:(j + 1) * 512],
                        b["s1b"][:, up * 512:up * 512 + 512],
                        b["psB"][:, up * 512:up * 512 + 512], ALU.mult),
                        reads=[uk3, "s1b%d" % up], writes=[hwk])
                    pop_d()

                def mk_down(e_i, tc, hw, hwk, w2, w2k):
                    def gen(il):
                        def f():
                            i = tc * 4 + il
                            for j in range(4):
                                for half in range(2):
                                    p.op("pe", lambda e, j=j, half=half,
                                         il=il, hw=hw, w2=w2: e.matmul(
                                             b["psC"][:, half * 512:
                                                      half * 512 + 512],
                                             hw[:, j * 512 + il * 128:
                                                j * 512 + il * 128 + 128],
                                             w2[:, j * 1024 + half * 512:
                                                j * 1024 + half * 512 + 512],
                                             start=(j == 0), stop=(j == 3)),
                                         reads=[hwk, w2k],
                                         writes=["psC%d" % half])
                            wcol = b["wcols"][:, i * NE + e_i:
                                              i * NE + e_i + 1]
                            p.op("dve", lambda e, i=i, wcol=wcol:
                                 e.scalar_tensor_tensor(
                                     yacc_slice(b, i), b["psC"][:], wcol,
                                     yacc_slice(b, i), ALU.mult, ALU.add),
                                 reads=["psC0", "psC1", "wcols", "yacc"],
                                 writes=["yacc"])
                        return f
                    return [gen(il) for il in range(4)]
                dq.extend(mk_down(e_i, tc, hw, hwk, w2, w2k))
    while dq:
        pop_d()


def emit_assembly_l1(p, b, x1own_src, out_dram):
    for i in range(NTO):
        par = i % 2
        xt = b["xt0"] if par == 0 else b["xt1"]
        xk = "xt%d" % par
        if x1own_src[0] == "dram":
            p.op("sync", lambda e, i=i, xt=xt: e.dma_start(
                out=xt[:], in_=x1own_src[1][i * 128:(i + 1) * 128, :]),
                writes=[xk], dma=True)
        else:
            idxs, x1d = x1own_src[1], x1own_src[2]
            p.op("pool", lambda e, i=i, xt=xt: e.indirect_dma_start(
                out=xt[:], out_offset=None, in_=x1d,
                in_offset=IndirectOffsetOnAxis(idxs[:, i:i + 1], 0)),
                reads=["idxX", "x1scr"], writes=[xk], dma=True)
        p.op("dve", lambda e, i=i, xt=xt: e.scalar_tensor_tensor(
            xt[:], yacc_slice(b, i), 1.0, xt[:], ALU.mult, ALU.add),
            reads=["yacc"], writes=[xk])
        p.op("sync", lambda e, i=i, xt=xt: e.dma_start(
            out=out_dram[i * 128:(i + 1) * 128, :], in_=xt[:]),
            reads=[xk], writes=["x2out"], dma=True)


def build_l0(x1_as_output):
    nc = bass.Bass("TRN2", target_bir_lowering=False)
    t = {}
    t["x0"] = nc.dram_tensor("x0", [T, D], F32, kind="ExternalInput").ap()
    t["wqk0"] = nc.dram_tensor("wqk0", [D, H * 128], F32,
                               kind="ExternalInput").ap()
    t["wv0"] = nc.dram_tensor("wv0", [D, H * 64], F32,
                              kind="ExternalInput").ap()
    t["bqk0"] = nc.dram_tensor("bqk0", [128, 16], F32,
                               kind="ExternalInput").ap()
    t["bv0"] = nc.dram_tensor("bv0", [16, 64], F32,
                              kind="ExternalInput").ap()
    t["wo0"] = nc.dram_tensor("wo0", [D, D], F32, kind="ExternalInput").ap()
    t["bproj"] = nc.dram_tensor("bproj", [1, D], F32,
                                kind="ExternalInput").ap()
    t["wg0"] = nc.dram_tensor("wg0", [D, 8], F32, kind="ExternalInput").ap()
    t["a0"] = nc.dram_tensor("a0", [D, 128], F32, kind="ExternalInput").ap()
    t["b0"] = nc.dram_tensor("b0", [128, D], F32, kind="ExternalInput").ap()
    x1_out = nc.dram_tensor("x1", [T, D], F32, kind="ExternalOutput").ap() \
        if x1_as_output else None

    with ExitStack() as st:
        b = alloc_bufs(nc, st, "l0")
        p = Prog(nc, st)
        emit_consts(p, b)
        p.op("sync", lambda e: e.dma_start(out=b["bqk"][:], in_=t["bqk0"]),
             writes=["bqk"], dma=True)
        emit_lnT(p, b, t["x0"], NT, b["xnT"], "xnT", T)
        p.barrier(["xnb", "sq1k"])
        emit_vb_ones(p, b)
        _attn_layer(p, b, 0, t, T, T, None)
        p.barrier(ATTN_KEYS)
        load_asm0_weights(p, b, t)
        emit_assembly_l0(p, b, t["x0"], ln1=False, x1_dram=None,
                         x1_out=x1_out)
        p.finish()
    return nc


def build_l1():
    nc = bass.Bass("TRN2", target_bir_lowering=False)
    t = {}
    t["x1f"] = nc.dram_tensor("x1f", [T, D], F32, kind="ExternalInput").ap()
    t["x1o"] = nc.dram_tensor("x1o", [TO, D], F32, kind="ExternalInput").ap()
    t["wqk1"] = nc.dram_tensor("wqk1", [D, H * 128], F32,
                               kind="ExternalInput").ap()
    t["wv1"] = nc.dram_tensor("wv1", [D, H * 64], F32,
                              kind="ExternalInput").ap()
    t["wo1"] = nc.dram_tensor("wo1", [D, D], F32, kind="ExternalInput").ap()
    t["wg1"] = nc.dram_tensor("wg1", [D, 8], F32, kind="ExternalInput").ap()
    we13 = [nc.dram_tensor("we13_%d" % i, [D, 2 * FF], F32,
                           kind="ExternalInput").ap() for i in range(NE)]
    we2 = [nc.dram_tensor("we2_%d" % i, [FF, D], F32,
                          kind="ExternalInput").ap() for i in range(NE)]
    out = nc.dram_tensor("x2", [TO, D], F32, kind="ExternalOutput").ap()

    with ExitStack() as st:
        b = alloc_bufs(nc, st, "l1")
        p = Prog(nc, st)
        emit_consts(p, b)
        emit_lnT(p, b, t["x1f"], NT, b["xnT"], "xnT", T)
        emit_lnT(p, b, t["x1o"], NTO, b["ownT"], "ownT", TO)
        p.barrier(["xnb", "sq1k"])
        emit_vb_ones(p, b)
        _attn_layer(p, b, 1, t, TO, TO, "ownT")
        p.barrier(ATTN_KEYS)
        p.op("pool", lambda e: e.memset(b["xnT"][:], 0.0),
             reads=["xnT"], writes=["xnT", "yacc"])
        emit_gating1_proj1(p, b, t)
        p.barrier(["wob"])
        emit_ffn1(p, b, we13, we2)
        emit_assembly_l1(p, b, ("dram", t["x1o"]), out)
        p.finish()
    return nc




ASM0_KEYS = ["wob", "a0b", "b0b", "hw0", "hw0t", "bpjb", "zrow", "wg0b",
             "xnb", "sq1k", "cw", "gbuf", "eq1", "eq2"]


def build_fused():
    nc = bass.Bass("TRN2", target_bir_lowering=False)
    t = {}
    t["x0"] = nc.dram_tensor("x0", [T, D], F32, kind="ExternalInput").ap()
    t["wqk0"] = nc.dram_tensor("wqk0", [D, H * 128], F32,
                               kind="ExternalInput").ap()
    t["wv0"] = nc.dram_tensor("wv0", [D, H * 64], F32,
                              kind="ExternalInput").ap()
    t["bqk0"] = nc.dram_tensor("bqk0", [128, 16], F32,
                               kind="ExternalInput").ap()
    t["bv0"] = nc.dram_tensor("bv0", [16, 64], F32,
                              kind="ExternalInput").ap()
    t["wo0"] = nc.dram_tensor("wo0", [D, D], F32, kind="ExternalInput").ap()
    t["bproj"] = nc.dram_tensor("bproj", [1, D], F32,
                                kind="ExternalInput").ap()
    t["wg0"] = nc.dram_tensor("wg0", [D, 8], F32, kind="ExternalInput").ap()
    t["a0"] = nc.dram_tensor("a0", [D, 128], F32, kind="ExternalInput").ap()
    t["b0"] = nc.dram_tensor("b0", [128, D], F32, kind="ExternalInput").ap()
    t["wqk1"] = nc.dram_tensor("wqk1", [D, H * 128], F32,
                               kind="ExternalInput").ap()
    t["wv1"] = nc.dram_tensor("wv1", [D, H * 64], F32,
                              kind="ExternalInput").ap()
    t["wo1"] = nc.dram_tensor("wo1", [D, D], F32, kind="ExternalInput").ap()
    t["wg1"] = nc.dram_tensor("wg1", [D, 8], F32, kind="ExternalInput").ap()
    we13 = [nc.dram_tensor("we13_%d" % i, [D, 2 * FF], F32,
                           kind="ExternalInput").ap() for i in range(NE)]
    we2 = [nc.dram_tensor("we2_%d" % i, [FF, D], F32,
                          kind="ExternalInput").ap() for i in range(NE)]
    idxT = nc.dram_tensor("idxT", [128, 8], I32, kind="ExternalInput").ap()
    idxX = nc.dram_tensor("idxX", [128, 16], I32, kind="ExternalInput").ap()
    x1scr = nc.dram_tensor("x1scr", [T, D], F32, kind="Internal").ap()
    xn1scr = nc.dram_tensor("xn1scr", [D, T], BF16, kind="Internal").ap()
    out = nc.dram_tensor("x2", [TO, D], F32, kind="ExternalOutput").ap()

    with ExitStack() as st:
        b = alloc_bufs(nc, st, "fused")
        p = Prog(nc, st)
        emit_consts(p, b)
        p.op("sync", lambda e: e.dma_start(out=b["bqk"][:], in_=t["bqk0"]),
             writes=["bqk"], dma=True)
        p.op("sync", lambda e: e.dma_start(out=b["idxT"][:], in_=idxT),
             writes=["idxT"], dma=True)
        p.op("sync", lambda e: e.dma_start(out=b["idxX"][:], in_=idxX),
             writes=["idxX"], dma=True)
        emit_lnT(p, b, t["x0"], NT, b["xnT"], "xnT", T)
        p.barrier(["xnb", "sq1k"])
        emit_vb_ones(p, b)
        _attn_layer(p, b, 0, t, T, T, None)
        p.barrier(ATTN_KEYS)
        load_asm0_weights(p, b, t)
        emit_assembly_l0(p, b, t["x0"], ln1=True, x1_dram=x1scr, x1_out=None)
        for ch in range(ND):
            p.op("sync", lambda e, ch=ch: e.dma_start(
                out=xn1scr[ch * 128:(ch + 1) * 128, :],
                in_=b["xnT"][:, ch * T:(ch + 1) * T]),
                reads=["xnT"], writes=["xn1scr"], dma=True)
        p.barrier(ASM0_KEYS + ["oT"])
        xn1flat = bass.AP(xn1scr.tensor, 0, [[1, D * T], [1, 1]])
        for ch in range(ND):
            p.op("pool", lambda e, ch=ch: e.indirect_dma_start(
                out=b["ownT"][:, ch * TO:(ch + 1) * TO], out_offset=None,
                in_=xn1flat,
                in_offset=IndirectOffsetOnAxis(b["idxT"][:, ch:ch + 1], 0)),
                reads=["xn1scr", "idxT"], writes=["ownT"], dma=True)
        emit_vb_ones(p, b)
        _attn_layer(p, b, 1, t, TO, TO, "ownT")
        p.barrier(ATTN_KEYS)
        p.op("pool", lambda e: e.memset(b["xnT"][:], 0.0),
             reads=["xnT"], writes=["xnT", "yacc"])
        emit_gating1_proj1(p, b, t)
        p.barrier(["wob", "cw", "gbuf", "eq1", "eq2"])
        emit_ffn1(p, b, we13, we2)
        emit_assembly_l1(p, b, ("gather", b["idxX"], x1scr), out)
        p.finish()
    return nc



def _prep_shared(w):
    g = {}
    wa = np.asarray(w["L0_attn_w_attn"], np.float32)
    qk = np.empty((D, H * 128), np.float32)
    for h in range(H):
        qk[:, h * 128:h * 128 + 64] = wa[:, D + h * 64:D + (h + 1) * 64]
        qk[:, h * 128 + 64:h * 128 + 128] = wa[:, h * 64:(h + 1) * 64]
    g["wqk0"] = qk
    g["wv0"] = np.ascontiguousarray(wa[:, 2 * D:3 * D])
    ba = np.asarray(w["L0_attn_b_attn"], np.float32)
    bqk = np.empty((128, 16), np.float32)
    for h in range(H):
        bqk[0:64, h] = ba[D + h * 64:D + (h + 1) * 64]
        bqk[64:128, h] = ba[h * 64:(h + 1) * 64]
    g["bqk0"] = bqk
    g["bv0"] = np.ascontiguousarray(ba[2 * D:].reshape(16, 64))
    g["wo0"] = np.ascontiguousarray(np.asarray(w["L0_attn_w_proj"],
                                               np.float32))
    g["bproj"] = np.ascontiguousarray(
        np.asarray(w["L0_attn_b_proj"], np.float32)[None, :])
    g["wg0"] = np.ascontiguousarray(np.asarray(w["L0_ffn_Wg"], np.float32))
    g["a0"] = np.ascontiguousarray(
        np.asarray(w["L0_ffn_A"], np.float32).transpose(1, 0, 2).reshape(
            D, 128))
    g["b0"] = np.ascontiguousarray(
        np.asarray(w["L0_ffn_B"], np.float32).reshape(128, D))
    wq = np.asarray(w["L1_attn_W_q"], np.float32)
    wk2 = np.asarray(w["L1_attn_W_k"], np.float32)
    wv2 = np.asarray(w["L1_attn_W_v"], np.float32)
    qk1 = np.empty((D, H * 128), np.float32)
    for h in range(H):
        qk1[:, h * 128:h * 128 + 64] = wk2[h]
        qk1[:, h * 128 + 64:h * 128 + 128] = wq[h]
    g["wqk1"] = qk1
    g["wv1"] = np.ascontiguousarray(wv2.transpose(1, 0, 2).reshape(D, H * 64))
    g["wo1"] = np.ascontiguousarray(np.asarray(w["L1_attn_W_o"], np.float32))
    g["wg1"] = np.ascontiguousarray(np.asarray(w["L1_ffn_Wg"], np.float32))
    s1 = np.asarray(w["L1_ffn_Ws1"], np.float32)
    s3 = np.asarray(w["L1_ffn_Ws3"], np.float32)
    s2 = np.asarray(w["L1_ffn_Ws2"], np.float32)
    e1 = np.asarray(w["L1_ffn_We1"], np.float32)
    e3 = np.asarray(w["L1_ffn_We3"], np.float32)
    e2 = np.asarray(w["L1_ffn_We2"], np.float32)

    def inter(a, bm):
        out = np.empty((D, 2 * FF), np.float32)
        for fq in range(4):
            out[:, fq * 1024:fq * 1024 + 512] = a[:, fq * 512:(fq + 1) * 512]
            out[:, fq * 1024 + 512:(fq + 1) * 1024] = \
                bm[:, fq * 512:(fq + 1) * 512]
        return out
    for i in range(NE):
        if i == 0:
            g["we13_%d" % i] = inter(s1, s3)
            g["we2_%d" % i] = np.ascontiguousarray(s2)
        else:
            g["we13_%d" % i] = inter(e1[i - 1], e3[i - 1])
            g["we2_%d" % i] = np.ascontiguousarray(e2[i - 1])
    return g


FUSED = True


def kernel(**inputs):
    g = _prep_shared(inputs)
    x0 = np.ascontiguousarray(np.asarray(inputs["L0_attn_x"], np.float32))
    if FUSED:
        return kernel_fused(inputs, g, x0)

    nc0 = build_l0(x1_as_output=True)
    l0_keys = ["wqk0", "wv0", "bqk0", "bv0", "wo0", "bproj", "wg0", "a0",
               "b0"]
    in_maps0 = []
    for c in range(8):
        m = {"x0": x0[c // 2]}
        for k in l0_keys:
            m[k] = g[k]
        in_maps0.append(m)
    r0 = run_bass_kernel_spmd(nc0, in_maps0, core_ids=list(range(8)))
    x1 = [np.asarray(r0.results[2 * bb]["x1"]) for bb in range(B)]

    nc1 = build_l1()
    l1_keys = (["wqk1", "wv1", "wo1", "wg1"] +
               ["we13_%d" % i for i in range(NE)] +
               ["we2_%d" % i for i in range(NE)])
    in_maps1 = []
    for c in range(8):
        bb, hf = c // 2, c % 2
        m = {"x1f": x1[bb],
             "x1o": np.ascontiguousarray(x1[bb][hf * TO:(hf + 1) * TO])}
        for k in l1_keys:
            m[k] = g[k]
        in_maps1.append(m)
    r1 = run_bass_kernel_spmd(nc1, in_maps1, core_ids=list(range(8)))

    out = np.empty((B, T, D), np.float32)
    for c in range(8):
        bb, hf = c // 2, c % 2
        out[bb, hf * TO:(hf + 1) * TO] = np.asarray(r1.results[c]["x2"])
    return out


def kernel_fused(inputs, g, x0):
    nc = build_fused()
    keys = (["wqk0", "wv0", "bqk0", "bv0", "wo0", "bproj", "wg0", "a0", "b0",
             "wqk1", "wv1", "wo1", "wg1"] +
            ["we13_%d" % i for i in range(NE)] +
            ["we2_%d" % i for i in range(NE)])
    pidx = np.arange(128)
    in_maps = []
    for c in range(8):
        bb, hf = c // 2, c % 2
        own0 = hf * TO
        idxT_np = np.empty((128, 8), np.int32)
        for ch in range(ND):
            idxT_np[:, ch] = (ch * 128 + pidx) * T + own0
        idxX_np = np.empty((128, 16), np.int32)
        for i in range(NTO):
            idxX_np[:, i] = own0 + i * 128 + pidx
        m = {"x0": x0[bb], "idxT": idxT_np, "idxX": idxX_np}
        for k in keys:
            m[k] = g[k]
        in_maps.append(m)
    r = run_bass_kernel_spmd(nc, in_maps, core_ids=list(range(8)))
    out = np.empty((B, T, D), np.float32)
    for c in range(8):
        bb, hf = c // 2, c % 2
        out[bb, hf * TO:(hf + 1) * TO] = np.asarray(r.results[c]["x2"])
    return out


# version 15
# speedup vs baseline: 1.3887x; 1.0263x over previous
import numpy as np
import ml_dtypes
from contextlib import ExitStack

import concourse.bass as bass
import concourse.mybir as mybir
from concourse.bass import IndirectOffsetOnAxis
from concourse.bass_utils import run_bass_kernel_spmd

F32 = mybir.dt.float32
BF16 = mybir.dt.bfloat16
I32 = mybir.dt.int32
AF = mybir.ActivationFunctionType
ALU = mybir.AluOpType
AX = mybir.AxisListType.X

B, T, D, H, DH = 4, 4096, 1024, 16, 64
TO = T // 2
NT, NTO = T // 128, TO // 128
ND = D // 128
CEXP = 20.0
EPS = 1e-5
FF = 2048
NE = 9
SLOPES = [2.0 ** (-(8.0 / H) * (i + 1)) for i in range(H)]
GW = 4480
GOFF = 384
SKIP_GAP = 25.0
QW = 512
UCOLS = 33024


class Prog:
    def __init__(self, nc, st):
        self.nc = nc
        self.st = st
        self.e = {"pe": nc.tensor, "act": nc.scalar, "dve": nc.vector,
                  "pool": nc.gpsimd, "sync": nc.sync}
        self.sem = {}
        for sname in ("pe", "act", "dve", "pool"):
            self.sem[sname] = st.enter_context(nc.semaphore("sem_" + sname))
        self.cnt = {sname: 0 for sname in self.sem}
        self.waited = {sname: {} for sname in
                       ("pe", "act", "dve", "pool", "sync")}
        self.buf = {}

    def _dma_sem(self, key):
        name = "d_" + key
        if name not in self.sem:
            self.sem[name] = self.st.enter_context(
                self.nc.semaphore("sem_" + name))
            self.cnt[name] = 0
        return name

    def wait(self, stream, tok):
        if tok is None:
            return
        chan, v = tok
        if stream == "pe" and chan == "pe":
            return
        if v <= self.waited[stream].get(chan, 0):
            return
        self.e[stream].wait_ge(self.sem[chan], v)
        self.waited[stream][chan] = v

    def op(self, stream, fn, reads=(), writes=(), dma=False, deps=()):
        for t in deps:
            self.wait(stream, t)
        for k in reads:
            sb_ = self.buf.get(k)
            if sb_ and sb_["w"]:
                self.wait(stream, sb_["w"])
        for k in writes:
            sb_ = self.buf.get(k)
            if sb_:
                if sb_["w"]:
                    self.wait(stream, sb_["w"])
                for t in sb_["r"]:
                    self.wait(stream, t)
        ins = fn(self.e[stream])
        if dma:
            assert writes, "DMA ops must declare a write key"
            chan = self._dma_sem(writes[0])
            inc = 16
        else:
            chan = stream
            inc = 1
        ins.then_inc(self.sem[chan], inc)
        self.cnt[chan] += inc
        tok = (chan, self.cnt[chan])
        for k in reads:
            self.buf.setdefault(k, {"w": None, "r": []})["r"].append(tok)
        for k in writes:
            self.buf[k] = {"w": tok, "r": []}
        return tok

    def barrier(self, keys, streams=("pe", "act", "dve", "pool", "sync")):
        best = {}
        for k in keys:
            sb_ = self.buf.get(k)
            if not sb_:
                continue
            for t in ([sb_["w"]] if sb_["w"] else []) + sb_["r"]:
                c, v = t
                if best.get(c, 0) < v:
                    best[c] = v
        for stream in streams:
            for c, v in best.items():
                self.wait(stream, (c, v))

    def finish(self):
        for chan in list(self.cnt):
            if self.cnt[chan]:
                self.wait("sync", (chan, self.cnt[chan]))


PS_KEYS = ["psA0", "psA1", "psB0", "psB1", "psC0", "psC1", "psD", "psT"]
ATTN_KEYS = ["qb", "vb", "gt", "sqA", "sqB", "pt00", "pt01", "pt10",
             "pt11", "wqkb0", "wqkb1", "wvb", "bvb", "rrowb", "rb0", "rb1",
             "ob0", "ob1", "kb", "qst0", "qst1", "mrow",
             "pcol"] + PS_KEYS
ASM_KEYS = ["wob", "a0b", "b0b", "hw0", "hw0t", "bpjb", "zrow", "wg0b",
            "xnb", "sq1k", "cw", "gbuf", "eq1", "eq2", "w13_0", "w13_1",
            "w2_0", "w2_1", "s1b", "hwb", "hwtb", "wg1b", "wcols"]


def uv(b, part0, nparts, col0, ncols, dtype=BF16):
    u = b["U"][:]
    ap = bass.AP(u.tensor, part0 * UCOLS + col0, [[UCOLS, nparts], [1, ncols]])
    if dtype == F32:
        ap = ap.bitcast(F32)
    return ap


def alloc_bufs(nc, st, phase):
    b = {}

    def sb(name, shape, dt):
        b[name] = st.enter_context(nc.sbuf_tensor(name, shape, dt))
    sb("xnT", [128, ND * T], BF16)
    if phase == "l1":
        sb("ownT", [128, ND * TO], BF16)
        sb("oT", [128, ND * TO], BF16)
    else:
        sb("oT", [128, ND * T], BF16)
    if phase == "fused":
        ot = b["oT"][:]
        b["ownT"] = bass.AP(ot.tensor, ot.offset + ND * TO,
                            [[ot.ap[0][0], 128], [1, ND * TO]])
        sb("idxTs", [128, 8], I32)
        sb("idxXs", [128, 16], I32)
        b["idxT"], b["idxX"] = b["idxTs"], b["idxXs"]
    sb("U", [128, UCOLS], BF16)
    sb("xt0", [128, 1024], F32)
    sb("xt1", [128, 1024], F32)
    sb("stats", [128, 24], F32)
    sb("mx", [128, 4], F32)
    sb("mxb", [128, 1], BF16)
    sb("mrow", [1, 24], F32)
    sb("mrowb", [1, 1], BF16)
    sb("cm", [128, 1], F32)
    sb("ident", [128, 128], BF16)
    sb("ones128", [128, 1], BF16)
    sb("onesr", [65, 128], BF16)
    sb("bvr", [1, 256], BF16)
    sb("bqk", [128, 16], F32)
    sb("konst", [128, 2], F32)
    nc.const_aps.register(F32, EPS, b["konst"][:, 0:1])
    nc.const_aps.register(F32, 1e-30, b["konst"][:, 1:2])
    b["kb"] = uv(b, 0, 64, 0, 4096)
    b["qstage"] = uv(b, 64, 64, 0, 4096)
    b["qb"] = uv(b, 0, 64, 4096, 4096)
    b["rrow"] = uv(b, 64, 1, 4096, 2048, F32)
    b["rrowb"] = uv(b, 64, 1, 6144, 1024)
    b["vb"] = uv(b, 0, 128, 8192, 8320)
    b["gt"] = uv(b, 0, 128, 16512, GW)
    b["sq"] = uv(b, 0, 64, 21504, 1024)
    b["pt0"] = uv(b, 0, 128, 22528, 1024)
    b["pt1"] = uv(b, 0, 128, 23552, 1024)
    b["rbA"] = uv(b, 0, 64, 29952, 1024, F32)
    b["rbB"] = uv(b, 0, 64, 24576, 1024, F32)
    b["wqkb0"] = uv(b, 0, 128, 25600, 1024)
    b["wqkb1"] = uv(b, 0, 128, 26624, 1024)
    b["wvb"] = uv(b, 0, 128, 27648, 2048)
    b["bvb"] = uv(b, 0, 128, 29696, 256)
    b["rb"] = uv(b, 0, 64, 29952, 1024)
    b["ob0"] = uv(b, 0, 64, 30976, 1024)
    b["ob1"] = uv(b, 0, 64, 32000, 1024)
    if phase in ("l0", "fused"):
        b["wob"] = uv(b, 0, 128, 0, 8192)
        b["a0b"] = uv(b, 0, 128, 8192, 1024)
        b["b0b"] = uv(b, 0, 128, 9216, 1024)
        b["hw0"] = uv(b, 0, 128, 10240, 128)
        b["hw0t"] = uv(b, 0, 128, 10368, 128)
        b["bpjb"] = uv(b, 0, 1, 10496, 1024)
        b["zrow"] = uv(b, 0, 1, 11520, 512)
        b["wg0b"] = uv(b, 0, 128, 12032, 64)
        b["xnb"] = uv(b, 0, 128, 12096, 1024)
        b["sq1k"] = uv(b, 0, 128, 13120, 1024)
        b["cw"] = uv(b, 0, 128, 14144, 32, F32)
        b["gbuf"] = uv(b, 0, 128, 14176, 32, F32)
        b["eq1"] = uv(b, 0, 128, 14208, 32, F32)
        b["eq2"] = uv(b, 0, 128, 14240, 32, F32)
    if phase in ("l1", "fused"):
        b["wob"] = uv(b, 0, 128, 0, 8192)
        b["w13b0"] = uv(b, 0, 128, 0, 8192)
        b["w13b1"] = uv(b, 0, 128, 8192, 8192)
        b["w2b0"] = uv(b, 0, 128, 16384, 4096)
        b["w2b1"] = uv(b, 0, 128, 20480, 4096)
        b["s1b"] = uv(b, 0, 128, 24576, 1024)
        b["wg1b"] = uv(b, 0, 128, 25600, 64)
        b["wcols"] = uv(b, 0, 128, 25664, 288, F32)
        b["hwT0"] = uv(b, 0, 128, 26240, 2048)
        b["hwT1"] = uv(b, 0, 128, 28288, 2048)
        if phase == "l1":
            b["cw"] = uv(b, 0, 128, 30336, 32, F32)
            b["gbuf"] = uv(b, 0, 128, 30368, 32, F32)
            b["eq1"] = uv(b, 0, 128, 30400, 32, F32)
            b["eq2"] = uv(b, 0, 128, 30432, 32, F32)
            b["xnb"] = uv(b, 0, 128, 30464, 1024)
            b["sq1k"] = uv(b, 0, 128, 31488, 1024)
    b["psA"] = st.enter_context(nc.psum_tensor("psA", [128, 1024], F32))
    b["psB"] = st.enter_context(nc.psum_tensor("psB", [128, 1024], F32))
    b["psC"] = st.enter_context(nc.psum_tensor("psC", [128, 1024], F32))
    b["psD"] = st.enter_context(nc.psum_tensor("psD", [128, 512], F32))
    b["psT"] = st.enter_context(nc.psum_tensor("psT", [128, 1024], BF16))
    return b


def emit_consts(p, b):
    import struct
    r0 = p.nc.gpsimd.alloc_register("fill0")
    p.nc.gpsimd.reg_mov(r0, 0)
    rn = p.nc.gpsimd.alloc_register("filln")
    p.nc.gpsimd.reg_mov(rn, struct.unpack("<i", struct.pack("<f", -1e30))[0])
    b["fill0"], b["filln"] = r0, rn
    p.op("pool", lambda e: e.memset(b["ident"][:], 1.0), writes=["ident"])
    p.op("pool", lambda e: e.affine_select(
        b["ident"][:], b["ident"][:], [[-1, 128]], ALU.is_equal, b["fill0"],
        channel_multiplier=1), reads=["ident"], writes=["ident"])
    p.op("pool", lambda e: e.memset(b["ones128"][:], 1.0), writes=["ones128"])
    p.op("pool", lambda e: e.memset(b["onesr"][0:65, :], 1.0), writes=["onesr"])
    p.op("pool", lambda e: e.memset(b["konst"][:, 0:1], EPS), writes=["konst"])
    p.op("pool", lambda e: e.memset(b["konst"][:, 1:2], 1e-30),
         reads=["konst"], writes=["konst"])
    p.op("pool", lambda e: e.iota(
        b["stats"][:, 16:17], [[0, 1]], base=0, channel_multiplier=1,
        allow_small_or_imprecise_dtypes=True), writes=["pcol"])


def emit_vb_ones(p, b):
    p.op("pool", lambda e: e.memset(b["vb"][:], 0.0), writes=["vb"])
    vb = b["vb"].rearrange("p (k h c) -> p k h c", k=NT, h=4)
    p.op("pool", lambda e: e.memset(vb[:, :, :, 64:65], 1.0),
         reads=["vb"], writes=["vb"])


def _ln_tile(p, b, xt, xk, i, dst, dst_key, dstT):
    st = b["stats"]
    p.op("dve", lambda e: e.reduce_sum(st[:, 0:1], xt[:], axis=AX),
         reads=[xk], writes=["st0"])
    p.op("dve", lambda e: e.scalar_tensor_tensor(
        b["sq1k"], xt[:], 1.0, xt[:], ALU.mult, ALU.mult,
        accum_out=st[:, 1:2]), reads=[xk], writes=["st1", "sq1k"])
    p.op("dve", lambda e: e.tensor_scalar_mul(st[:, 2:3], st[:, 0:1], 1.0 / D),
         reads=["st0"], writes=["st2"])
    p.op("dve", lambda e: e.scalar_tensor_tensor(
        st[:, 3:4], st[:, 2:3], -1.0, st[:, 2:3], ALU.mult, ALU.mult),
        reads=["st2"], writes=["st3"])
    p.op("dve", lambda e: e.scalar_tensor_tensor(
        st[:, 4:5], st[:, 1:2], 1.0 / D, st[:, 3:4], ALU.mult, ALU.add),
        reads=["st1", "st3"], writes=["st4"])
    p.op("act", lambda e: e.activation(st[:, 5:6], st[:, 4:5], AF.Ln,
                                       bias=EPS),
         reads=["st4", "konst"], writes=["st5"])
    p.op("act", lambda e: e.activation(st[:, 6:7], st[:, 5:6], AF.Exp,
                                       scale=-0.5),
         reads=["st5"], writes=["st6"])
    p.op("dve", lambda e: e.tensor_scalar(
        b["xnb"], xt[:], st[:, 2:3], st[:, 6:7], ALU.subtract, ALU.mult),
        reads=[xk, "st2", "st6"], writes=["xnb"])
    for j in range(ND):
        p.op("pe", lambda e, j=j: e.transpose(
            b["psT"][:, j * 128:(j + 1) * 128],
            b["xnb"][:, j * 128:(j + 1) * 128], b["ident"][:]),
            reads=["xnb", "ident"], writes=["psT"])
    dv = dst[:]
    dst_ap = bass.AP(dv.tensor, dv.offset + i * 128,
                     [[dv.ap[0][0], 128], [dstT, ND], [1, 128]])
    p.op("dve", lambda e: e.tensor_copy(
        dst_ap, b["psT"][:].rearrange("p (j c) -> p j c", j=ND)),
        reads=["psT"], writes=[dst_key])


def emit_lnT(p, b, x_dram, ntiles, dst, dst_key, dstT):
    for i in range(ntiles):
        par = i % 2
        xt = b["xt0"] if par == 0 else b["xt1"]
        xk = "xt%d" % par
        p.op("sync", lambda e, i=i, xt=xt: e.dma_start(
            out=xt[:], in_=x_dram[i * 128:(i + 1) * 128, :]),
            writes=[xk], dma=True)
        _ln_tile(p, b, xt, xk, i, dst, dst_key, dstT)


def emit_qk_head(p, b, h, wqk_dram, has_bias, Tq, q_src_key=None):
    wk = "wqkb%d" % (h % 2)
    wb = b[wk]
    wslc = wqk_dram[:, h * 128:(h + 1) * 128]
    p.op("pool", lambda e: e.dma_start(
        out=wb.rearrange("p (ch c) -> p ch c", ch=ND),
        in_=wslc.rearrange("(ch p) c -> p ch c", p=128)),
        writes=[wk], dma=True)
    bk = b["bqk"][0:64, h:h + 1] if has_bias else 0.0
    bq = b["bqk"][64:128, h:h + 1] if has_bias else 0.0
    rds = ["bqk"] if has_bias else []
    nq = Tq // 512

    def q_evac_shift(tc, ps, bank):
        qk = "qst%d" % (tc % 2)
        p.op("dve", lambda e, tc=tc, ps=ps: e.tensor_scalar(
            b["qstage"][0:64, tc * 512:(tc + 1) * 512], ps[64:128, 0:512],
            bq, 0.125, ALU.add, ALU.mult),
            reads=[bank] + rds, writes=[qk])
        p.op("sync", lambda e, tc=tc: e.dma_start(
            out=b["qb"][0:64, tc * 512:(tc + 1) * 512],
            in_=b["qstage"][0:64, tc * 512:(tc + 1) * 512]),
            reads=[qk], writes=["qb"], dma=True)

    for tc in range(T // 512):
        bank = "psA0" if tc % 2 == 0 else "psB0"
        ps = b["psA"] if tc % 2 == 0 else b["psB"]
        for ch in range(ND):
            p.op("pe", lambda e, ch=ch, tc=tc, ps=ps: e.matmul(
                ps[:, 0:512], wb[:, ch * 128:(ch + 1) * 128],
                b["xnT"][:, ch * T + tc * 512: ch * T + tc * 512 + 512],
                start=(ch == 0), stop=(ch == ND - 1)),
                reads=[wk, "xnT"], writes=[bank])
        p.op("dve", lambda e, tc=tc, ps=ps: e.tensor_scalar(
            b["kb"][0:64, tc * 512:(tc + 1) * 512], ps[0:64, 0:512],
            bk, None, ALU.add),
            reads=[bank] + rds, writes=["kb"])
        if q_src_key is None and tc < nq:
            q_evac_shift(tc, ps, bank)
    if q_src_key is not None:
        qsrc = b[q_src_key]
        for tc in range(nq):
            bank = "psA0" if tc % 2 == 0 else "psB0"
            ps = b["psA"] if tc % 2 == 0 else b["psB"]
            for ch in range(ND):
                p.op("pe", lambda e, ch=ch, tc=tc, ps=ps: e.matmul(
                    ps[:, 0:512], wb[:, ch * 128:(ch + 1) * 128],
                    qsrc[:, ch * TO + tc * 512: ch * TO + tc * 512 + 512],
                    start=(ch == 0), stop=(ch == ND - 1)),
                    reads=[wk, q_src_key], writes=[bank])
            q_evac_shift(tc, ps, bank)


def emit_v_group(p, b, hg, wv_dram, bv_dram):
    p.op("pool", lambda e: e.dma_start(
        out=b["wvb"].rearrange("p (ch c) -> p ch c", ch=ND),
        in_=wv_dram[:, hg * 256:(hg + 1) * 256].rearrange(
            "(ch p) c -> p ch c", p=128)),
        writes=["wvb"], dma=True)
    have_bias = bv_dram is not None
    if have_bias:
        p.op("pool", lambda e: e.dma_start(
            out=b["bvr"][0:1, 0:256],
            in_=bv_dram[hg * 4:(hg + 1) * 4, :].rearrange("(o h) c -> o (h c)", o=1)),
            writes=["bvr"], dma=True)
        p.op("pe", lambda e: e.matmul(
            b["psD"][:, 0:256], b["onesr"][0:1, :], b["bvr"][0:1, 0:256],
            start=True, stop=True), reads=["bvr", "onesr"], writes=["psD"])
        p.op("dve", lambda e: e.tensor_copy(b["bvb"][:, 0:256],
                                            b["psD"][:, 0:256]),
             reads=["psD"], writes=["bvb"])
    vv = b["vb"]
    for i in range(NT):
        bank = "psA0" if i % 2 == 0 else "psB0"
        ps = b["psA"] if i % 2 == 0 else b["psB"]
        for ch in range(ND):
            p.op("pe", lambda e, ch=ch, i=i, ps=ps: e.matmul(
                ps[:, 0:256],
                b["xnT"][:, ch * T + i * 128: ch * T + i * 128 + 128],
                b["wvb"][:, ch * 256:(ch + 1) * 256],
                start=(ch == 0), stop=(ch == ND - 1)),
                reads=["wvb", "xnT"], writes=[bank])
        out_ap = bass.AP(vv.tensor, vv.offset + i * 260,
                         [[UCOLS, 128], [65, 4], [1, 64]])
        src = ps[:, 0:256].rearrange("p (h c) -> p h c", h=4)
        if have_bias:
            p.op("dve", lambda e, src=src, out_ap=out_ap:
                 e.scalar_tensor_tensor(
                     out_ap, src, 1.0,
                     b["bvb"][:, 0:256].rearrange("p (h c) -> p h c", h=4),
                     ALU.mult, ALU.add),
                 reads=[bank, "bvb"], writes=["vb"])
        else:
            p.op("dve", lambda e, src=src, out_ap=out_ap: e.tensor_copy(
                out_ap, src), reads=[bank], writes=["vb"])


def emit_head_M(p, b, Tq):
    nq, nk = Tq // 512, T // 512
    mr = b["mrow"]

    def colsums(src, src_key, n, mcol0):
        for c in range(n):
            slot = c % 2
            sqk = "sqA" if slot == 0 else "sqB"
            sqv = b["sq"][0:64, slot * 512:(slot + 1) * 512]
            psv = b["psD"] if slot == 0 else b["psC"]
            psk = "psD" if slot == 0 else "psC0"
            p.op("act", lambda e, c=c, sqv=sqv: e.activation(
                sqv, src[0:64, c * 512:(c + 1) * 512], AF.Square),
                reads=[src_key], writes=[sqk])
            p.op("pe", lambda e, sqv=sqv, psv=psv: e.matmul(
                psv[0:1, 0:512], b["ones128"][0:64, 0:1], sqv,
                start=True, stop=True),
                reads=[sqk, "ones128"], writes=[psk])
            p.op("dve", lambda e, c=c, psv=psv, mcol0=mcol0: e.reduce_max(
                mr[0:1, mcol0 + c:mcol0 + c + 1], psv[0:1, 0:512], axis=AX),
                reads=[psk], writes=["mrow"])
    colsums(b["qb"], "qb", nq, 0)
    colsums(b["kb"], "kb", nk, 8)
    p.op("dve", lambda e: e.reduce_max(mr[0:1, 16:17], mr[0:1, 0:nq],
                                       axis=AX),
         reads=["mrow"], writes=["mrow"])
    p.op("dve", lambda e: e.reduce_max(mr[0:1, 17:18], mr[0:1, 8:8 + nk],
                                       axis=AX),
         reads=["mrow"], writes=["mrow"])
    p.op("dve", lambda e: e.tensor_tensor(mr[0:1, 18:19], mr[0:1, 16:17],
                                          mr[0:1, 17:18], ALU.mult),
         reads=["mrow"], writes=["mrow"])
    p.op("act", lambda e: e.activation(mr[0:1, 19:20], mr[0:1, 18:19],
                                       AF.Ln, bias=1e-30),
         reads=["mrow", "konst"], writes=["mrow"])
    p.op("act", lambda e: e.activation(mr[0:1, 20:21], mr[0:1, 19:20],
                                       AF.Exp, scale=0.5),
         reads=["mrow"], writes=["mrow"])
    p.op("dve", lambda e: e.tensor_scalar(mr[0:1, 21:22], mr[0:1, 20:21],
                                          -1.0, CEXP, ALU.mult, ALU.add),
         reads=["mrow"], writes=["mrow"])
    p.op("dve", lambda e: e.tensor_copy(b["mrowb"][0:1, 0:1],
                                        mr[0:1, 21:22]),
         reads=["mrow"], writes=["mrowb"])
    p.op("pe", lambda e: e.matmul(b["psD"][:, 200:201], b["onesr"][0:1, :],
                                  b["mrowb"][0:1, 0:1], start=True,
                                  stop=True),
         reads=["mrowb", "onesr"], writes=["psD"])
    p.op("dve", lambda e: e.tensor_copy(b["cm"][:], b["psD"][:, 200:201]),
         reads=["psD"], writes=["cm"])


def valid_kts(layer, q0, qw, slope):
    if layer == 1:
        return list(range(NT))
    kts = []
    for kt in range(NT):
        ktb = kt * 128
        if ktb >= q0 + qw:
            continue
        if slope * (q0 - ktb - 127) >= SKIP_GAP:
            continue
        kts.append(kt)
    return kts


def emit_attn_head(p, b, layer, h, Tq, oT_cols):
    slope = SLOPES[h] if layer == 0 else 0.0
    nch = Tq // QW
    psn = {0: "psA", 1: "psB"}
    ptn = {0: "pt0", 1: "pt1"}
    for cp in range(nch // 2):
        q0s = [(2 * cp) * QW, (2 * cp + 1) * QW]
        ktss = [valid_kts(layer, q0, QW, slope) for q0 in q0s]
        L = max(len(k) for k in ktss)

        def s_mm(s, n):
            kt = ktss[s][n]
            key = psn[s] + str(n % 2)
            p.op("pe", lambda e, kt=kt, s=s, n=n: e.matmul(
                b[psn[s]][:, (n % 2) * 512:(n % 2) * 512 + 512],
                b["kb"][0:64, kt * 128:(kt + 1) * 128],
                b["qb"][0:64, q0s[s]:q0s[s] + QW],
                start=True, stop=True),
                reads=["kb", "qb"], writes=[key])

        def do_exp(s, n):
            kt = ktss[s][n]
            ktb = kt * 128
            pskey = psn[s] + str(n % 2)
            pb = ptn[s]
            ptkey = pb + str(n % 2)
            ptv = b[pb][:, (n % 2) * 512:(n % 2) * 512 + 512]
            p.op("act", lambda e, pskey=pskey, ptv=ptv, s=s, n=n:
                 e.activation(ptv, b[psn[s]][:, (n % 2) * 512:
                                             (n % 2) * 512 + 512],
                              AF.Exp, bias=b["cm"][:]),
                 reads=[pskey, "cm"], writes=[ptkey])
            if layer == 0:
                off = GOFF + q0s[s] - ktb
                p.op("dve", lambda e, ptv=ptv, off=off: e.tensor_tensor(
                    ptv, ptv, b["gt"][:, off:off + QW], ALU.mult),
                    reads=[ptkey, "gt"], writes=[ptkey])

        def do_av(s, n):
            kt = ktss[s][n]
            ptkey = ptn[s] + str(n % 2)
            first, last = (n == 0), (n == len(ktss[s]) - 1)
            p.op("pe", lambda e, kt=kt, s=s, n=n, first=first, last=last:
                 e.matmul(
                     b["psC"][0:65, s * 512:s * 512 + 512],
                     b["vb"][:, kt * 260 + (h % 4) * 65:
                             kt * 260 + (h % 4) * 65 + 65],
                     b[ptn[s]][:, (n % 2) * 512:(n % 2) * 512 + 512],
                     start=first, stop=last),
                 reads=[ptkey, "vb"], writes=["psC%d" % s])

        for n in range(L + 1):
            for s in range(2):
                if n < len(ktss[s]):
                    s_mm(s, n)
                if 1 <= n <= len(ktss[s]):
                    do_av(s, n - 1)
                if n < len(ktss[s]):
                    do_exp(s, n)
        ch, sub = h // 2, (h % 2) * 64
        for s in range(2):
            rbv = b["rbA"] if s == 0 else b["rbB"]
            p.op("dve", lambda e, s=s: e.tensor_copy(
                b["rrowb"][0:1, 0:QW], b["psC"][64:65, s * 512:s * 512 + 512]),
                reads=["psC%d" % s], writes=["rrowb"])
            p.op("pe", lambda e: e.matmul(
                b["psD"][0:64, 0:512], b["onesr"][64:65, 0:64],
                b["rrowb"][0:1, 0:QW], start=True, stop=True),
                reads=["rrowb", "onesr"], writes=["psD"])
            p.op("dve", lambda e, rbv=rbv: e.reciprocal(
                rbv[0:64, 0:512], b["psD"][0:64, 0:512]),
                reads=["psD"], writes=["rb%d" % s])
            obk = "ob%d" % s
            p.op("dve", lambda e, s=s, obk=obk, rbv=rbv: e.tensor_tensor(
                b[obk][0:64, 0:QW], b["psC"][0:64, s * 512:s * 512 + 512],
                rbv[0:64, 0:512], ALU.mult),
                reads=["psC%d" % s, "rb%d" % s], writes=[obk])
            dst = b["oT"][sub:sub + 64,
                          ch * oT_cols + q0s[s]: ch * oT_cols + q0s[s] + QW]
            p.op("sync", lambda e, dst=dst, obk=obk: e.dma_start(
                out=dst, in_=b[obk][0:64, 0:QW]),
                reads=[obk], writes=["oT"], dma=True)


def _attn_layer(p, b, layer, t, Tq, oT_cols, q_src_key):
    wqk = t["wqk0"] if layer == 0 else t["wqk1"]
    wv = t["wv0"] if layer == 0 else t["wv1"]
    has_bias = (layer == 0)
    bv = t["bv0"] if layer == 0 else None
    for hg in range(4):
        emit_v_group(p, b, hg, wv, bv)
        for hh in range(4):
            h = hg * 4 + hh
            if layer == 0:
                p.op("pool", lambda e, h=h: e.dma_start(
                    out=b["gt"], in_=t["gtab"][h * 128:(h + 1) * 128, :]),
                    writes=["gt"], dma=True)
            emit_qk_head(p, b, h, wqk, has_bias, Tq, q_src_key=q_src_key)
            emit_head_M(p, b, Tq)
            emit_attn_head(p, b, layer, h, Tq, oT_cols)


def _top2_softmax(p, b, sigmoid):
    st = b["stats"]
    p.op("dve", lambda e: e.tensor_copy(b["gbuf"][:, 0:8], b["psD"][:, 0:8]),
         reads=["psD"], writes=["gbuf"])
    p.op("dve", lambda e: e.reduce_max(st[:, 8:9], b["gbuf"][:, 0:8],
                                       axis=AX),
         reads=["gbuf"], writes=["m1"])
    p.op("dve", lambda e: e.tensor_scalar(b["eq1"][:, 0:8], b["gbuf"][:, 0:8],
                                          st[:, 8:9], None, ALU.is_equal),
         reads=["gbuf", "m1"], writes=["eq1"])
    p.op("dve", lambda e: e.scalar_tensor_tensor(
        b["eq2"][:, 0:8], b["eq1"][:, 0:8], -1e30, b["gbuf"][:, 0:8],
        ALU.mult, ALU.add), reads=["eq1", "gbuf"], writes=["eq2"])
    p.op("dve", lambda e: e.reduce_max(st[:, 9:10], b["eq2"][:, 0:8],
                                       axis=AX),
         reads=["eq2"], writes=["m2"])
    p.op("dve", lambda e: e.tensor_scalar(b["eq2"][:, 0:8], b["gbuf"][:, 0:8],
                                          st[:, 9:10], None, ALU.is_equal),
         reads=["gbuf", "m2"], writes=["eq2"])
    if not sigmoid:
        p.op("dve", lambda e: e.tensor_sub(st[:, 10:11], st[:, 9:10],
                                           st[:, 8:9]),
             reads=["m1", "m2"], writes=["d21"])
        p.op("act", lambda e: e.activation(st[:, 11:12], st[:, 10:11],
                                           AF.Exp),
             reads=["d21"], writes=["e21"])
        p.op("dve", lambda e: e.tensor_scalar_add(st[:, 12:13],
                                                  st[:, 11:12], 1.0),
             reads=["e21"], writes=["s21"])
        p.op("dve", lambda e: e.reciprocal(st[:, 13:14], st[:, 12:13]),
             reads=["s21"], writes=["wA"])
        p.op("dve", lambda e: e.tensor_tensor(st[:, 14:15], st[:, 11:12],
                                              st[:, 13:14], ALU.mult),
             reads=["e21", "wA"], writes=["wB"])
        wa, wb_ = 13, 14
    else:
        for k, src, kk in ((10, 8, "m1"), (11, 9, "m2")):
            p.op("act", lambda e, k=k, src=src: e.activation(
                st[:, k:k + 1], st[:, src:src + 1], AF.Exp, scale=-1.0),
                reads=[kk], writes=["sg%d" % k])
            p.op("dve", lambda e, k=k: e.tensor_scalar_add(
                st[:, k:k + 1], st[:, k:k + 1], 1.0),
                reads=["sg%d" % k], writes=["sg%d" % k])
            p.op("dve", lambda e, k=k: e.reciprocal(st[:, k + 2:k + 3],
                                                    st[:, k:k + 1]),
                 reads=["sg%d" % k], writes=["v%d" % k])
        p.op("dve", lambda e: e.tensor_add(st[:, 14:15], st[:, 12:13],
                                           st[:, 13:14]),
             reads=["v10", "v11"], writes=["vs"])
        p.op("dve", lambda e: e.reciprocal(st[:, 15:16], st[:, 14:15]),
             reads=["vs"], writes=["vsr"])
        p.op("dve", lambda e: e.tensor_scalar_mul(st[:, 15:16],
                                                  st[:, 15:16], 1.0 / 3.0),
             reads=["vsr"], writes=["vsr"])
        p.op("dve", lambda e: e.tensor_tensor(st[:, 10:11], st[:, 12:13],
                                              st[:, 15:16], ALU.mult),
             reads=["v10", "vsr"], writes=["wA"])
        p.op("dve", lambda e: e.tensor_tensor(st[:, 11:12], st[:, 13:14],
                                              st[:, 15:16], ALU.mult),
             reads=["v11", "vsr"], writes=["wB"])
        wa, wb_ = 10, 11
    p.op("dve", lambda e, wa=wa: e.tensor_scalar(
        b["cw"][:, 0:8], b["eq1"][:, 0:8], st[:, wa:wa + 1], None, ALU.mult),
        reads=["eq1", "wA"], writes=["cw"])
    p.op("dve", lambda e, wb_=wb_: e.scalar_tensor_tensor(
        b["cw"][:, 0:8], b["eq2"][:, 0:8], st[:, wb_:wb_ + 1],
        b["cw"][:, 0:8], ALU.mult, ALU.add),
        reads=["eq2", "wB", "cw"], writes=["cw"])


def emit_gate0_logits(p, b, i):
    for ch in range(ND):
        p.op("pe", lambda e, ch=ch: e.matmul(
            b["psD"][:, 0:8],
            b["xnT"][:, ch * T + i * 128: ch * T + i * 128 + 128],
            b["wg0b"][:, ch * 8:(ch + 1) * 8],
            start=(ch == 0), stop=(ch == ND - 1)),
            reads=["wg0b", "xnT"], writes=["psD"])


def emit_gate0_rest(p, b, i):
    _top2_softmax(p, b, sigmoid=False)
    for ch in range(ND):
        p.op("pe", lambda e, ch=ch: e.matmul(
            b["psB"][:, 0:128],
            b["xnT"][:, ch * T + i * 128: ch * T + i * 128 + 128],
            b["a0b"][:, ch * 128:(ch + 1) * 128],
            start=(ch == 0), stop=(ch == ND - 1)),
            reads=["a0b", "xnT"], writes=["psB0"])
    cw8 = b["cw"][:, 0:8]
    cwb = bass.AP(cw8.tensor, cw8.offset,
                  [list(d) for d in cw8.ap] + [[0, 16]])
    p.op("dve", lambda e: e.tensor_tensor(
        b["hw0"].rearrange("p (g r) -> p g r", g=8),
        b["psB"][:, 0:128].rearrange("p (g r) -> p g r", g=8), cwb,
        ALU.mult),
        reads=["psB0", "cw"], writes=["hw0"])
    p.op("pe", lambda e: e.transpose(b["psT"][:, 0:128], b["hw0"],
                                     b["ident"][:]),
         reads=["hw0", "ident"], writes=["psT"])
    p.op("dve", lambda e: e.tensor_copy(b["hw0t"], b["psT"][:, 0:128]),
         reads=["psT"], writes=["hw0t"])
    for half in range(2):
        p.op("pe", lambda e, half=half: e.matmul(
            b["psC"][:, half * 512:(half + 1) * 512], b["hw0t"],
            b["b0b"][:, half * 512:(half + 1) * 512],
            start=False, stop=False), reads=["hw0t", "b0b"],
            writes=["psC%d" % half])


def load_asm0_weights(p, b, t):
    p.op("pool", lambda e: e.memset(b["zrow"][0:1, :], 0.0), writes=["zrow"])
    p.op("pool", lambda e: e.dma_start(
        out=b["wob"].rearrange("p (ch c) -> p ch c", ch=ND),
        in_=t["wo0"].rearrange("(ch p) c -> p ch c", p=128)),
        writes=["wob"], dma=True)
    p.op("pool", lambda e: e.dma_start(
        out=b["wg0b"].rearrange("p (ch c) -> p ch c", ch=ND),
        in_=t["wg0"].rearrange("(ch p) c -> p ch c", p=128)),
        writes=["wg0b"], dma=True)
    p.op("pool", lambda e: e.dma_start(
        out=b["a0b"].rearrange("p (ch c) -> p ch c", ch=ND),
        in_=t["a0"].rearrange("(ch p) c -> p ch c", p=128)),
        writes=["a0b"], dma=True)
    p.op("pool", lambda e: e.dma_start(out=b["b0b"], in_=t["b0"]),
         writes=["b0b"], dma=True)
    p.op("pool", lambda e: e.dma_start(out=b["bpjb"][0:1, :], in_=t["bproj"]),
         writes=["bpjb"], dma=True)


def emit_assembly_l0(p, b, x0_dram, ln1, x1_dram, x1_out):
    for i in range(NT):
        psy = b["psC"]
        emit_gate0_logits(p, b, i)
        for half in range(2):
            p.op("pe", lambda e, half=half: e.matmul(
                psy[:, half * 512:(half + 1) * 512], b["onesr"][0:1, :],
                b["bpjb"][0:1, half * 512:(half + 1) * 512],
                start=True, stop=False),
                reads=["bpjb", "onesr"], writes=["psC%d" % half])
        for ch in range(ND):
            for half in range(2):
                p.op("pe", lambda e, ch=ch, i=i, half=half: e.matmul(
                    psy[:, half * 512:(half + 1) * 512],
                    b["oT"][:, ch * T + i * 128: ch * T + i * 128 + 128],
                    b["wob"][:, ch * 1024 + half * 512:
                             ch * 1024 + (half + 1) * 512],
                    start=False, stop=False), reads=["oT", "wob"],
                    writes=["psC%d" % half])
        emit_gate0_rest(p, b, i)
        for half in range(2):
            p.op("pe", lambda e, half=half: e.matmul(
                psy[:, half * 512:(half + 1) * 512], b["onesr"][0:1, :],
                b["zrow"][0:1, 0:512],
                start=False, stop=True),
                reads=["zrow", "onesr"], writes=["psC%d" % half])
        par = i % 2
        xt = b["xt0"] if par == 0 else b["xt1"]
        xk = "xt%d" % par
        p.op("sync", lambda e, i=i, xt=xt: e.dma_start(
            out=xt[:], in_=x0_dram[i * 128:(i + 1) * 128, :]),
            writes=[xk], dma=True)
        p.op("dve", lambda e, xt=xt: e.scalar_tensor_tensor(
            xt[:], psy[:], 1.0, xt[:], ALU.mult, ALU.add),
            reads=["psC0", "psC1"], writes=[xk])
        if x1_dram is not None:
            p.op("sync", lambda e, i=i, xt=xt: e.dma_start(
                out=x1_dram[i * 128:(i + 1) * 128, :], in_=xt[:]),
                reads=[xk], writes=["x1scr"], dma=True)
        if x1_out is not None:
            p.op("sync", lambda e, i=i, xt=xt: e.dma_start(
                out=x1_out[i * 128:(i + 1) * 128, :], in_=xt[:]),
                reads=[xk], writes=["x1out"], dma=True)
        if ln1:
            _ln_tile(p, b, xt, xk, i, b["xnT"], "xnT", T)


def yacc_slice(b, i):
    return b["xnT"][:, i * 2048:(i + 1) * 2048].bitcast(F32)


def emit_gating1_proj1(p, b, t):
    p.op("pool", lambda e: e.dma_start(
        out=b["wg1b"].rearrange("p (ch c) -> p ch c", ch=ND),
        in_=t["wg1"].rearrange("(ch p) c -> p ch c", p=128)),
        writes=["wg1b"], dma=True)
    p.op("pool", lambda e: e.dma_start(
        out=b["wob"].rearrange("p (ch c) -> p ch c", ch=ND),
        in_=t["wo1"].rearrange("(ch p) c -> p ch c", p=128)),
        writes=["wob"], dma=True)
    sh = b["wcols"].rearrange("p (i e) -> p i e", i=NTO)
    p.op("pool", lambda e: e.memset(sh[:, :, 0:1], 1.0 / 3.0),
         writes=["wcols"])
    for i in range(NTO):
        for ch in range(ND):
            p.op("pe", lambda e, ch=ch, i=i: e.matmul(
                b["psD"][:, 0:8],
                b["ownT"][:, ch * TO + i * 128: ch * TO + i * 128 + 128],
                b["wg1b"][:, ch * 8:(ch + 1) * 8],
                start=(ch == 0), stop=(ch == ND - 1)),
                reads=["wg1b", "ownT"], writes=["psD"])
        for ch in range(ND):
            for half in range(2):
                p.op("pe", lambda e, ch=ch, i=i, half=half: e.matmul(
                    b["psC"][:, half * 512:(half + 1) * 512],
                    b["oT"][:, ch * TO + i * 128: ch * TO + i * 128 + 128],
                    b["wob"][:, ch * 1024 + half * 512:
                             ch * 1024 + (half + 1) * 512],
                    start=(ch == 0), stop=(ch == ND - 1)),
                    reads=["oT", "wob"], writes=["psC%d" % half])
        _top2_softmax(p, b, sigmoid=True)
        p.op("dve", lambda e, i=i: e.tensor_copy(
            b["wcols"][:, i * NE + 1: i * NE + 9], b["cw"][:, 0:8]),
            reads=["cw"], writes=["wcols"])
        p.op("dve", lambda e, i=i: e.scalar_tensor_tensor(
            yacc_slice(b, i), b["psC"][:], 1.0, yacc_slice(b, i),
            ALU.mult, ALU.add),
            reads=["psC0", "psC1", "yacc"], writes=["yacc"])


def emit_ffn1(p, b, we13_dram, we2_dram):
    dq = []

    def pop_d():
        if dq:
            dq.pop(0)()

    for e_i in range(NE):
        for fq in range(4):
            par = (e_i * 4 + fq) % 2
            w13k, w2k = "w13_%d" % par, "w2_%d" % par
            w13 = b["w13b0"] if par == 0 else b["w13b1"]
            w2 = b["w2b0"] if par == 0 else b["w2b1"]
            p.op("pool", lambda e, e_i=e_i, fq=fq, w13=w13: e.dma_start(
                out=w13.rearrange("p (ch c) -> p ch c", ch=ND),
                in_=we13_dram[e_i][:, fq * 1024:(fq + 1) * 1024].rearrange(
                    "(ch p) c -> p ch c", p=128)),
                writes=[w13k], dma=True)
            p.op("pool", lambda e, e_i=e_i, fq=fq, w2=w2: e.dma_start(
                out=w2.rearrange("p (ch c) -> p ch c", ch=4),
                in_=we2_dram[e_i][fq * 512:(fq + 1) * 512, :].rearrange(
                    "(ch p) c -> p ch c", p=128)),
                writes=[w2k], dma=True)
            for tc in range(4):
                gpar = (e_i * 16 + fq * 4 + tc) % 2
                hw = b["hwT0"] if gpar == 0 else b["hwT1"]
                hwk = "hwT%d" % gpar
                for j in range(4):
                    up = j % 2
                    uk1, uk3 = "psA%d" % up, "psB%d" % up
                    for ch in range(ND):
                        p.op("pe", lambda e, ch=ch, j=j, tc=tc, up=up,
                             w13=w13: e.matmul(
                                 b["psA"][:, up * 512:up * 512 + 512],
                                 w13[:, ch * 1024 + j * 128:
                                     ch * 1024 + j * 128 + 128],
                                 b["ownT"][:, ch * TO + tc * 512:
                                           ch * TO + tc * 512 + 512],
                                 start=(ch == 0), stop=(ch == ND - 1)),
                             reads=[w13k, "ownT"], writes=[uk1])
                    for ch in range(ND):
                        p.op("pe", lambda e, ch=ch, j=j, tc=tc, up=up,
                             w13=w13: e.matmul(
                                 b["psB"][:, up * 512:up * 512 + 512],
                                 w13[:, ch * 1024 + 512 + j * 128:
                                     ch * 1024 + 512 + j * 128 + 128],
                                 b["ownT"][:, ch * TO + tc * 512:
                                           ch * TO + tc * 512 + 512],
                                 start=(ch == 0), stop=(ch == ND - 1)),
                             reads=[w13k, "ownT"], writes=[uk3])
                    p.op("act", lambda e, up=up, j=j: e.activation(
                        b["s1b"][:, up * 512:up * 512 + 512],
                        b["psA"][:, up * 512:up * 512 + 512], AF.Silu),
                        reads=[uk1], writes=["s1b%d" % up])
                    p.op("dve", lambda e, up=up, j=j, hw=hw: e.tensor_tensor(
                        hw[:, j * 512:(j + 1) * 512],
                        b["s1b"][:, up * 512:up * 512 + 512],
                        b["psB"][:, up * 512:up * 512 + 512], ALU.mult),
                        reads=[uk3, "s1b%d" % up], writes=[hwk])
                    pop_d()

                def mk_down(e_i, tc, hw, hwk, w2, w2k):
                    def gen(il):
                        def f():
                            i = tc * 4 + il
                            for j in range(4):
                                for half in range(2):
                                    p.op("pe", lambda e, j=j, half=half,
                                         il=il, hw=hw, w2=w2: e.matmul(
                                             b["psC"][:, half * 512:
                                                      half * 512 + 512],
                                             hw[:, j * 512 + il * 128:
                                                j * 512 + il * 128 + 128],
                                             w2[:, j * 1024 + half * 512:
                                                j * 1024 + half * 512 + 512],
                                             start=(j == 0), stop=(j == 3)),
                                         reads=[hwk, w2k],
                                         writes=["psC%d" % half])
                            wcol = b["wcols"][:, i * NE + e_i:
                                              i * NE + e_i + 1]
                            p.op("dve", lambda e, i=i, wcol=wcol:
                                 e.scalar_tensor_tensor(
                                     yacc_slice(b, i), b["psC"][:], wcol,
                                     yacc_slice(b, i), ALU.mult, ALU.add),
                                 reads=["psC0", "psC1", "wcols", "yacc"],
                                 writes=["yacc"])
                        return f
                    return [gen(il) for il in range(4)]
                dq.extend(mk_down(e_i, tc, hw, hwk, w2, w2k))
    while dq:
        pop_d()


def emit_assembly_l1(p, b, x1own_src, out_dram):
    for i in range(NTO):
        par = i % 2
        xt = b["xt0"] if par == 0 else b["xt1"]
        xk = "xt%d" % par
        if x1own_src[0] == "dram":
            p.op("sync", lambda e, i=i, xt=xt: e.dma_start(
                out=xt[:], in_=x1own_src[1][i * 128:(i + 1) * 128, :]),
                writes=[xk], dma=True)
        else:
            idxs, x1d = x1own_src[1], x1own_src[2]
            p.op("pool", lambda e, i=i, xt=xt: e.indirect_dma_start(
                out=xt[:], out_offset=None, in_=x1d,
                in_offset=IndirectOffsetOnAxis(idxs[:, i:i + 1], 0)),
                reads=["idxX", "x1scr"], writes=[xk], dma=True)
        p.op("dve", lambda e, i=i, xt=xt: e.scalar_tensor_tensor(
            xt[:], yacc_slice(b, i), 1.0, xt[:], ALU.mult, ALU.add),
            reads=["yacc"], writes=[xk])
        p.op("sync", lambda e, i=i, xt=xt: e.dma_start(
            out=out_dram[i * 128:(i + 1) * 128, :], in_=xt[:]),
            reads=[xk], writes=["x2out"], dma=True)


def build_l0(x1_as_output):
    nc = bass.Bass("TRN2", target_bir_lowering=False)
    t = {}
    t["x0"] = nc.dram_tensor("x0", [T, D], F32, kind="ExternalInput").ap()
    t["wqk0"] = nc.dram_tensor("wqk0", [D, H * 128], F32,
                               kind="ExternalInput").ap()
    t["wv0"] = nc.dram_tensor("wv0", [D, H * 64], F32,
                              kind="ExternalInput").ap()
    t["bqk0"] = nc.dram_tensor("bqk0", [128, 16], F32,
                               kind="ExternalInput").ap()
    t["bv0"] = nc.dram_tensor("bv0", [16, 64], F32,
                              kind="ExternalInput").ap()
    t["wo0"] = nc.dram_tensor("wo0", [D, D], F32, kind="ExternalInput").ap()
    t["bproj"] = nc.dram_tensor("bproj", [1, D], F32,
                                kind="ExternalInput").ap()
    t["wg0"] = nc.dram_tensor("wg0", [D, 8], F32, kind="ExternalInput").ap()
    t["a0"] = nc.dram_tensor("a0", [D, 128], F32, kind="ExternalInput").ap()
    t["b0"] = nc.dram_tensor("b0", [128, D], F32, kind="ExternalInput").ap()
    t["gtab"] = nc.dram_tensor("gtab", [H * 128, GW], BF16,
                               kind="ExternalInput").ap()
    x1_out = nc.dram_tensor("x1", [T, D], F32, kind="ExternalOutput").ap() \
        if x1_as_output else None

    with ExitStack() as st:
        b = alloc_bufs(nc, st, "l0")
        p = Prog(nc, st)
        emit_consts(p, b)
        p.op("sync", lambda e: e.dma_start(out=b["bqk"][:], in_=t["bqk0"]),
             writes=["bqk"], dma=True)
        emit_lnT(p, b, t["x0"], NT, b["xnT"], "xnT", T)
        p.barrier(["xnb", "sq1k"])
        emit_vb_ones(p, b)
        _attn_layer(p, b, 0, t, T, T, None)
        p.barrier(ATTN_KEYS)
        load_asm0_weights(p, b, t)
        emit_assembly_l0(p, b, t["x0"], ln1=False, x1_dram=None,
                         x1_out=x1_out)
        p.finish()
    return nc


def build_l1():
    nc = bass.Bass("TRN2", target_bir_lowering=False)
    t = {}
    t["x1f"] = nc.dram_tensor("x1f", [T, D], F32, kind="ExternalInput").ap()
    t["x1o"] = nc.dram_tensor("x1o", [TO, D], F32, kind="ExternalInput").ap()
    t["wqk1"] = nc.dram_tensor("wqk1", [D, H * 128], F32,
                               kind="ExternalInput").ap()
    t["wv1"] = nc.dram_tensor("wv1", [D, H * 64], F32,
                              kind="ExternalInput").ap()
    t["wo1"] = nc.dram_tensor("wo1", [D, D], F32, kind="ExternalInput").ap()
    t["wg1"] = nc.dram_tensor("wg1", [D, 8], F32, kind="ExternalInput").ap()
    we13 = [nc.dram_tensor("we13_%d" % i, [D, 2 * FF], F32,
                           kind="ExternalInput").ap() for i in range(NE)]
    we2 = [nc.dram_tensor("we2_%d" % i, [FF, D], F32,
                          kind="ExternalInput").ap() for i in range(NE)]
    out = nc.dram_tensor("x2", [TO, D], F32, kind="ExternalOutput").ap()

    with ExitStack() as st:
        b = alloc_bufs(nc, st, "l1")
        p = Prog(nc, st)
        emit_consts(p, b)
        emit_lnT(p, b, t["x1f"], NT, b["xnT"], "xnT", T)
        emit_lnT(p, b, t["x1o"], NTO, b["ownT"], "ownT", TO)
        p.barrier(["xnb", "sq1k"])
        emit_vb_ones(p, b)
        _attn_layer(p, b, 1, t, TO, TO, "ownT")
        p.barrier(ATTN_KEYS)
        p.op("pool", lambda e: e.memset(b["xnT"][:], 0.0),
             reads=["xnT"], writes=["xnT", "yacc"])
        emit_gating1_proj1(p, b, t)
        p.barrier(["wob"])
        emit_ffn1(p, b, we13, we2)
        emit_assembly_l1(p, b, ("dram", t["x1o"]), out)
        p.finish()
    return nc




ASM0_KEYS = ["wob", "a0b", "b0b", "hw0", "hw0t", "bpjb", "zrow", "wg0b",
             "xnb", "sq1k", "cw", "gbuf", "eq1", "eq2"]


def build_fused():
    nc = bass.Bass("TRN2", target_bir_lowering=False)
    t = {}
    t["x0"] = nc.dram_tensor("x0", [T, D], F32, kind="ExternalInput").ap()
    t["wqk0"] = nc.dram_tensor("wqk0", [D, H * 128], F32,
                               kind="ExternalInput").ap()
    t["wv0"] = nc.dram_tensor("wv0", [D, H * 64], F32,
                              kind="ExternalInput").ap()
    t["bqk0"] = nc.dram_tensor("bqk0", [128, 16], F32,
                               kind="ExternalInput").ap()
    t["bv0"] = nc.dram_tensor("bv0", [16, 64], F32,
                              kind="ExternalInput").ap()
    t["wo0"] = nc.dram_tensor("wo0", [D, D], F32, kind="ExternalInput").ap()
    t["bproj"] = nc.dram_tensor("bproj", [1, D], F32,
                                kind="ExternalInput").ap()
    t["wg0"] = nc.dram_tensor("wg0", [D, 8], F32, kind="ExternalInput").ap()
    t["a0"] = nc.dram_tensor("a0", [D, 128], F32, kind="ExternalInput").ap()
    t["b0"] = nc.dram_tensor("b0", [128, D], F32, kind="ExternalInput").ap()
    t["gtab"] = nc.dram_tensor("gtab", [H * 128, GW], BF16,
                               kind="ExternalInput").ap()
    t["wqk1"] = nc.dram_tensor("wqk1", [D, H * 128], F32,
                               kind="ExternalInput").ap()
    t["wv1"] = nc.dram_tensor("wv1", [D, H * 64], F32,
                              kind="ExternalInput").ap()
    t["wo1"] = nc.dram_tensor("wo1", [D, D], F32, kind="ExternalInput").ap()
    t["wg1"] = nc.dram_tensor("wg1", [D, 8], F32, kind="ExternalInput").ap()
    we13 = [nc.dram_tensor("we13_%d" % i, [D, 2 * FF], F32,
                           kind="ExternalInput").ap() for i in range(NE)]
    we2 = [nc.dram_tensor("we2_%d" % i, [FF, D], F32,
                          kind="ExternalInput").ap() for i in range(NE)]
    idxT = nc.dram_tensor("idxT", [128, 8], I32, kind="ExternalInput").ap()
    idxX = nc.dram_tensor("idxX", [128, 16], I32, kind="ExternalInput").ap()
    x1scr = nc.dram_tensor("x1scr", [T, D], F32, kind="Internal").ap()
    xn1scr = nc.dram_tensor("xn1scr", [D, T], BF16, kind="Internal").ap()
    out = nc.dram_tensor("x2", [TO, D], F32, kind="ExternalOutput").ap()

    with ExitStack() as st:
        b = alloc_bufs(nc, st, "fused")
        p = Prog(nc, st)
        emit_consts(p, b)
        p.op("sync", lambda e: e.dma_start(out=b["bqk"][:], in_=t["bqk0"]),
             writes=["bqk"], dma=True)
        p.op("sync", lambda e: e.dma_start(out=b["idxT"][:], in_=idxT),
             writes=["idxT"], dma=True)
        p.op("sync", lambda e: e.dma_start(out=b["idxX"][:], in_=idxX),
             writes=["idxX"], dma=True)
        emit_lnT(p, b, t["x0"], NT, b["xnT"], "xnT", T)
        p.barrier(["xnb", "sq1k"])
        emit_vb_ones(p, b)
        _attn_layer(p, b, 0, t, T, T, None)
        p.barrier(ATTN_KEYS)
        load_asm0_weights(p, b, t)
        emit_assembly_l0(p, b, t["x0"], ln1=True, x1_dram=x1scr, x1_out=None)
        for ch in range(ND):
            p.op("sync", lambda e, ch=ch: e.dma_start(
                out=xn1scr[ch * 128:(ch + 1) * 128, :],
                in_=b["xnT"][:, ch * T:(ch + 1) * T]),
                reads=["xnT"], writes=["xn1scr"], dma=True)
        p.barrier(ASM0_KEYS + ["oT"])
        xn1flat = bass.AP(xn1scr.tensor, 0, [[1, D * T], [1, 1]])
        for ch in range(ND):
            p.op("pool", lambda e, ch=ch: e.indirect_dma_start(
                out=b["ownT"][:, ch * TO:(ch + 1) * TO], out_offset=None,
                in_=xn1flat,
                in_offset=IndirectOffsetOnAxis(b["idxT"][:, ch:ch + 1], 0)),
                reads=["xn1scr", "idxT"], writes=["ownT"], dma=True)
        emit_vb_ones(p, b)
        _attn_layer(p, b, 1, t, TO, TO, "ownT")
        p.barrier(ATTN_KEYS)
        p.op("pool", lambda e: e.memset(b["xnT"][:], 0.0),
             reads=["xnT"], writes=["xnT", "yacc"])
        emit_gating1_proj1(p, b, t)
        p.barrier(["wob", "cw", "gbuf", "eq1", "eq2"])
        emit_ffn1(p, b, we13, we2)
        emit_assembly_l1(p, b, ("gather", b["idxX"], x1scr), out)
        p.finish()
    return nc



def _prep_shared(w):
    g = {}
    wa = np.asarray(w["L0_attn_w_attn"], np.float32)
    qk = np.empty((D, H * 128), np.float32)
    for h in range(H):
        qk[:, h * 128:h * 128 + 64] = wa[:, D + h * 64:D + (h + 1) * 64]
        qk[:, h * 128 + 64:h * 128 + 128] = wa[:, h * 64:(h + 1) * 64]
    g["wqk0"] = qk
    g["wv0"] = np.ascontiguousarray(wa[:, 2 * D:3 * D])
    ba = np.asarray(w["L0_attn_b_attn"], np.float32)
    bqk = np.empty((128, 16), np.float32)
    for h in range(H):
        bqk[0:64, h] = ba[D + h * 64:D + (h + 1) * 64]
        bqk[64:128, h] = ba[h * 64:(h + 1) * 64]
    g["bqk0"] = bqk
    g["bv0"] = np.ascontiguousarray(ba[2 * D:].reshape(16, 64))
    g["wo0"] = np.ascontiguousarray(np.asarray(w["L0_attn_w_proj"],
                                               np.float32))
    g["bproj"] = np.ascontiguousarray(
        np.asarray(w["L0_attn_b_proj"], np.float32)[None, :])
    g["wg0"] = np.ascontiguousarray(np.asarray(w["L0_ffn_Wg"], np.float32))
    g["a0"] = np.ascontiguousarray(
        np.asarray(w["L0_ffn_A"], np.float32).transpose(1, 0, 2).reshape(
            D, 128))
    g["b0"] = np.ascontiguousarray(
        np.asarray(w["L0_ffn_B"], np.float32).reshape(128, D))
    wq = np.asarray(w["L1_attn_W_q"], np.float32)
    wk2 = np.asarray(w["L1_attn_W_k"], np.float32)
    wv2 = np.asarray(w["L1_attn_W_v"], np.float32)
    qk1 = np.empty((D, H * 128), np.float32)
    for h in range(H):
        qk1[:, h * 128:h * 128 + 64] = wk2[h]
        qk1[:, h * 128 + 64:h * 128 + 128] = wq[h]
    g["wqk1"] = qk1
    g["wv1"] = np.ascontiguousarray(wv2.transpose(1, 0, 2).reshape(D, H * 64))
    g["wo1"] = np.ascontiguousarray(np.asarray(w["L1_attn_W_o"], np.float32))
    g["wg1"] = np.ascontiguousarray(np.asarray(w["L1_ffn_Wg"], np.float32))
    s1 = np.asarray(w["L1_ffn_Ws1"], np.float32)
    s3 = np.asarray(w["L1_ffn_Ws3"], np.float32)
    s2 = np.asarray(w["L1_ffn_Ws2"], np.float32)
    e1 = np.asarray(w["L1_ffn_We1"], np.float32)
    e3 = np.asarray(w["L1_ffn_We3"], np.float32)
    e2 = np.asarray(w["L1_ffn_We2"], np.float32)

    def inter(a, bm):
        out = np.empty((D, 2 * FF), np.float32)
        for fq in range(4):
            out[:, fq * 1024:fq * 1024 + 512] = a[:, fq * 512:(fq + 1) * 512]
            out[:, fq * 1024 + 512:(fq + 1) * 1024] = \
                bm[:, fq * 512:(fq + 1) * 512]
        return out
    for i in range(NE):
        if i == 0:
            g["we13_%d" % i] = inter(s1, s3)
            g["we2_%d" % i] = np.ascontiguousarray(s2)
        else:
            g["we13_%d" % i] = inter(e1[i - 1], e3[i - 1])
            g["we2_%d" % i] = np.ascontiguousarray(e2[i - 1])
    r = (np.arange(128, dtype=np.float64)[:, None]
         - (np.arange(GW, dtype=np.float64)[None, :] - GOFF))
    gtab = np.zeros((H * 128, GW), dtype=ml_dtypes.bfloat16)
    with np.errstate(under="ignore"):
        for h in range(H):
            gh = np.where(r <= 0, np.exp(SLOPES[h] * r), 0.0)
            gtab[h * 128:(h + 1) * 128] = gh.astype(ml_dtypes.bfloat16)
    g["gtab"] = gtab
    return g


FUSED = True


def kernel(**inputs):
    g = _prep_shared(inputs)
    x0 = np.ascontiguousarray(np.asarray(inputs["L0_attn_x"], np.float32))
    if FUSED:
        return kernel_fused(inputs, g, x0)

    nc0 = build_l0(x1_as_output=True)
    l0_keys = ["wqk0", "wv0", "bqk0", "bv0", "wo0", "bproj", "wg0", "a0",
               "b0", "gtab"]
    in_maps0 = []
    for c in range(8):
        m = {"x0": x0[c // 2]}
        for k in l0_keys:
            m[k] = g[k]
        in_maps0.append(m)
    r0 = run_bass_kernel_spmd(nc0, in_maps0, core_ids=list(range(8)))
    x1 = [np.asarray(r0.results[2 * bb]["x1"]) for bb in range(B)]

    nc1 = build_l1()
    l1_keys = (["wqk1", "wv1", "wo1", "wg1"] +
               ["we13_%d" % i for i in range(NE)] +
               ["we2_%d" % i for i in range(NE)])
    in_maps1 = []
    for c in range(8):
        bb, hf = c // 2, c % 2
        m = {"x1f": x1[bb],
             "x1o": np.ascontiguousarray(x1[bb][hf * TO:(hf + 1) * TO])}
        for k in l1_keys:
            m[k] = g[k]
        in_maps1.append(m)
    r1 = run_bass_kernel_spmd(nc1, in_maps1, core_ids=list(range(8)))

    out = np.empty((B, T, D), np.float32)
    for c in range(8):
        bb, hf = c // 2, c % 2
        out[bb, hf * TO:(hf + 1) * TO] = np.asarray(r1.results[c]["x2"])
    return out


def kernel_fused(inputs, g, x0):
    nc = build_fused()
    keys = (["wqk0", "wv0", "bqk0", "bv0", "wo0", "bproj", "wg0", "a0", "b0",
             "gtab", "wqk1", "wv1", "wo1", "wg1"] +
            ["we13_%d" % i for i in range(NE)] +
            ["we2_%d" % i for i in range(NE)])
    pidx = np.arange(128)
    in_maps = []
    for c in range(8):
        bb, hf = c // 2, c % 2
        own0 = hf * TO
        idxT_np = np.empty((128, 8), np.int32)
        for ch in range(ND):
            idxT_np[:, ch] = (ch * 128 + pidx) * T + own0
        idxX_np = np.empty((128, 16), np.int32)
        for i in range(NTO):
            idxX_np[:, i] = own0 + i * 128 + pidx
        m = {"x0": x0[bb], "idxT": idxT_np, "idxX": idxX_np}
        for k in keys:
            m[k] = g[k]
        in_maps.append(m)
    r = run_bass_kernel_spmd(nc, in_maps, core_ids=list(range(8)))
    out = np.empty((B, T, D), np.float32)
    for c in range(8):
        bb, hf = c // 2, c % 2
        out[bb, hf * TO:(hf + 1) * TO] = np.asarray(r.results[c]["x2"])
    return out


# version 16
# speedup vs baseline: 1.6190x; 1.1659x over previous
import numpy as np
import ml_dtypes
from contextlib import ExitStack

import concourse.bass as bass
import concourse.mybir as mybir
from concourse.bass import IndirectOffsetOnAxis
from concourse.bass_utils import run_bass_kernel_spmd

F32 = mybir.dt.float32
BF16 = mybir.dt.bfloat16
I32 = mybir.dt.int32
AF = mybir.ActivationFunctionType
ALU = mybir.AluOpType
AX = mybir.AxisListType.X

B, T, D, H, DH = 4, 4096, 1024, 16, 64
TO = T // 2
NT, NTO = T // 128, TO // 128
ND = D // 128
CEXP = 20.0
EPS = 1e-5
FF = 2048
NE = 9
SLOPES = [2.0 ** (-(8.0 / H) * (i + 1)) for i in range(H)]
GW = 4480
GOFF = 384
SKIP_GAP = 25.0
QW = 512
UCOLS = 33024


class Prog:
    def __init__(self, nc, st):
        self.nc = nc
        self.st = st
        self.e = {"pe": nc.tensor, "act": nc.scalar, "dve": nc.vector,
                  "pool": nc.gpsimd, "sync": nc.sync}
        self.sem = {}
        for sname in ("pe", "act", "dve", "pool"):
            self.sem[sname] = st.enter_context(nc.semaphore("sem_" + sname))
        self.cnt = {sname: 0 for sname in self.sem}
        self.waited = {sname: {} for sname in
                       ("pe", "act", "dve", "pool", "sync")}
        self.buf = {}

    def _dma_sem(self, key):
        name = "d_" + key
        if name not in self.sem:
            self.sem[name] = self.st.enter_context(
                self.nc.semaphore("sem_" + name))
            self.cnt[name] = 0
        return name

    def wait(self, stream, tok):
        if tok is None:
            return
        chan, v = tok
        if stream == "pe" and chan == "pe":
            return
        if v <= self.waited[stream].get(chan, 0):
            return
        self.e[stream].wait_ge(self.sem[chan], v)
        self.waited[stream][chan] = v

    def op(self, stream, fn, reads=(), writes=(), dma=False, deps=()):
        for t in deps:
            self.wait(stream, t)
        for k in reads:
            sb_ = self.buf.get(k)
            if sb_ and sb_["w"]:
                self.wait(stream, sb_["w"])
        for k in writes:
            sb_ = self.buf.get(k)
            if sb_:
                if sb_["w"]:
                    self.wait(stream, sb_["w"])
                for t in sb_["r"]:
                    self.wait(stream, t)
        ins = fn(self.e[stream])
        if dma:
            assert writes, "DMA ops must declare a write key"
            chan = self._dma_sem(writes[0])
            inc = 16
        else:
            chan = stream
            inc = 1
        ins.then_inc(self.sem[chan], inc)
        self.cnt[chan] += inc
        tok = (chan, self.cnt[chan])
        for k in reads:
            self.buf.setdefault(k, {"w": None, "r": []})["r"].append(tok)
        for k in writes:
            self.buf[k] = {"w": tok, "r": []}
        return tok

    def barrier(self, keys, streams=("pe", "act", "dve", "pool", "sync")):
        best = {}
        for k in keys:
            sb_ = self.buf.get(k)
            if not sb_:
                continue
            for t in ([sb_["w"]] if sb_["w"] else []) + sb_["r"]:
                c, v = t
                if best.get(c, 0) < v:
                    best[c] = v
        for stream in streams:
            for c, v in best.items():
                self.wait(stream, (c, v))

    def finish(self):
        for chan in list(self.cnt):
            if self.cnt[chan]:
                self.wait("sync", (chan, self.cnt[chan]))


PS_KEYS = ["psA0", "psA1", "psB0", "psB1", "psC0", "psC1", "psD", "psT"]
ATTN_KEYS = ["qb", "vb", "gt", "sqA", "sqB", "pt00", "pt01", "pt10",
             "pt11", "wqkb0", "wqkb1", "wvb", "bvb", "rs0", "rs1",
             "rb0", "rb1", "ob0", "ob1", "kb", "qst0", "qst1", "mrow",
             "mx0", "mxb", "pcol"] + PS_KEYS
ASM_KEYS = ["wob", "a0b", "b0b", "hw0", "hw0t", "bpjb", "zrow", "wg0b",
            "xnb", "sq1k", "cw", "gbuf", "eq1", "eq2", "w13_0", "w13_1",
            "w2_0", "w2_1", "s1b", "hwb", "hwtb", "wg1b", "wcols"]


def uv(b, part0, nparts, col0, ncols, dtype=BF16):
    u = b["U"][:]
    ap = bass.AP(u.tensor, part0 * UCOLS + col0, [[UCOLS, nparts], [1, ncols]])
    if dtype == F32:
        ap = ap.bitcast(F32)
    return ap


def alloc_bufs(nc, st, phase):
    b = {}

    def sb(name, shape, dt):
        b[name] = st.enter_context(nc.sbuf_tensor(name, shape, dt))
    sb("xnT", [128, ND * T], BF16)
    if phase == "l1":
        sb("ownT", [128, ND * TO], BF16)
        sb("oT", [128, ND * TO], BF16)
    else:
        sb("oT", [128, ND * T], BF16)
    if phase == "fused":
        ot = b["oT"][:]
        b["ownT"] = bass.AP(ot.tensor, ot.offset + ND * TO,
                            [[ot.ap[0][0], 128], [1, ND * TO]])
        sb("idxTs", [128, 8], I32)
        sb("idxXs", [128, 16], I32)
        b["idxT"], b["idxX"] = b["idxTs"], b["idxXs"]
    sb("U", [128, UCOLS], BF16)
    sb("xt0", [128, 1024], F32)
    sb("xt1", [128, 1024], F32)
    sb("stats", [128, 24], F32)
    sb("mx", [128, 4], F32)
    sb("mxb", [128, 1], BF16)
    sb("mrow", [1, 24], F32)
    sb("mrowb", [1, 1], BF16)
    sb("cm", [128, 1], F32)
    sb("ident", [128, 128], BF16)
    sb("ones128", [128, 1], BF16)
    sb("onesr", [65, 128], BF16)
    sb("bvr", [1, 256], BF16)
    sb("bqk", [128, 16], F32)
    sb("konst", [128, 2], F32)
    sb("onesq", [128, 128], BF16)
    sb("rsum", [128, 1024], BF16)
    sb("cmsrc", [128, 1], BF16)
    sb("mselb", [128, 1040], BF16)
    nc.const_aps.register(F32, EPS, b["konst"][:, 0:1])
    nc.const_aps.register(F32, 1e-30, b["konst"][:, 1:2])
    b["kb"] = uv(b, 0, 64, 0, 4096)
    b["kbU"] = uv(b, 64, 64, 0, 4096)
    b["kb128"] = uv(b, 0, 128, 0, 4096)
    b["qb"] = uv(b, 0, 64, 4096, 4096)
    b["qstage"] = uv(b, 64, 64, 4096, 4096)
    b["qb128"] = uv(b, 0, 128, 4096, 4096)
    b["vb"] = uv(b, 0, 128, 8192, 8320)
    b["gt"] = uv(b, 0, 128, 16512, GW)
    b["sq"] = uv(b, 0, 64, 21504, 1024)
    b["sqH"] = uv(b, 64, 64, 21504, 1024)
    b["sq128"] = uv(b, 0, 128, 21504, 1024)
    b["pt0"] = uv(b, 0, 128, 22528, 1024)
    b["pt1"] = uv(b, 0, 128, 23552, 1024)
    b["rbA"] = uv(b, 0, 64, 29952, 1024, F32)
    b["rbB"] = uv(b, 0, 64, 24576, 1024, F32)
    b["wqkb0"] = uv(b, 0, 128, 25600, 1024)
    b["wqkb1"] = uv(b, 0, 128, 26624, 1024)
    b["wvb"] = uv(b, 0, 128, 27648, 2048)
    b["bvb"] = uv(b, 0, 128, 29696, 256)
    b["rb"] = uv(b, 0, 64, 29952, 1024)
    b["ob0"] = uv(b, 0, 64, 30976, 1024)
    b["ob1"] = uv(b, 0, 64, 32000, 1024)
    if phase in ("l0", "fused"):
        b["wob"] = uv(b, 0, 128, 0, 8192)
        b["a0b"] = uv(b, 0, 128, 8192, 1024)
        b["b0b"] = uv(b, 0, 128, 9216, 1024)
        b["hw0"] = uv(b, 0, 128, 10240, 128)
        b["hw0t"] = uv(b, 0, 128, 10368, 128)
        b["bpjb"] = uv(b, 0, 1, 10496, 1024)
        b["zrow"] = uv(b, 0, 1, 11520, 512)
        b["wg0b"] = uv(b, 0, 128, 12032, 64)
        b["xnb"] = uv(b, 0, 128, 12096, 1024)
        b["sq1k"] = uv(b, 0, 128, 13120, 1024)
        b["cw"] = uv(b, 0, 128, 14144, 32, F32)
        b["gbuf"] = uv(b, 0, 128, 14176, 32, F32)
        b["eq1"] = uv(b, 0, 128, 14208, 32, F32)
        b["eq2"] = uv(b, 0, 128, 14240, 32, F32)
    if phase in ("l1", "fused"):
        b["wob"] = uv(b, 0, 128, 0, 8192)
        b["w13b0"] = uv(b, 0, 128, 0, 8192)
        b["w13b1"] = uv(b, 0, 128, 8192, 8192)
        b["w2b0"] = uv(b, 0, 128, 16384, 4096)
        b["w2b1"] = uv(b, 0, 128, 20480, 4096)
        b["s1b"] = uv(b, 0, 128, 24576, 1024)
        b["wg1b"] = uv(b, 0, 128, 25600, 64)
        b["wcols"] = uv(b, 0, 128, 25664, 288, F32)
        b["hwT0"] = uv(b, 0, 128, 26240, 2048)
        b["hwT1"] = uv(b, 0, 128, 28288, 2048)
        if phase == "l1":
            b["cw"] = uv(b, 0, 128, 30336, 32, F32)
            b["gbuf"] = uv(b, 0, 128, 30368, 32, F32)
            b["eq1"] = uv(b, 0, 128, 30400, 32, F32)
            b["eq2"] = uv(b, 0, 128, 30432, 32, F32)
            b["xnb"] = uv(b, 0, 128, 30464, 1024)
            b["sq1k"] = uv(b, 0, 128, 31488, 1024)
    b["psA"] = st.enter_context(nc.psum_tensor("psA", [128, 1024], F32))
    b["psB"] = st.enter_context(nc.psum_tensor("psB", [128, 1024], F32))
    b["psC"] = st.enter_context(nc.psum_tensor("psC", [128, 1024], F32))
    b["psD"] = st.enter_context(nc.psum_tensor("psD", [128, 512], F32))
    b["psT"] = st.enter_context(nc.psum_tensor("psT", [128, 1024], BF16))
    return b


def emit_consts(p, b):
    import struct
    r0 = p.nc.gpsimd.alloc_register("fill0")
    p.nc.gpsimd.reg_mov(r0, 0)
    rn = p.nc.gpsimd.alloc_register("filln")
    p.nc.gpsimd.reg_mov(rn, struct.unpack("<i", struct.pack("<f", -1e30))[0])
    b["fill0"], b["filln"] = r0, rn
    p.op("pool", lambda e: e.memset(b["ident"][:], 1.0), writes=["ident"])
    p.op("pool", lambda e: e.affine_select(
        b["ident"][:], b["ident"][:], [[-1, 128]], ALU.is_equal, b["fill0"],
        channel_multiplier=1), reads=["ident"], writes=["ident"])
    p.op("pool", lambda e: e.memset(b["ones128"][:], 1.0), writes=["ones128"])
    p.op("pool", lambda e: e.memset(b["onesr"][0:65, :], 1.0), writes=["onesr"])
    p.op("pool", lambda e: e.memset(b["onesq"][:], 1.0), writes=["onesq"])
    p.op("pool", lambda e: e.memset(b["rsum"][:], 0.0), writes=["rs0", "rs1"])
    p.op("pool", lambda e: e.memset(b["cmsrc"][:], 0.0), writes=["cmsrc"])
    p.op("pool", lambda e: e.memset(b["kbU"][:], 0.0), writes=["kb"])
    p.op("pool", lambda e: e.memset(b["sq128"][:], 0.0),
         writes=["sqA", "sqB"])
    p.op("pool", lambda e: e.memset(b["konst"][:, 0:1], EPS), writes=["konst"])
    p.op("pool", lambda e: e.memset(b["konst"][:, 1:2], 1e-30),
         reads=["konst"], writes=["konst"])
    p.op("pool", lambda e: e.iota(
        b["stats"][:, 16:17], [[0, 1]], base=0, channel_multiplier=1,
        allow_small_or_imprecise_dtypes=True), writes=["pcol"])


def emit_vb_ones(p, b):
    p.op("pool", lambda e: e.memset(b["vb"][:], 0.0), writes=["vb"])
    vb = b["vb"].rearrange("p (k h c) -> p k h c", k=NT, h=4)
    p.op("pool", lambda e: e.memset(vb[:, :, :, 64:65], 1.0),
         reads=["vb"], writes=["vb"])


def _ln_tile(p, b, xt, xk, i, dst, dst_key, dstT):
    st = b["stats"]
    p.op("dve", lambda e: e.reduce_sum(st[:, 0:1], xt[:], axis=AX),
         reads=[xk], writes=["st0"])
    p.op("dve", lambda e: e.scalar_tensor_tensor(
        b["sq1k"], xt[:], 1.0, xt[:], ALU.mult, ALU.mult,
        accum_out=st[:, 1:2]), reads=[xk], writes=["st1", "sq1k"])
    p.op("dve", lambda e: e.tensor_scalar_mul(st[:, 2:3], st[:, 0:1], 1.0 / D),
         reads=["st0"], writes=["st2"])
    p.op("dve", lambda e: e.scalar_tensor_tensor(
        st[:, 3:4], st[:, 2:3], -1.0, st[:, 2:3], ALU.mult, ALU.mult),
        reads=["st2"], writes=["st3"])
    p.op("dve", lambda e: e.scalar_tensor_tensor(
        st[:, 4:5], st[:, 1:2], 1.0 / D, st[:, 3:4], ALU.mult, ALU.add),
        reads=["st1", "st3"], writes=["st4"])
    p.op("act", lambda e: e.activation(st[:, 5:6], st[:, 4:5], AF.Ln,
                                       bias=EPS),
         reads=["st4", "konst"], writes=["st5"])
    p.op("act", lambda e: e.activation(st[:, 6:7], st[:, 5:6], AF.Exp,
                                       scale=-0.5),
         reads=["st5"], writes=["st6"])
    p.op("dve", lambda e: e.tensor_scalar(
        b["xnb"], xt[:], st[:, 2:3], st[:, 6:7], ALU.subtract, ALU.mult),
        reads=[xk, "st2", "st6"], writes=["xnb"])
    for j in range(ND):
        p.op("pe", lambda e, j=j: e.transpose(
            b["psT"][:, j * 128:(j + 1) * 128],
            b["xnb"][:, j * 128:(j + 1) * 128], b["ident"][:]),
            reads=["xnb", "ident"], writes=["psT"])
    dv = dst[:]
    dst_ap = bass.AP(dv.tensor, dv.offset + i * 128,
                     [[dv.ap[0][0], 128], [dstT, ND], [1, 128]])
    p.op("dve", lambda e: e.tensor_copy(
        dst_ap, b["psT"][:].rearrange("p (j c) -> p j c", j=ND)),
        reads=["psT"], writes=[dst_key])


def emit_lnT(p, b, x_dram, ntiles, dst, dst_key, dstT):
    for i in range(ntiles):
        par = i % 2
        xt = b["xt0"] if par == 0 else b["xt1"]
        xk = "xt%d" % par
        p.op("sync", lambda e, i=i, xt=xt: e.dma_start(
            out=xt[:], in_=x_dram[i * 128:(i + 1) * 128, :]),
            writes=[xk], dma=True)
        _ln_tile(p, b, xt, xk, i, dst, dst_key, dstT)


def emit_qk_head(p, b, h, wqk_dram, has_bias, Tq, q_src_key=None):
    wk = "wqkb%d" % (h % 2)
    wb = b[wk]
    wslc = wqk_dram[:, h * 128:(h + 1) * 128]
    p.op("pool", lambda e: e.dma_start(
        out=wb.rearrange("p (ch c) -> p ch c", ch=ND),
        in_=wslc.rearrange("(ch p) c -> p ch c", p=128)),
        writes=[wk], dma=True)
    bk = b["bqk"][0:64, h:h + 1] if has_bias else 0.0
    bq = b["bqk"][64:128, h:h + 1] if has_bias else 0.0
    rds = ["bqk"] if has_bias else []
    nq = Tq // 512
    nk = T // 512
    n_cs = nq + nk
    cs_i = [0]
    pend = []

    def mk_colsum(slot, s_idx):
        i = cs_i[0]
        cs_i[0] += 1
        sqk = "sqA" if slot == 0 else "sqB"
        st, sp = (i == 0), (i == n_cs - 1)

        def emitfn():
            p.op("pe", lambda e: e.matmul(
                b["psD"][0:65, 0:512],
                b["mselb"][:, s_idx * 65:(s_idx + 1) * 65],
                b["sq128"][:, slot * 512:(slot + 1) * 512],
                start=st, stop=sp, skip_group_check=True),
                reads=[sqk, "mselb"], writes=["psD"])
        pend.append(emitfn)

    def flush():
        while pend:
            pend.pop(0)()

    def k_evac(tc, ps, bank):
        slot = tc % 2
        sqk = "sqA" if slot == 0 else "sqB"
        p.op("dve", lambda e, tc=tc, ps=ps: e.tensor_scalar(
            b["kb"][0:64, tc * 512:(tc + 1) * 512], ps[0:64, 0:512],
            bk, None, ALU.add),
            reads=[bank] + rds, writes=["kb"])
        p.op("act", lambda e, tc=tc, slot=slot: e.activation(
            b["sq"][0:64, slot * 512:(slot + 1) * 512],
            b["kb"][0:64, tc * 512:(tc + 1) * 512], AF.Square),
            reads=["kb"], writes=[sqk])
        mk_colsum(slot, 8 + tc)

    def q_evac_shift(tc, ps, bank):
        slot = tc % 2
        qk = "qst%d" % slot
        sqk = "sqA" if slot == 0 else "sqB"
        p.op("dve", lambda e, tc=tc, ps=ps: e.tensor_scalar(
            b["qstage"][0:64, tc * 512:(tc + 1) * 512], ps[64:128, 0:512],
            bq, 0.125, ALU.add, ALU.mult),
            reads=[bank] + rds, writes=[qk])
        p.op("act", lambda e, tc=tc: e.dma_start(
            out=b["qb"][0:64, tc * 512:(tc + 1) * 512],
            in_=b["qstage"][0:64, tc * 512:(tc + 1) * 512]),
            reads=[qk], writes=["qb"], dma=True)
        p.op("act", lambda e, tc=tc, slot=slot: e.activation(
            b["sqH"][0:64, slot * 512:(slot + 1) * 512],
            b["qstage"][0:64, tc * 512:(tc + 1) * 512], AF.Square),
            reads=[qk], writes=[sqk])
        mk_colsum(slot, tc)

    for tc in range(nk):
        bank = "psA0" if tc % 2 == 0 else "psB0"
        ps = b["psA"] if tc % 2 == 0 else b["psB"]
        for ch in range(ND):
            p.op("pe", lambda e, ch=ch, tc=tc, ps=ps: e.matmul(
                ps[:, 0:512], wb[:, ch * 128:(ch + 1) * 128],
                b["xnT"][:, ch * T + tc * 512: ch * T + tc * 512 + 512],
                start=(ch == 0), stop=(ch == ND - 1)),
                reads=[wk, "xnT"], writes=[bank])
        flush()
        k_evac(tc, ps, bank)
        if q_src_key is None and tc < nq:
            q_evac_shift(tc, ps, bank)
    if q_src_key is not None:
        qsrc = b[q_src_key]
        for tc in range(nq):
            bank = "psA0" if tc % 2 == 0 else "psB0"
            ps = b["psA"] if tc % 2 == 0 else b["psB"]
            for ch in range(ND):
                p.op("pe", lambda e, ch=ch, tc=tc, ps=ps: e.matmul(
                    ps[:, 0:512], wb[:, ch * 128:(ch + 1) * 128],
                    qsrc[:, ch * TO + tc * 512: ch * TO + tc * 512 + 512],
                    start=(ch == 0), stop=(ch == ND - 1)),
                    reads=[wk, q_src_key], writes=[bank])
            flush()
            q_evac_shift(tc, ps, bank)
    flush()


def emit_v_group(p, b, hg, wv_dram, bv_dram):
    p.op("pool", lambda e: e.dma_start(
        out=b["wvb"].rearrange("p (ch c) -> p ch c", ch=ND),
        in_=wv_dram[:, hg * 256:(hg + 1) * 256].rearrange(
            "(ch p) c -> p ch c", p=128)),
        writes=["wvb"], dma=True)
    have_bias = bv_dram is not None
    if have_bias:
        p.op("pool", lambda e: e.dma_start(
            out=b["bvr"][0:1, 0:256],
            in_=bv_dram[hg * 4:(hg + 1) * 4, :].rearrange("(o h) c -> o (h c)", o=1)),
            writes=["bvr"], dma=True)
        p.op("pe", lambda e: e.matmul(
            b["psD"][:, 0:256], b["onesr"][0:1, :], b["bvr"][0:1, 0:256],
            start=True, stop=True), reads=["bvr", "onesr"], writes=["psD"])
        p.op("dve", lambda e: e.tensor_copy(b["bvb"][:, 0:256],
                                            b["psD"][:, 0:256]),
             reads=["psD"], writes=["bvb"])
    vv = b["vb"]
    for i in range(NT):
        bank = "psA0" if i % 2 == 0 else "psB0"
        ps = b["psA"] if i % 2 == 0 else b["psB"]
        for ch in range(ND):
            p.op("pe", lambda e, ch=ch, i=i, ps=ps: e.matmul(
                ps[:, 0:256],
                b["xnT"][:, ch * T + i * 128: ch * T + i * 128 + 128],
                b["wvb"][:, ch * 256:(ch + 1) * 256],
                start=(ch == 0), stop=(ch == ND - 1)),
                reads=["wvb", "xnT"], writes=[bank])
        out_ap = bass.AP(vv.tensor, vv.offset + i * 260,
                         [[UCOLS, 128], [65, 4], [1, 64]])
        src = ps[:, 0:256].rearrange("p (h c) -> p h c", h=4)
        if have_bias:
            p.op("dve", lambda e, src=src, out_ap=out_ap:
                 e.scalar_tensor_tensor(
                     out_ap, src, 1.0,
                     b["bvb"][:, 0:256].rearrange("p (h c) -> p h c", h=4),
                     ALU.mult, ALU.add),
                 reads=[bank, "bvb"], writes=["vb"])
        else:
            p.op("dve", lambda e, src=src, out_ap=out_ap: e.tensor_copy(
                out_ap, src), reads=[bank], writes=["vb"])


def emit_head_M(p, b, Tq):
    nq = Tq // 512
    mr = b["mrow"]
    p.op("dve", lambda e: e.reduce_max(b["mx"][0:16, 0:1],
                                       b["psD"][0:16, 0:512], axis=AX),
         reads=["psD"], writes=["mx0"])
    p.op("dve", lambda e: e.tensor_copy(b["mxb"][0:16, 0:1],
                                        b["mx"][0:16, 0:1]),
         reads=["mx0"], writes=["mxb"])
    p.op("pe", lambda e: e.transpose(b["psT"][0:1, 0:128], b["mxb"][:],
                                     b["ident"][:]),
         reads=["mxb", "ident"], writes=["psT"])
    p.op("dve", lambda e: e.reduce_max(mr[0:1, 16:17], b["psT"][0:1, 0:nq],
                                       axis=AX),
         reads=["psT"], writes=["mrow"])
    p.op("dve", lambda e: e.reduce_max(mr[0:1, 17:18], b["psT"][0:1, 8:16],
                                       axis=AX),
         reads=["psT"], writes=["mrow"])
    p.op("dve", lambda e: e.tensor_tensor(mr[0:1, 18:19], mr[0:1, 16:17],
                                          mr[0:1, 17:18], ALU.mult),
         reads=["mrow"], writes=["mrow"])
    p.op("act", lambda e: e.activation(mr[0:1, 19:20], mr[0:1, 18:19],
                                       AF.Ln, bias=1e-30),
         reads=["mrow", "konst"], writes=["mrow"])
    p.op("act", lambda e: e.activation(mr[0:1, 20:21], mr[0:1, 19:20],
                                       AF.Exp, scale=0.5),
         reads=["mrow"], writes=["mrow"])
    p.op("dve", lambda e: e.tensor_scalar(mr[0:1, 21:22], mr[0:1, 20:21],
                                          -1.0, CEXP, ALU.mult, ALU.add),
         reads=["mrow"], writes=["mrow"])
    p.op("dve", lambda e: e.tensor_copy(b["cmsrc"][0:1, 0:1],
                                        mr[0:1, 21:22]),
         reads=["mrow"], writes=["cmsrc"])
    p.op("pe", lambda e: e.matmul(b["psD"][:, 200:201], b["onesq"][:, 0:128],
                                  b["cmsrc"][:, 0:1], start=True,
                                  stop=True),
         reads=["cmsrc", "onesq"], writes=["psD"])
    p.op("dve", lambda e: e.tensor_copy(b["cm"][:], b["psD"][:, 200:201]),
         reads=["psD"], writes=["cm"])


def valid_kts(layer, q0, qw, slope):
    if layer == 1:
        return list(range(NT))
    kts = []
    for kt in range(NT):
        ktb = kt * 128
        if ktb >= q0 + qw:
            continue
        if slope * (q0 - ktb - 127) >= SKIP_GAP:
            continue
        kts.append(kt)
    return kts


def emit_attn_head(p, b, layer, h, Tq, oT_cols):
    slope = SLOPES[h] if layer == 0 else 0.0
    nch = Tq // QW
    psn = {0: "psA", 1: "psB"}
    ptn = {0: "pt0", 1: "pt1"}
    for cp in range(nch // 2):
        q0s = [(2 * cp) * QW, (2 * cp + 1) * QW]
        ktss = [valid_kts(layer, q0, QW, slope) for q0 in q0s]
        L = max(len(k) for k in ktss)

        def s_mm(s, n):
            kt = ktss[s][n]
            key = psn[s] + str(n % 2)
            p.op("pe", lambda e, kt=kt, s=s, n=n: e.matmul(
                b[psn[s]][:, (n % 2) * 512:(n % 2) * 512 + 512],
                b["kb128"][:, kt * 128:(kt + 1) * 128],
                b["qb128"][:, q0s[s]:q0s[s] + QW],
                start=True, stop=True),
                reads=["kb", "qb"], writes=[key])

        def do_exp(s, n):
            kt = ktss[s][n]
            ktb = kt * 128
            pskey = psn[s] + str(n % 2)
            pb = ptn[s]
            ptkey = pb + str(n % 2)
            ptv = b[pb][:, (n % 2) * 512:(n % 2) * 512 + 512]
            p.op("act", lambda e, pskey=pskey, ptv=ptv, s=s, n=n:
                 e.activation(ptv, b[psn[s]][:, (n % 2) * 512:
                                             (n % 2) * 512 + 512],
                              AF.Exp, bias=b["cm"][:]),
                 reads=[pskey, "cm"], writes=[ptkey])
            if layer == 0:
                off = GOFF + q0s[s] - ktb
                p.op("dve", lambda e, ptv=ptv, off=off: e.tensor_tensor(
                    ptv, ptv, b["gt"][:, off:off + QW], ALU.mult),
                    reads=[ptkey, "gt"], writes=[ptkey])

        def do_av(s, n):
            kt = ktss[s][n]
            ptkey = ptn[s] + str(n % 2)
            first, last = (n == 0), (n == len(ktss[s]) - 1)
            p.op("pe", lambda e, kt=kt, s=s, n=n, first=first, last=last:
                 e.matmul(
                     b["psC"][0:65, s * 512:s * 512 + 512],
                     b["vb"][:, kt * 260 + (h % 4) * 65:
                             kt * 260 + (h % 4) * 65 + 65],
                     b[ptn[s]][:, (n % 2) * 512:(n % 2) * 512 + 512],
                     start=first, stop=last),
                 reads=[ptkey, "vb"], writes=["psC%d" % s])

        for n in range(L + 1):
            for s in range(2):
                if n < len(ktss[s]):
                    s_mm(s, n)
                if 1 <= n <= len(ktss[s]):
                    do_av(s, n - 1)
                if n < len(ktss[s]):
                    do_exp(s, n)
        ch, sub = h // 2, (h % 2) * 64
        for s in range(2):
            rbv = b["rbA"] if s == 0 else b["rbB"]
            rsk = "rs%d" % s
            p.op("dve", lambda e, s=s: e.tensor_copy(
                b["rsum"][64:65, s * 512:(s + 1) * 512],
                b["psC"][64:65, s * 512:s * 512 + 512]),
                reads=["psC%d" % s], writes=[rsk])
            p.op("pe", lambda e, s=s: e.matmul(
                b["psD"][:, 0:512], b["onesq"][:, 0:128],
                b["rsum"][:, s * 512:(s + 1) * 512], start=True, stop=True),
                reads=[rsk, "onesq"], writes=["psD"])
            p.op("dve", lambda e, rbv=rbv: e.reciprocal(
                rbv[0:64, 0:512], b["psD"][0:64, 0:512]),
                reads=["psD"], writes=["rb%d" % s])
            obk = "ob%d" % s
            p.op("dve", lambda e, s=s, obk=obk, rbv=rbv: e.tensor_tensor(
                b[obk][0:64, 0:QW], b["psC"][0:64, s * 512:s * 512 + 512],
                rbv[0:64, 0:512], ALU.mult),
                reads=["psC%d" % s, "rb%d" % s], writes=[obk])
            dst = b["oT"][sub:sub + 64,
                          ch * oT_cols + q0s[s]: ch * oT_cols + q0s[s] + QW]
            p.op("sync", lambda e, dst=dst, obk=obk: e.dma_start(
                out=dst, in_=b[obk][0:64, 0:QW]),
                reads=[obk], writes=["oT"], dma=True)


def _attn_layer(p, b, layer, t, Tq, oT_cols, q_src_key):
    wqk = t["wqk0"] if layer == 0 else t["wqk1"]
    wv = t["wv0"] if layer == 0 else t["wv1"]
    has_bias = (layer == 0)
    bv = t["bv0"] if layer == 0 else None
    for hg in range(4):
        emit_v_group(p, b, hg, wv, bv)
        for hh in range(4):
            h = hg * 4 + hh
            if layer == 0:
                p.op("pool", lambda e, h=h: e.dma_start(
                    out=b["gt"], in_=t["gtab"][h * 128:(h + 1) * 128, :]),
                    writes=["gt"], dma=True)
            emit_qk_head(p, b, h, wqk, has_bias, Tq, q_src_key=q_src_key)
            emit_head_M(p, b, Tq)
            emit_attn_head(p, b, layer, h, Tq, oT_cols)


def _top2_softmax(p, b, sigmoid):
    st = b["stats"]
    p.op("dve", lambda e: e.tensor_copy(b["gbuf"][:, 0:8], b["psD"][:, 0:8]),
         reads=["psD"], writes=["gbuf"])
    p.op("dve", lambda e: e.reduce_max(st[:, 8:9], b["gbuf"][:, 0:8],
                                       axis=AX),
         reads=["gbuf"], writes=["m1"])
    p.op("dve", lambda e: e.tensor_scalar(b["eq1"][:, 0:8], b["gbuf"][:, 0:8],
                                          st[:, 8:9], None, ALU.is_equal),
         reads=["gbuf", "m1"], writes=["eq1"])
    p.op("dve", lambda e: e.scalar_tensor_tensor(
        b["eq2"][:, 0:8], b["eq1"][:, 0:8], -1e30, b["gbuf"][:, 0:8],
        ALU.mult, ALU.add), reads=["eq1", "gbuf"], writes=["eq2"])
    p.op("dve", lambda e: e.reduce_max(st[:, 9:10], b["eq2"][:, 0:8],
                                       axis=AX),
         reads=["eq2"], writes=["m2"])
    p.op("dve", lambda e: e.tensor_scalar(b["eq2"][:, 0:8], b["gbuf"][:, 0:8],
                                          st[:, 9:10], None, ALU.is_equal),
         reads=["gbuf", "m2"], writes=["eq2"])
    if not sigmoid:
        p.op("dve", lambda e: e.tensor_sub(st[:, 10:11], st[:, 9:10],
                                           st[:, 8:9]),
             reads=["m1", "m2"], writes=["d21"])
        p.op("act", lambda e: e.activation(st[:, 11:12], st[:, 10:11],
                                           AF.Exp),
             reads=["d21"], writes=["e21"])
        p.op("dve", lambda e: e.tensor_scalar_add(st[:, 12:13],
                                                  st[:, 11:12], 1.0),
             reads=["e21"], writes=["s21"])
        p.op("dve", lambda e: e.reciprocal(st[:, 13:14], st[:, 12:13]),
             reads=["s21"], writes=["wA"])
        p.op("dve", lambda e: e.tensor_tensor(st[:, 14:15], st[:, 11:12],
                                              st[:, 13:14], ALU.mult),
             reads=["e21", "wA"], writes=["wB"])
        wa, wb_ = 13, 14
    else:
        for k, src, kk in ((10, 8, "m1"), (11, 9, "m2")):
            p.op("act", lambda e, k=k, src=src: e.activation(
                st[:, k:k + 1], st[:, src:src + 1], AF.Exp, scale=-1.0),
                reads=[kk], writes=["sg%d" % k])
            p.op("dve", lambda e, k=k: e.tensor_scalar_add(
                st[:, k:k + 1], st[:, k:k + 1], 1.0),
                reads=["sg%d" % k], writes=["sg%d" % k])
            p.op("dve", lambda e, k=k: e.reciprocal(st[:, k + 2:k + 3],
                                                    st[:, k:k + 1]),
                 reads=["sg%d" % k], writes=["v%d" % k])
        p.op("dve", lambda e: e.tensor_add(st[:, 14:15], st[:, 12:13],
                                           st[:, 13:14]),
             reads=["v10", "v11"], writes=["vs"])
        p.op("dve", lambda e: e.reciprocal(st[:, 15:16], st[:, 14:15]),
             reads=["vs"], writes=["vsr"])
        p.op("dve", lambda e: e.tensor_scalar_mul(st[:, 15:16],
                                                  st[:, 15:16], 1.0 / 3.0),
             reads=["vsr"], writes=["vsr"])
        p.op("dve", lambda e: e.tensor_tensor(st[:, 10:11], st[:, 12:13],
                                              st[:, 15:16], ALU.mult),
             reads=["v10", "vsr"], writes=["wA"])
        p.op("dve", lambda e: e.tensor_tensor(st[:, 11:12], st[:, 13:14],
                                              st[:, 15:16], ALU.mult),
             reads=["v11", "vsr"], writes=["wB"])
        wa, wb_ = 10, 11
    p.op("dve", lambda e, wa=wa: e.tensor_scalar(
        b["cw"][:, 0:8], b["eq1"][:, 0:8], st[:, wa:wa + 1], None, ALU.mult),
        reads=["eq1", "wA"], writes=["cw"])
    p.op("dve", lambda e, wb_=wb_: e.scalar_tensor_tensor(
        b["cw"][:, 0:8], b["eq2"][:, 0:8], st[:, wb_:wb_ + 1],
        b["cw"][:, 0:8], ALU.mult, ALU.add),
        reads=["eq2", "wB", "cw"], writes=["cw"])


def emit_gate0_logits(p, b, i):
    for ch in range(ND):
        p.op("pe", lambda e, ch=ch: e.matmul(
            b["psD"][:, 0:8],
            b["xnT"][:, ch * T + i * 128: ch * T + i * 128 + 128],
            b["wg0b"][:, ch * 8:(ch + 1) * 8],
            start=(ch == 0), stop=(ch == ND - 1)),
            reads=["wg0b", "xnT"], writes=["psD"])


def emit_gate0_rest(p, b, i):
    _top2_softmax(p, b, sigmoid=False)
    for ch in range(ND):
        p.op("pe", lambda e, ch=ch: e.matmul(
            b["psB"][:, 0:128],
            b["xnT"][:, ch * T + i * 128: ch * T + i * 128 + 128],
            b["a0b"][:, ch * 128:(ch + 1) * 128],
            start=(ch == 0), stop=(ch == ND - 1)),
            reads=["a0b", "xnT"], writes=["psB0"])
    cw8 = b["cw"][:, 0:8]
    cwb = bass.AP(cw8.tensor, cw8.offset,
                  [list(d) for d in cw8.ap] + [[0, 16]])
    p.op("dve", lambda e: e.tensor_tensor(
        b["hw0"].rearrange("p (g r) -> p g r", g=8),
        b["psB"][:, 0:128].rearrange("p (g r) -> p g r", g=8), cwb,
        ALU.mult),
        reads=["psB0", "cw"], writes=["hw0"])
    p.op("pe", lambda e: e.transpose(b["psT"][:, 0:128], b["hw0"],
                                     b["ident"][:]),
         reads=["hw0", "ident"], writes=["psT"])
    p.op("dve", lambda e: e.tensor_copy(b["hw0t"], b["psT"][:, 0:128]),
         reads=["psT"], writes=["hw0t"])
    for half in range(2):
        p.op("pe", lambda e, half=half: e.matmul(
            b["psC"][:, half * 512:(half + 1) * 512], b["hw0t"],
            b["b0b"][:, half * 512:(half + 1) * 512],
            start=False, stop=False), reads=["hw0t", "b0b"],
            writes=["psC%d" % half])


def load_asm0_weights(p, b, t):
    p.op("pool", lambda e: e.memset(b["zrow"][0:1, :], 0.0), writes=["zrow"])
    p.op("pool", lambda e: e.dma_start(
        out=b["wob"].rearrange("p (ch c) -> p ch c", ch=ND),
        in_=t["wo0"].rearrange("(ch p) c -> p ch c", p=128)),
        writes=["wob"], dma=True)
    p.op("pool", lambda e: e.dma_start(
        out=b["wg0b"].rearrange("p (ch c) -> p ch c", ch=ND),
        in_=t["wg0"].rearrange("(ch p) c -> p ch c", p=128)),
        writes=["wg0b"], dma=True)
    p.op("pool", lambda e: e.dma_start(
        out=b["a0b"].rearrange("p (ch c) -> p ch c", ch=ND),
        in_=t["a0"].rearrange("(ch p) c -> p ch c", p=128)),
        writes=["a0b"], dma=True)
    p.op("pool", lambda e: e.dma_start(out=b["b0b"], in_=t["b0"]),
         writes=["b0b"], dma=True)
    p.op("pool", lambda e: e.dma_start(out=b["bpjb"][0:1, :], in_=t["bproj"]),
         writes=["bpjb"], dma=True)


def emit_assembly_l0(p, b, x0_dram, ln1, x1_dram, x1_out):
    for i in range(NT):
        psy = b["psC"]
        emit_gate0_logits(p, b, i)
        for half in range(2):
            p.op("pe", lambda e, half=half: e.matmul(
                psy[:, half * 512:(half + 1) * 512], b["onesr"][0:1, :],
                b["bpjb"][0:1, half * 512:(half + 1) * 512],
                start=True, stop=False),
                reads=["bpjb", "onesr"], writes=["psC%d" % half])
        for ch in range(ND):
            for half in range(2):
                p.op("pe", lambda e, ch=ch, i=i, half=half: e.matmul(
                    psy[:, half * 512:(half + 1) * 512],
                    b["oT"][:, ch * T + i * 128: ch * T + i * 128 + 128],
                    b["wob"][:, ch * 1024 + half * 512:
                             ch * 1024 + (half + 1) * 512],
                    start=False, stop=False), reads=["oT", "wob"],
                    writes=["psC%d" % half])
        emit_gate0_rest(p, b, i)
        for half in range(2):
            p.op("pe", lambda e, half=half: e.matmul(
                psy[:, half * 512:(half + 1) * 512], b["onesr"][0:1, :],
                b["zrow"][0:1, 0:512],
                start=False, stop=True),
                reads=["zrow", "onesr"], writes=["psC%d" % half])
        par = i % 2
        xt = b["xt0"] if par == 0 else b["xt1"]
        xk = "xt%d" % par
        p.op("sync", lambda e, i=i, xt=xt: e.dma_start(
            out=xt[:], in_=x0_dram[i * 128:(i + 1) * 128, :]),
            writes=[xk], dma=True)
        p.op("dve", lambda e, xt=xt: e.scalar_tensor_tensor(
            xt[:], psy[:], 1.0, xt[:], ALU.mult, ALU.add),
            reads=["psC0", "psC1"], writes=[xk])
        if x1_dram is not None:
            p.op("sync", lambda e, i=i, xt=xt: e.dma_start(
                out=x1_dram[i * 128:(i + 1) * 128, :], in_=xt[:]),
                reads=[xk], writes=["x1scr"], dma=True)
        if x1_out is not None:
            p.op("sync", lambda e, i=i, xt=xt: e.dma_start(
                out=x1_out[i * 128:(i + 1) * 128, :], in_=xt[:]),
                reads=[xk], writes=["x1out"], dma=True)
        if ln1:
            _ln_tile(p, b, xt, xk, i, b["xnT"], "xnT", T)


def yacc_slice(b, i):
    return b["xnT"][:, i * 2048:(i + 1) * 2048].bitcast(F32)


def emit_gating1_proj1(p, b, t):
    p.op("pool", lambda e: e.dma_start(
        out=b["wg1b"].rearrange("p (ch c) -> p ch c", ch=ND),
        in_=t["wg1"].rearrange("(ch p) c -> p ch c", p=128)),
        writes=["wg1b"], dma=True)
    p.op("pool", lambda e: e.dma_start(
        out=b["wob"].rearrange("p (ch c) -> p ch c", ch=ND),
        in_=t["wo1"].rearrange("(ch p) c -> p ch c", p=128)),
        writes=["wob"], dma=True)
    sh = b["wcols"].rearrange("p (i e) -> p i e", i=NTO)
    p.op("pool", lambda e: e.memset(sh[:, :, 0:1], 1.0 / 3.0),
         writes=["wcols"])
    for i in range(NTO):
        for ch in range(ND):
            p.op("pe", lambda e, ch=ch, i=i: e.matmul(
                b["psD"][:, 0:8],
                b["ownT"][:, ch * TO + i * 128: ch * TO + i * 128 + 128],
                b["wg1b"][:, ch * 8:(ch + 1) * 8],
                start=(ch == 0), stop=(ch == ND - 1)),
                reads=["wg1b", "ownT"], writes=["psD"])
        for ch in range(ND):
            for half in range(2):
                p.op("pe", lambda e, ch=ch, i=i, half=half: e.matmul(
                    b["psC"][:, half * 512:(half + 1) * 512],
                    b["oT"][:, ch * TO + i * 128: ch * TO + i * 128 + 128],
                    b["wob"][:, ch * 1024 + half * 512:
                             ch * 1024 + (half + 1) * 512],
                    start=(ch == 0), stop=(ch == ND - 1)),
                    reads=["oT", "wob"], writes=["psC%d" % half])
        _top2_softmax(p, b, sigmoid=True)
        p.op("dve", lambda e, i=i: e.tensor_copy(
            b["wcols"][:, i * NE + 1: i * NE + 9], b["cw"][:, 0:8]),
            reads=["cw"], writes=["wcols"])
        p.op("dve", lambda e, i=i: e.scalar_tensor_tensor(
            yacc_slice(b, i), b["psC"][:], 1.0, yacc_slice(b, i),
            ALU.mult, ALU.add),
            reads=["psC0", "psC1", "yacc"], writes=["yacc"])


def emit_ffn1(p, b, we13_dram, we2_dram):
    dq = []

    def pop_d():
        if dq:
            dq.pop(0)()

    for e_i in range(NE):
        for fq in range(4):
            par = (e_i * 4 + fq) % 2
            w13k, w2k = "w13_%d" % par, "w2_%d" % par
            w13 = b["w13b0"] if par == 0 else b["w13b1"]
            w2 = b["w2b0"] if par == 0 else b["w2b1"]
            p.op("pool", lambda e, e_i=e_i, fq=fq, w13=w13: e.dma_start(
                out=w13.rearrange("p (ch c) -> p ch c", ch=ND),
                in_=we13_dram[e_i][:, fq * 1024:(fq + 1) * 1024].rearrange(
                    "(ch p) c -> p ch c", p=128)),
                writes=[w13k], dma=True)
            p.op("pool", lambda e, e_i=e_i, fq=fq, w2=w2: e.dma_start(
                out=w2.rearrange("p (ch c) -> p ch c", ch=4),
                in_=we2_dram[e_i][fq * 512:(fq + 1) * 512, :].rearrange(
                    "(ch p) c -> p ch c", p=128)),
                writes=[w2k], dma=True)
            for tc in range(4):
                gpar = (e_i * 16 + fq * 4 + tc) % 2
                hw = b["hwT0"] if gpar == 0 else b["hwT1"]
                hwk = "hwT%d" % gpar
                for j in range(4):
                    up = j % 2
                    uk1, uk3 = "psA%d" % up, "psB%d" % up
                    for ch in range(ND):
                        p.op("pe", lambda e, ch=ch, j=j, tc=tc, up=up,
                             w13=w13: e.matmul(
                                 b["psA"][:, up * 512:up * 512 + 512],
                                 w13[:, ch * 1024 + j * 128:
                                     ch * 1024 + j * 128 + 128],
                                 b["ownT"][:, ch * TO + tc * 512:
                                           ch * TO + tc * 512 + 512],
                                 start=(ch == 0), stop=(ch == ND - 1)),
                             reads=[w13k, "ownT"], writes=[uk1])
                    for ch in range(ND):
                        p.op("pe", lambda e, ch=ch, j=j, tc=tc, up=up,
                             w13=w13: e.matmul(
                                 b["psB"][:, up * 512:up * 512 + 512],
                                 w13[:, ch * 1024 + 512 + j * 128:
                                     ch * 1024 + 512 + j * 128 + 128],
                                 b["ownT"][:, ch * TO + tc * 512:
                                           ch * TO + tc * 512 + 512],
                                 start=(ch == 0), stop=(ch == ND - 1)),
                             reads=[w13k, "ownT"], writes=[uk3])
                    p.op("act", lambda e, up=up, j=j: e.activation(
                        b["s1b"][:, up * 512:up * 512 + 512],
                        b["psA"][:, up * 512:up * 512 + 512], AF.Silu),
                        reads=[uk1], writes=["s1b%d" % up])
                    p.op("dve", lambda e, up=up, j=j, hw=hw: e.tensor_tensor(
                        hw[:, j * 512:(j + 1) * 512],
                        b["s1b"][:, up * 512:up * 512 + 512],
                        b["psB"][:, up * 512:up * 512 + 512], ALU.mult),
                        reads=[uk3, "s1b%d" % up], writes=[hwk])
                    pop_d()

                def mk_down(e_i, tc, hw, hwk, w2, w2k):
                    def gen(il):
                        def f():
                            i = tc * 4 + il
                            for j in range(4):
                                for half in range(2):
                                    p.op("pe", lambda e, j=j, half=half,
                                         il=il, hw=hw, w2=w2: e.matmul(
                                             b["psC"][:, half * 512:
                                                      half * 512 + 512],
                                             hw[:, j * 512 + il * 128:
                                                j * 512 + il * 128 + 128],
                                             w2[:, j * 1024 + half * 512:
                                                j * 1024 + half * 512 + 512],
                                             start=(j == 0), stop=(j == 3)),
                                         reads=[hwk, w2k],
                                         writes=["psC%d" % half])
                            wcol = b["wcols"][:, i * NE + e_i:
                                              i * NE + e_i + 1]
                            p.op("dve", lambda e, i=i, wcol=wcol:
                                 e.scalar_tensor_tensor(
                                     yacc_slice(b, i), b["psC"][:], wcol,
                                     yacc_slice(b, i), ALU.mult, ALU.add),
                                 reads=["psC0", "psC1", "wcols", "yacc"],
                                 writes=["yacc"])
                        return f
                    return [gen(il) for il in range(4)]
                dq.extend(mk_down(e_i, tc, hw, hwk, w2, w2k))
    while dq:
        pop_d()


def emit_assembly_l1(p, b, x1own_src, out_dram):
    for i in range(NTO):
        par = i % 2
        xt = b["xt0"] if par == 0 else b["xt1"]
        xk = "xt%d" % par
        if x1own_src[0] == "dram":
            p.op("sync", lambda e, i=i, xt=xt: e.dma_start(
                out=xt[:], in_=x1own_src[1][i * 128:(i + 1) * 128, :]),
                writes=[xk], dma=True)
        else:
            idxs, x1d = x1own_src[1], x1own_src[2]
            p.op("pool", lambda e, i=i, xt=xt: e.indirect_dma_start(
                out=xt[:], out_offset=None, in_=x1d,
                in_offset=IndirectOffsetOnAxis(idxs[:, i:i + 1], 0)),
                reads=["idxX", "x1scr"], writes=[xk], dma=True)
        p.op("dve", lambda e, i=i, xt=xt: e.scalar_tensor_tensor(
            xt[:], yacc_slice(b, i), 1.0, xt[:], ALU.mult, ALU.add),
            reads=["yacc"], writes=[xk])
        p.op("sync", lambda e, i=i, xt=xt: e.dma_start(
            out=out_dram[i * 128:(i + 1) * 128, :], in_=xt[:]),
            reads=[xk], writes=["x2out"], dma=True)


def build_l0(x1_as_output):
    nc = bass.Bass("TRN2", target_bir_lowering=False)
    t = {}
    t["x0"] = nc.dram_tensor("x0", [T, D], F32, kind="ExternalInput").ap()
    t["wqk0"] = nc.dram_tensor("wqk0", [D, H * 128], F32,
                               kind="ExternalInput").ap()
    t["wv0"] = nc.dram_tensor("wv0", [D, H * 64], F32,
                              kind="ExternalInput").ap()
    t["bqk0"] = nc.dram_tensor("bqk0", [128, 16], F32,
                               kind="ExternalInput").ap()
    t["bv0"] = nc.dram_tensor("bv0", [16, 64], F32,
                              kind="ExternalInput").ap()
    t["wo0"] = nc.dram_tensor("wo0", [D, D], F32, kind="ExternalInput").ap()
    t["bproj"] = nc.dram_tensor("bproj", [1, D], F32,
                                kind="ExternalInput").ap()
    t["wg0"] = nc.dram_tensor("wg0", [D, 8], F32, kind="ExternalInput").ap()
    t["a0"] = nc.dram_tensor("a0", [D, 128], F32, kind="ExternalInput").ap()
    t["b0"] = nc.dram_tensor("b0", [128, D], F32, kind="ExternalInput").ap()
    t["gtab"] = nc.dram_tensor("gtab", [H * 128, GW], BF16,
                               kind="ExternalInput").ap()
    t["msel"] = nc.dram_tensor("msel", [128, 1040], BF16,
                               kind="ExternalInput").ap()
    x1_out = nc.dram_tensor("x1", [T, D], F32, kind="ExternalOutput").ap() \
        if x1_as_output else None

    with ExitStack() as st:
        b = alloc_bufs(nc, st, "l0")
        p = Prog(nc, st)
        emit_consts(p, b)
        p.op("sync", lambda e: e.dma_start(out=b["bqk"][:], in_=t["bqk0"]),
             writes=["bqk"], dma=True)
        p.op("sync", lambda e: e.dma_start(out=b["mselb"][:], in_=t["msel"]),
             writes=["mselb"], dma=True)
        emit_lnT(p, b, t["x0"], NT, b["xnT"], "xnT", T)
        p.barrier(["xnb", "sq1k"])
        emit_vb_ones(p, b)
        _attn_layer(p, b, 0, t, T, T, None)
        p.barrier(ATTN_KEYS)
        load_asm0_weights(p, b, t)
        emit_assembly_l0(p, b, t["x0"], ln1=False, x1_dram=None,
                         x1_out=x1_out)
        p.finish()
    return nc


def build_l1():
    nc = bass.Bass("TRN2", target_bir_lowering=False)
    t = {}
    t["x1f"] = nc.dram_tensor("x1f", [T, D], F32, kind="ExternalInput").ap()
    t["x1o"] = nc.dram_tensor("x1o", [TO, D], F32, kind="ExternalInput").ap()
    t["wqk1"] = nc.dram_tensor("wqk1", [D, H * 128], F32,
                               kind="ExternalInput").ap()
    t["wv1"] = nc.dram_tensor("wv1", [D, H * 64], F32,
                              kind="ExternalInput").ap()
    t["wo1"] = nc.dram_tensor("wo1", [D, D], F32, kind="ExternalInput").ap()
    t["wg1"] = nc.dram_tensor("wg1", [D, 8], F32, kind="ExternalInput").ap()
    we13 = [nc.dram_tensor("we13_%d" % i, [D, 2 * FF], F32,
                           kind="ExternalInput").ap() for i in range(NE)]
    we2 = [nc.dram_tensor("we2_%d" % i, [FF, D], F32,
                          kind="ExternalInput").ap() for i in range(NE)]
    t["msel"] = nc.dram_tensor("msel", [128, 1040], BF16,
                               kind="ExternalInput").ap()
    out = nc.dram_tensor("x2", [TO, D], F32, kind="ExternalOutput").ap()

    with ExitStack() as st:
        b = alloc_bufs(nc, st, "l1")
        p = Prog(nc, st)
        emit_consts(p, b)
        p.op("sync", lambda e: e.dma_start(out=b["mselb"][:], in_=t["msel"]),
             writes=["mselb"], dma=True)
        emit_lnT(p, b, t["x1f"], NT, b["xnT"], "xnT", T)
        emit_lnT(p, b, t["x1o"], NTO, b["ownT"], "ownT", TO)
        p.barrier(["xnb", "sq1k"])
        emit_vb_ones(p, b)
        _attn_layer(p, b, 1, t, TO, TO, "ownT")
        p.barrier(ATTN_KEYS)
        p.op("pool", lambda e: e.memset(b["xnT"][:], 0.0),
             reads=["xnT"], writes=["xnT", "yacc"])
        emit_gating1_proj1(p, b, t)
        p.barrier(["wob"])
        emit_ffn1(p, b, we13, we2)
        emit_assembly_l1(p, b, ("dram", t["x1o"]), out)
        p.finish()
    return nc




ASM0_KEYS = ["wob", "a0b", "b0b", "hw0", "hw0t", "bpjb", "zrow", "wg0b",
             "xnb", "sq1k", "cw", "gbuf", "eq1", "eq2"]


def build_fused():
    nc = bass.Bass("TRN2", target_bir_lowering=False)
    t = {}
    t["x0"] = nc.dram_tensor("x0", [T, D], F32, kind="ExternalInput").ap()
    t["wqk0"] = nc.dram_tensor("wqk0", [D, H * 128], F32,
                               kind="ExternalInput").ap()
    t["wv0"] = nc.dram_tensor("wv0", [D, H * 64], F32,
                              kind="ExternalInput").ap()
    t["bqk0"] = nc.dram_tensor("bqk0", [128, 16], F32,
                               kind="ExternalInput").ap()
    t["bv0"] = nc.dram_tensor("bv0", [16, 64], F32,
                              kind="ExternalInput").ap()
    t["wo0"] = nc.dram_tensor("wo0", [D, D], F32, kind="ExternalInput").ap()
    t["bproj"] = nc.dram_tensor("bproj", [1, D], F32,
                                kind="ExternalInput").ap()
    t["wg0"] = nc.dram_tensor("wg0", [D, 8], F32, kind="ExternalInput").ap()
    t["a0"] = nc.dram_tensor("a0", [D, 128], F32, kind="ExternalInput").ap()
    t["b0"] = nc.dram_tensor("b0", [128, D], F32, kind="ExternalInput").ap()
    t["gtab"] = nc.dram_tensor("gtab", [H * 128, GW], BF16,
                               kind="ExternalInput").ap()
    t["wqk1"] = nc.dram_tensor("wqk1", [D, H * 128], F32,
                               kind="ExternalInput").ap()
    t["wv1"] = nc.dram_tensor("wv1", [D, H * 64], F32,
                              kind="ExternalInput").ap()
    t["wo1"] = nc.dram_tensor("wo1", [D, D], F32, kind="ExternalInput").ap()
    t["wg1"] = nc.dram_tensor("wg1", [D, 8], F32, kind="ExternalInput").ap()
    we13 = [nc.dram_tensor("we13_%d" % i, [D, 2 * FF], F32,
                           kind="ExternalInput").ap() for i in range(NE)]
    we2 = [nc.dram_tensor("we2_%d" % i, [FF, D], F32,
                          kind="ExternalInput").ap() for i in range(NE)]
    t["msel"] = nc.dram_tensor("msel", [128, 1040], BF16,
                               kind="ExternalInput").ap()
    idxT = nc.dram_tensor("idxT", [128, 8], I32, kind="ExternalInput").ap()
    idxX = nc.dram_tensor("idxX", [128, 16], I32, kind="ExternalInput").ap()
    x1scr = nc.dram_tensor("x1scr", [T, D], F32, kind="Internal").ap()
    xn1scr = nc.dram_tensor("xn1scr", [D, T], BF16, kind="Internal").ap()
    out = nc.dram_tensor("x2", [TO, D], F32, kind="ExternalOutput").ap()

    with ExitStack() as st:
        b = alloc_bufs(nc, st, "fused")
        p = Prog(nc, st)
        emit_consts(p, b)
        p.op("sync", lambda e: e.dma_start(out=b["bqk"][:], in_=t["bqk0"]),
             writes=["bqk"], dma=True)
        p.op("sync", lambda e: e.dma_start(out=b["mselb"][:], in_=t["msel"]),
             writes=["mselb"], dma=True)
        p.op("sync", lambda e: e.dma_start(out=b["idxT"][:], in_=idxT),
             writes=["idxT"], dma=True)
        p.op("sync", lambda e: e.dma_start(out=b["idxX"][:], in_=idxX),
             writes=["idxX"], dma=True)
        emit_lnT(p, b, t["x0"], NT, b["xnT"], "xnT", T)
        p.barrier(["xnb", "sq1k"])
        emit_vb_ones(p, b)
        _attn_layer(p, b, 0, t, T, T, None)
        p.barrier(ATTN_KEYS)
        load_asm0_weights(p, b, t)
        emit_assembly_l0(p, b, t["x0"], ln1=True, x1_dram=x1scr, x1_out=None)
        for ch in range(ND):
            p.op("sync", lambda e, ch=ch: e.dma_start(
                out=xn1scr[ch * 128:(ch + 1) * 128, :],
                in_=b["xnT"][:, ch * T:(ch + 1) * T]),
                reads=["xnT"], writes=["xn1scr"], dma=True)
        p.barrier(ASM0_KEYS + ["oT"])
        xn1flat = bass.AP(xn1scr.tensor, 0, [[1, D * T], [1, 1]])
        for ch in range(ND):
            p.op("pool", lambda e, ch=ch: e.indirect_dma_start(
                out=b["ownT"][:, ch * TO:(ch + 1) * TO], out_offset=None,
                in_=xn1flat,
                in_offset=IndirectOffsetOnAxis(b["idxT"][:, ch:ch + 1], 0)),
                reads=["xn1scr", "idxT"], writes=["ownT"], dma=True)
        emit_vb_ones(p, b)
        _attn_layer(p, b, 1, t, TO, TO, "ownT")
        p.barrier(ATTN_KEYS)
        p.op("pool", lambda e: e.memset(b["xnT"][:], 0.0),
             reads=["xnT"], writes=["xnT", "yacc"])
        emit_gating1_proj1(p, b, t)
        p.barrier(["wob", "cw", "gbuf", "eq1", "eq2"])
        emit_ffn1(p, b, we13, we2)
        emit_assembly_l1(p, b, ("gather", b["idxX"], x1scr), out)
        p.finish()
    return nc



def _prep_shared(w):
    g = {}
    wa = np.asarray(w["L0_attn_w_attn"], np.float32)
    qk = np.empty((D, H * 128), np.float32)
    for h in range(H):
        qk[:, h * 128:h * 128 + 64] = wa[:, D + h * 64:D + (h + 1) * 64]
        qk[:, h * 128 + 64:h * 128 + 128] = wa[:, h * 64:(h + 1) * 64]
    g["wqk0"] = qk
    g["wv0"] = np.ascontiguousarray(wa[:, 2 * D:3 * D])
    ba = np.asarray(w["L0_attn_b_attn"], np.float32)
    bqk = np.empty((128, 16), np.float32)
    for h in range(H):
        bqk[0:64, h] = ba[D + h * 64:D + (h + 1) * 64]
        bqk[64:128, h] = ba[h * 64:(h + 1) * 64]
    g["bqk0"] = bqk
    g["bv0"] = np.ascontiguousarray(ba[2 * D:].reshape(16, 64))
    g["wo0"] = np.ascontiguousarray(np.asarray(w["L0_attn_w_proj"],
                                               np.float32))
    g["bproj"] = np.ascontiguousarray(
        np.asarray(w["L0_attn_b_proj"], np.float32)[None, :])
    g["wg0"] = np.ascontiguousarray(np.asarray(w["L0_ffn_Wg"], np.float32))
    g["a0"] = np.ascontiguousarray(
        np.asarray(w["L0_ffn_A"], np.float32).transpose(1, 0, 2).reshape(
            D, 128))
    g["b0"] = np.ascontiguousarray(
        np.asarray(w["L0_ffn_B"], np.float32).reshape(128, D))
    wq = np.asarray(w["L1_attn_W_q"], np.float32)
    wk2 = np.asarray(w["L1_attn_W_k"], np.float32)
    wv2 = np.asarray(w["L1_attn_W_v"], np.float32)
    qk1 = np.empty((D, H * 128), np.float32)
    for h in range(H):
        qk1[:, h * 128:h * 128 + 64] = wk2[h]
        qk1[:, h * 128 + 64:h * 128 + 128] = wq[h]
    g["wqk1"] = qk1
    g["wv1"] = np.ascontiguousarray(wv2.transpose(1, 0, 2).reshape(D, H * 64))
    g["wo1"] = np.ascontiguousarray(np.asarray(w["L1_attn_W_o"], np.float32))
    g["wg1"] = np.ascontiguousarray(np.asarray(w["L1_ffn_Wg"], np.float32))
    s1 = np.asarray(w["L1_ffn_Ws1"], np.float32)
    s3 = np.asarray(w["L1_ffn_Ws3"], np.float32)
    s2 = np.asarray(w["L1_ffn_Ws2"], np.float32)
    e1 = np.asarray(w["L1_ffn_We1"], np.float32)
    e3 = np.asarray(w["L1_ffn_We3"], np.float32)
    e2 = np.asarray(w["L1_ffn_We2"], np.float32)

    def inter(a, bm):
        out = np.empty((D, 2 * FF), np.float32)
        for fq in range(4):
            out[:, fq * 1024:fq * 1024 + 512] = a[:, fq * 512:(fq + 1) * 512]
            out[:, fq * 1024 + 512:(fq + 1) * 1024] = \
                bm[:, fq * 512:(fq + 1) * 512]
        return out
    for i in range(NE):
        if i == 0:
            g["we13_%d" % i] = inter(s1, s3)
            g["we2_%d" % i] = np.ascontiguousarray(s2)
        else:
            g["we13_%d" % i] = inter(e1[i - 1], e3[i - 1])
            g["we2_%d" % i] = np.ascontiguousarray(e2[i - 1])
    r = (np.arange(128, dtype=np.float64)[:, None]
         - (np.arange(GW, dtype=np.float64)[None, :] - GOFF))
    gtab = np.zeros((H * 128, GW), dtype=ml_dtypes.bfloat16)
    with np.errstate(under="ignore"):
        for h in range(H):
            gh = np.where(r <= 0, np.exp(SLOPES[h] * r), 0.0)
            gtab[h * 128:(h + 1) * 128] = gh.astype(ml_dtypes.bfloat16)
    g["gtab"] = gtab
    msel = np.zeros((128, 1040), dtype=ml_dtypes.bfloat16)
    for s in range(16):
        if s < 8:
            msel[64:128, s * 65 + s] = 1
        else:
            msel[0:64, s * 65 + s] = 1
    g["msel"] = msel
    return g


FUSED = True


def kernel(**inputs):
    g = _prep_shared(inputs)
    x0 = np.ascontiguousarray(np.asarray(inputs["L0_attn_x"], np.float32))
    if FUSED:
        return kernel_fused(inputs, g, x0)

    nc0 = build_l0(x1_as_output=True)
    l0_keys = ["wqk0", "wv0", "bqk0", "bv0", "wo0", "bproj", "wg0", "a0",
               "b0", "gtab", "msel"]
    in_maps0 = []
    for c in range(8):
        m = {"x0": x0[c // 2]}
        for k in l0_keys:
            m[k] = g[k]
        in_maps0.append(m)
    r0 = run_bass_kernel_spmd(nc0, in_maps0, core_ids=list(range(8)))
    x1 = [np.asarray(r0.results[2 * bb]["x1"]) for bb in range(B)]

    nc1 = build_l1()
    l1_keys = (["wqk1", "wv1", "wo1", "wg1", "msel"] +
               ["we13_%d" % i for i in range(NE)] +
               ["we2_%d" % i for i in range(NE)])
    in_maps1 = []
    for c in range(8):
        bb, hf = c // 2, c % 2
        m = {"x1f": x1[bb],
             "x1o": np.ascontiguousarray(x1[bb][hf * TO:(hf + 1) * TO])}
        for k in l1_keys:
            m[k] = g[k]
        in_maps1.append(m)
    r1 = run_bass_kernel_spmd(nc1, in_maps1, core_ids=list(range(8)))

    out = np.empty((B, T, D), np.float32)
    for c in range(8):
        bb, hf = c // 2, c % 2
        out[bb, hf * TO:(hf + 1) * TO] = np.asarray(r1.results[c]["x2"])
    return out


def kernel_fused(inputs, g, x0):
    nc = build_fused()
    keys = (["wqk0", "wv0", "bqk0", "bv0", "wo0", "bproj", "wg0", "a0", "b0",
             "gtab", "msel", "wqk1", "wv1", "wo1", "wg1"] +
            ["we13_%d" % i for i in range(NE)] +
            ["we2_%d" % i for i in range(NE)])
    pidx = np.arange(128)
    in_maps = []
    for c in range(8):
        bb, hf = c // 2, c % 2
        own0 = hf * TO
        idxT_np = np.empty((128, 8), np.int32)
        for ch in range(ND):
            idxT_np[:, ch] = (ch * 128 + pidx) * T + own0
        idxX_np = np.empty((128, 16), np.int32)
        for i in range(NTO):
            idxX_np[:, i] = own0 + i * 128 + pidx
        m = {"x0": x0[bb], "idxT": idxT_np, "idxX": idxX_np}
        for k in keys:
            m[k] = g[k]
        in_maps.append(m)
    r = run_bass_kernel_spmd(nc, in_maps, core_ids=list(range(8)))
    out = np.empty((B, T, D), np.float32)
    for c in range(8):
        bb, hf = c // 2, c % 2
        out[bb, hf * TO:(hf + 1) * TO] = np.asarray(r.results[c]["x2"])
    return out


# version 18
# speedup vs baseline: 1.6820x; 1.0389x over previous
import numpy as np
import ml_dtypes
from contextlib import ExitStack

import concourse.bass as bass
import concourse.mybir as mybir
from concourse.bass import IndirectOffsetOnAxis
from concourse.bass_utils import run_bass_kernel_spmd

F32 = mybir.dt.float32
BF16 = mybir.dt.bfloat16
I32 = mybir.dt.int32
AF = mybir.ActivationFunctionType
ALU = mybir.AluOpType
AX = mybir.AxisListType.X

B, T, D, H, DH = 4, 4096, 1024, 16, 64
TO = T // 2
NT, NTO = T // 128, TO // 128
ND = D // 128
CEXP = 20.0
EPS = 1e-5
FF = 2048
NE = 9
SLOPES = [2.0 ** (-(8.0 / H) * (i + 1)) for i in range(H)]
GW = 4480
GOFF = 384
SKIP_GAP = 25.0
QW = 512
UCOLS = 33024


class Prog:
    def __init__(self, nc, st):
        self.nc = nc
        self.st = st
        self.e = {"pe": nc.tensor, "act": nc.scalar, "dve": nc.vector,
                  "pool": nc.gpsimd, "sync": nc.sync}
        self.sem = {}
        for sname in ("pe", "act", "dve", "pool"):
            self.sem[sname] = st.enter_context(nc.semaphore("sem_" + sname))
        self.cnt = {sname: 0 for sname in self.sem}
        self.waited = {sname: {} for sname in
                       ("pe", "act", "dve", "pool", "sync")}
        self.buf = {}

    def _dma_sem(self, key):
        name = "d_" + key
        if name not in self.sem:
            self.sem[name] = self.st.enter_context(
                self.nc.semaphore("sem_" + name))
            self.cnt[name] = 0
        return name

    def wait(self, stream, tok):
        if tok is None:
            return
        chan, v = tok
        if stream == "pe" and chan == "pe":
            return
        if v <= self.waited[stream].get(chan, 0):
            return
        self.e[stream].wait_ge(self.sem[chan], v)
        self.waited[stream][chan] = v

    def op(self, stream, fn, reads=(), writes=(), dma=False, deps=()):
        for t in deps:
            self.wait(stream, t)
        for k in reads:
            sb_ = self.buf.get(k)
            if sb_ and sb_["w"]:
                self.wait(stream, sb_["w"])
        for k in writes:
            sb_ = self.buf.get(k)
            if sb_:
                if sb_["w"]:
                    self.wait(stream, sb_["w"])
                for t in sb_["r"]:
                    self.wait(stream, t)
        ins = fn(self.e[stream])
        if dma:
            assert writes, "DMA ops must declare a write key"
            chan = self._dma_sem(writes[0])
            inc = 16
        else:
            chan = stream
            inc = 1
        ins.then_inc(self.sem[chan], inc)
        self.cnt[chan] += inc
        tok = (chan, self.cnt[chan])
        for k in reads:
            self.buf.setdefault(k, {"w": None, "r": []})["r"].append(tok)
        for k in writes:
            self.buf[k] = {"w": tok, "r": []}
        return tok

    def barrier(self, keys, streams=("pe", "act", "dve", "pool", "sync")):
        best = {}
        for k in keys:
            sb_ = self.buf.get(k)
            if not sb_:
                continue
            for t in ([sb_["w"]] if sb_["w"] else []) + sb_["r"]:
                c, v = t
                if best.get(c, 0) < v:
                    best[c] = v
        for stream in streams:
            for c, v in best.items():
                self.wait(stream, (c, v))

    def finish(self):
        for chan in list(self.cnt):
            if self.cnt[chan]:
                self.wait("sync", (chan, self.cnt[chan]))


PS_KEYS = ["psA0", "psA1", "psB0", "psB1", "psC0", "psC1", "psD", "psT"]
ATTN_KEYS = ["qb", "vb", "gt", "sqA", "sqB", "pt00", "pt01", "pt10",
             "pt11", "wqkb0", "wqkb1", "wvb", "bvb", "rs0", "rs1",
             "rb0", "rb1", "ob0", "ob1", "kb", "qst0", "qst1", "mrow",
             "mx0", "mxb", "pcol"] + PS_KEYS
ASM_KEYS = ["wob", "a0b", "b0b", "hw0", "hw0t", "bpjb", "zrow", "wg0b",
            "xnb", "sq1k", "cw", "gbuf", "eq1", "eq2", "w13_0", "w13_1",
            "w2_0", "w2_1", "s1b", "hwb", "hwtb", "wg1b", "wcols"]


def uv(b, part0, nparts, col0, ncols, dtype=BF16):
    u = b["U"][:]
    ap = bass.AP(u.tensor, part0 * UCOLS + col0, [[UCOLS, nparts], [1, ncols]])
    if dtype == F32:
        ap = ap.bitcast(F32)
    return ap


def alloc_bufs(nc, st, phase):
    b = {}

    def sb(name, shape, dt):
        b[name] = st.enter_context(nc.sbuf_tensor(name, shape, dt))
    sb("xnT", [128, ND * T], BF16)
    if phase == "l1":
        sb("ownT", [128, ND * TO], BF16)
        sb("oT", [128, ND * TO], BF16)
    else:
        sb("oT", [128, ND * T], BF16)
    if phase == "fused":
        ot = b["oT"][:]
        b["ownT"] = bass.AP(ot.tensor, ot.offset + ND * TO,
                            [[ot.ap[0][0], 128], [1, ND * TO]])
        sb("idxTs", [128, 8], I32)
        sb("idxXs", [128, 16], I32)
        b["idxT"], b["idxX"] = b["idxTs"], b["idxXs"]
    sb("U", [128, UCOLS], BF16)
    sb("xt0", [128, 1024], F32)
    sb("xt1", [128, 1024], F32)
    sb("stats", [128, 24], F32)
    sb("mx", [128, 4], F32)
    sb("mxb", [128, 1], BF16)
    sb("mrow", [1, 24], F32)
    sb("mrowb", [1, 1], BF16)
    sb("cm", [128, 1], F32)
    sb("ident", [128, 128], BF16)
    sb("ones128", [128, 1], BF16)
    sb("onesr", [65, 128], BF16)
    sb("bvr", [1, 256], BF16)
    sb("bqk", [128, 16], F32)
    sb("konst", [128, 2], F32)
    sb("onesq", [128, 128], BF16)
    sb("rsum", [128, 1024], BF16)
    sb("cmsrc", [128, 1], BF16)
    sb("mselb", [128, 1040], BF16)
    nc.const_aps.register(F32, EPS, b["konst"][:, 0:1])
    nc.const_aps.register(F32, 1e-30, b["konst"][:, 1:2])
    b["kb"] = uv(b, 0, 64, 0, 4096)
    b["kbU"] = uv(b, 64, 64, 0, 4096)
    b["kb128"] = uv(b, 0, 128, 0, 4096)
    b["qb"] = uv(b, 0, 64, 4096, 4096)
    b["qstage"] = uv(b, 64, 64, 4096, 4096)
    b["qb128"] = uv(b, 0, 128, 4096, 4096)
    b["vb"] = uv(b, 0, 128, 8192, 8320)
    b["gt"] = uv(b, 0, 128, 16512, GW)
    b["sq"] = uv(b, 0, 64, 21504, 1024)
    b["sqH"] = uv(b, 64, 64, 21504, 1024)
    b["sq128"] = uv(b, 0, 128, 21504, 1024)
    b["pt0"] = uv(b, 0, 128, 22528, 1024)
    b["pt1"] = uv(b, 0, 128, 23552, 1024)
    b["rbA"] = uv(b, 0, 64, 29952, 1024, F32)
    b["rbB"] = uv(b, 0, 64, 24576, 1024, F32)
    b["wqkb0"] = uv(b, 0, 128, 25600, 1024)
    b["wqkb1"] = uv(b, 0, 128, 26624, 1024)
    b["wvb"] = uv(b, 0, 128, 27648, 2048)
    b["bvb"] = uv(b, 0, 128, 29696, 256)
    b["rb"] = uv(b, 0, 64, 29952, 1024)
    b["ob0"] = uv(b, 0, 64, 30976, 1024)
    b["ob1"] = uv(b, 0, 64, 32000, 1024)
    if phase in ("l0", "fused"):
        b["wob"] = uv(b, 0, 128, 0, 8192)
        b["a0b"] = uv(b, 0, 128, 8192, 1024)
        b["b0b"] = uv(b, 0, 128, 9216, 1024)
        b["hw0"] = uv(b, 0, 128, 10240, 128)
        b["hw0t"] = uv(b, 0, 128, 10368, 128)
        b["bpjb"] = uv(b, 0, 1, 10496, 1024)
        b["bpjb128"] = uv(b, 0, 128, 10496, 1024)
        b["bpjbZ"] = uv(b, 1, 127, 10496, 1024)
        b["zrow"] = uv(b, 0, 1, 11520, 512)
        b["wg0b"] = uv(b, 0, 128, 12032, 64)
        b["xnb"] = uv(b, 0, 128, 12096, 1024)
        b["sq1k"] = uv(b, 0, 128, 13120, 1024)
        b["cw"] = uv(b, 0, 128, 14144, 32, F32)
        b["gbuf"] = uv(b, 0, 128, 14176, 32, F32)
        b["eq1"] = uv(b, 0, 128, 14208, 32, F32)
        b["eq2"] = uv(b, 0, 128, 14240, 32, F32)
    if phase in ("l1", "fused"):
        b["wob"] = uv(b, 0, 128, 0, 8192)
        b["w13b0"] = uv(b, 0, 128, 0, 8192)
        b["w13b1"] = uv(b, 0, 128, 8192, 8192)
        b["w2b0"] = uv(b, 0, 128, 16384, 4096)
        b["w2b1"] = uv(b, 0, 128, 20480, 4096)
        b["s1b"] = uv(b, 0, 128, 24576, 1024)
        b["wg1b"] = uv(b, 0, 128, 25600, 64)
        b["wcols"] = uv(b, 0, 128, 25664, 288, F32)
        b["hwT0"] = uv(b, 0, 128, 26240, 2048)
        b["hwT1"] = uv(b, 0, 128, 28288, 2048)
        if phase == "l1":
            b["cw"] = uv(b, 0, 128, 30336, 32, F32)
            b["gbuf"] = uv(b, 0, 128, 30368, 32, F32)
            b["eq1"] = uv(b, 0, 128, 30400, 32, F32)
            b["eq2"] = uv(b, 0, 128, 30432, 32, F32)
            b["xnb"] = uv(b, 0, 128, 30464, 1024)
            b["sq1k"] = uv(b, 0, 128, 31488, 1024)
    b["psA"] = st.enter_context(nc.psum_tensor("psA", [128, 1024], F32))
    b["psB"] = st.enter_context(nc.psum_tensor("psB", [128, 1024], F32))
    b["psC"] = st.enter_context(nc.psum_tensor("psC", [128, 1024], F32))
    b["psD"] = st.enter_context(nc.psum_tensor("psD", [128, 512], F32))
    b["psT"] = st.enter_context(nc.psum_tensor("psT", [128, 1024], BF16))
    return b


def emit_consts(p, b):
    import struct
    r0 = p.nc.gpsimd.alloc_register("fill0")
    p.nc.gpsimd.reg_mov(r0, 0)
    rn = p.nc.gpsimd.alloc_register("filln")
    p.nc.gpsimd.reg_mov(rn, struct.unpack("<i", struct.pack("<f", -1e30))[0])
    b["fill0"], b["filln"] = r0, rn
    p.op("pool", lambda e: e.memset(b["ident"][:], 1.0), writes=["ident"])
    p.op("pool", lambda e: e.affine_select(
        b["ident"][:], b["ident"][:], [[-1, 128]], ALU.is_equal, b["fill0"],
        channel_multiplier=1), reads=["ident"], writes=["ident"])
    p.op("pool", lambda e: e.memset(b["ones128"][:], 1.0), writes=["ones128"])
    p.op("pool", lambda e: e.memset(b["onesr"][0:65, :], 1.0), writes=["onesr"])
    p.op("pool", lambda e: e.memset(b["onesq"][:], 1.0), writes=["onesq"])
    p.op("pool", lambda e: e.memset(b["rsum"][:], 0.0), writes=["rs0", "rs1"])
    p.op("pool", lambda e: e.memset(b["cmsrc"][:], 0.0), writes=["cmsrc"])
    p.op("pool", lambda e: e.memset(b["kbU"][:], 0.0), writes=["kb"])
    p.op("pool", lambda e: e.memset(b["sq128"][:], 0.0),
         writes=["sqA", "sqB"])
    p.op("pool", lambda e: e.memset(b["konst"][:, 0:1], EPS), writes=["konst"])
    p.op("pool", lambda e: e.memset(b["konst"][:, 1:2], 1e-30),
         reads=["konst"], writes=["konst"])
    p.op("pool", lambda e: e.iota(
        b["stats"][:, 16:17], [[0, 1]], base=0, channel_multiplier=1,
        allow_small_or_imprecise_dtypes=True), writes=["pcol"])


def emit_vb_ones(p, b):
    p.op("pool", lambda e: e.memset(b["vb"][:], 0.0), writes=["vb"])
    vb = b["vb"].rearrange("p (k h c) -> p k h c", k=NT, h=4)
    p.op("pool", lambda e: e.memset(vb[:, :, :, 64:65], 1.0),
         reads=["vb"], writes=["vb"])


def _ln_tile(p, b, xt, xk, i, dst, dst_key, dstT):
    st = b["stats"]
    p.op("dve", lambda e: e.reduce_sum(st[:, 0:1], xt[:], axis=AX),
         reads=[xk], writes=["st0"])
    p.op("dve", lambda e: e.scalar_tensor_tensor(
        b["sq1k"], xt[:], 1.0, xt[:], ALU.mult, ALU.mult,
        accum_out=st[:, 1:2]), reads=[xk], writes=["st1", "sq1k"])
    p.op("dve", lambda e: e.tensor_scalar_mul(st[:, 2:3], st[:, 0:1], 1.0 / D),
         reads=["st0"], writes=["st2"])
    p.op("dve", lambda e: e.scalar_tensor_tensor(
        st[:, 3:4], st[:, 2:3], -1.0, st[:, 2:3], ALU.mult, ALU.mult),
        reads=["st2"], writes=["st3"])
    p.op("dve", lambda e: e.scalar_tensor_tensor(
        st[:, 4:5], st[:, 1:2], 1.0 / D, st[:, 3:4], ALU.mult, ALU.add),
        reads=["st1", "st3"], writes=["st4"])
    p.op("act", lambda e: e.activation(st[:, 5:6], st[:, 4:5], AF.Ln,
                                       bias=EPS),
         reads=["st4", "konst"], writes=["st5"])
    p.op("act", lambda e: e.activation(st[:, 6:7], st[:, 5:6], AF.Exp,
                                       scale=-0.5),
         reads=["st5"], writes=["st6"])
    p.op("dve", lambda e: e.tensor_scalar(
        b["xnb"], xt[:], st[:, 2:3], st[:, 6:7], ALU.subtract, ALU.mult),
        reads=[xk, "st2", "st6"], writes=["xnb"])
    for j in range(ND):
        p.op("pe", lambda e, j=j: e.transpose(
            b["psT"][:, j * 128:(j + 1) * 128],
            b["xnb"][:, j * 128:(j + 1) * 128], b["ident"][:]),
            reads=["xnb", "ident"], writes=["psT"])
    dv = dst[:]
    dst_ap = bass.AP(dv.tensor, dv.offset + i * 128,
                     [[dv.ap[0][0], 128], [dstT, ND], [1, 128]])
    p.op("dve", lambda e: e.tensor_copy(
        dst_ap, b["psT"][:].rearrange("p (j c) -> p j c", j=ND)),
        reads=["psT"], writes=[dst_key])


def emit_lnT(p, b, x_dram, ntiles, dst, dst_key, dstT):
    for i in range(ntiles):
        par = i % 2
        xt = b["xt0"] if par == 0 else b["xt1"]
        xk = "xt%d" % par
        p.op("sync", lambda e, i=i, xt=xt: e.dma_start(
            out=xt[:], in_=x_dram[i * 128:(i + 1) * 128, :]),
            writes=[xk], dma=True)
        _ln_tile(p, b, xt, xk, i, dst, dst_key, dstT)


def emit_qk_head(p, b, h, wqk_dram, has_bias, Tq, q_src_key=None):
    wk = "wqkb%d" % (h % 2)
    wb = b[wk]
    wslc = wqk_dram[:, h * 128:(h + 1) * 128]
    p.op("pool", lambda e: e.dma_start(
        out=wb.rearrange("p (ch c) -> p ch c", ch=ND),
        in_=wslc.rearrange("(ch p) c -> p ch c", p=128)),
        writes=[wk], dma=True)
    bk = b["bqk"][0:64, h:h + 1] if has_bias else 0.0
    bq = b["bqk"][64:128, h:h + 1] if has_bias else 0.0
    rds = ["bqk"] if has_bias else []
    nq = Tq // 512
    nk = T // 512
    n_cs = nq + nk
    cs_i = [0]
    pend = []

    def mk_colsum(slot, s_idx):
        i = cs_i[0]
        cs_i[0] += 1
        sqk = "sqA" if slot == 0 else "sqB"
        st, sp = (i == 0), (i == n_cs - 1)

        def emitfn():
            p.op("pe", lambda e: e.matmul(
                b["psD"][0:65, 0:512],
                b["mselb"][:, s_idx * 65:(s_idx + 1) * 65],
                b["sq128"][:, slot * 512:(slot + 1) * 512],
                start=st, stop=sp, skip_group_check=True),
                reads=[sqk, "mselb"], writes=["psD"])
        pend.append(emitfn)

    def flush():
        while pend:
            pend.pop(0)()

    def k_evac(tc, ps, bank):
        slot = tc % 2
        sqk = "sqA" if slot == 0 else "sqB"
        p.op("dve", lambda e, tc=tc, ps=ps: e.tensor_scalar(
            b["kb"][0:64, tc * 512:(tc + 1) * 512], ps[0:64, 0:512],
            bk, None, ALU.add),
            reads=[bank] + rds, writes=["kb"])
        p.op("act", lambda e, tc=tc, slot=slot: e.activation(
            b["sq"][0:64, slot * 512:(slot + 1) * 512],
            b["kb"][0:64, tc * 512:(tc + 1) * 512], AF.Square),
            reads=["kb"], writes=[sqk])
        mk_colsum(slot, 8 + tc)

    def q_evac_shift(tc, ps, bank):
        slot = tc % 2
        qk = "qst%d" % slot
        sqk = "sqA" if slot == 0 else "sqB"
        p.op("dve", lambda e, tc=tc, ps=ps: e.tensor_scalar(
            b["qstage"][0:64, tc * 512:(tc + 1) * 512], ps[64:128, 0:512],
            bq, 0.125, ALU.add, ALU.mult),
            reads=[bank] + rds, writes=[qk])
        p.op("act", lambda e, tc=tc: e.dma_start(
            out=b["qb"][0:64, tc * 512:(tc + 1) * 512],
            in_=b["qstage"][0:64, tc * 512:(tc + 1) * 512]),
            reads=[qk], writes=["qb"], dma=True)
        p.op("act", lambda e, tc=tc, slot=slot: e.activation(
            b["sqH"][0:64, slot * 512:(slot + 1) * 512],
            b["qstage"][0:64, tc * 512:(tc + 1) * 512], AF.Square),
            reads=[qk], writes=[sqk])
        mk_colsum(slot, tc)

    for tc in range(nk):
        bank = "psA0" if tc % 2 == 0 else "psB0"
        ps = b["psA"] if tc % 2 == 0 else b["psB"]
        for ch in range(ND):
            p.op("pe", lambda e, ch=ch, tc=tc, ps=ps: e.matmul(
                ps[:, 0:512], wb[:, ch * 128:(ch + 1) * 128],
                b["xnT"][:, ch * T + tc * 512: ch * T + tc * 512 + 512],
                start=(ch == 0), stop=(ch == ND - 1)),
                reads=[wk, "xnT"], writes=[bank])
        flush()
        k_evac(tc, ps, bank)
        if q_src_key is None and tc < nq:
            q_evac_shift(tc, ps, bank)
    if q_src_key is not None:
        qsrc = b[q_src_key]
        for tc in range(nq):
            bank = "psA0" if tc % 2 == 0 else "psB0"
            ps = b["psA"] if tc % 2 == 0 else b["psB"]
            for ch in range(ND):
                p.op("pe", lambda e, ch=ch, tc=tc, ps=ps: e.matmul(
                    ps[:, 0:512], wb[:, ch * 128:(ch + 1) * 128],
                    qsrc[:, ch * TO + tc * 512: ch * TO + tc * 512 + 512],
                    start=(ch == 0), stop=(ch == ND - 1)),
                    reads=[wk, q_src_key], writes=[bank])
            flush()
            q_evac_shift(tc, ps, bank)
    flush()


def emit_v_group(p, b, hg, wv_dram, bv_dram):
    p.op("pool", lambda e: e.dma_start(
        out=b["wvb"].rearrange("p (ch c) -> p ch c", ch=ND),
        in_=wv_dram[:, hg * 256:(hg + 1) * 256].rearrange(
            "(ch p) c -> p ch c", p=128)),
        writes=["wvb"], dma=True)
    have_bias = bv_dram is not None
    if have_bias:
        p.op("pool", lambda e: e.dma_start(
            out=b["bvr"][0:1, 0:256],
            in_=bv_dram[hg * 4:(hg + 1) * 4, :].rearrange("(o h) c -> o (h c)", o=1)),
            writes=["bvr"], dma=True)
        p.op("pe", lambda e: e.matmul(
            b["psD"][:, 0:256], b["onesr"][0:1, :], b["bvr"][0:1, 0:256],
            start=True, stop=True), reads=["bvr", "onesr"], writes=["psD"])
        p.op("dve", lambda e: e.tensor_copy(b["bvb"][:, 0:256],
                                            b["psD"][:, 0:256]),
             reads=["psD"], writes=["bvb"])
    vv = b["vb"]
    for i in range(NT):
        bank = "psA0" if i % 2 == 0 else "psB0"
        ps = b["psA"] if i % 2 == 0 else b["psB"]
        for ch in range(ND):
            p.op("pe", lambda e, ch=ch, i=i, ps=ps: e.matmul(
                ps[:, 0:256],
                b["xnT"][:, ch * T + i * 128: ch * T + i * 128 + 128],
                b["wvb"][:, ch * 256:(ch + 1) * 256],
                start=(ch == 0), stop=(ch == ND - 1)),
                reads=["wvb", "xnT"], writes=[bank])
        out_ap = bass.AP(vv.tensor, vv.offset + i * 260,
                         [[UCOLS, 128], [65, 4], [1, 64]])
        src = ps[:, 0:256].rearrange("p (h c) -> p h c", h=4)
        if have_bias:
            p.op("dve", lambda e, src=src, out_ap=out_ap:
                 e.scalar_tensor_tensor(
                     out_ap, src, 1.0,
                     b["bvb"][:, 0:256].rearrange("p (h c) -> p h c", h=4),
                     ALU.mult, ALU.add),
                 reads=[bank, "bvb"], writes=["vb"])
        else:
            p.op("dve", lambda e, src=src, out_ap=out_ap: e.tensor_copy(
                out_ap, src), reads=[bank], writes=["vb"])


def emit_head_M(p, b, Tq):
    nq = Tq // 512
    mr = b["mrow"]
    p.op("dve", lambda e: e.reduce_max(b["mx"][0:16, 0:1],
                                       b["psD"][0:16, 0:512], axis=AX),
         reads=["psD"], writes=["mx0"])
    p.op("dve", lambda e: e.tensor_copy(b["mxb"][0:16, 0:1],
                                        b["mx"][0:16, 0:1]),
         reads=["mx0"], writes=["mxb"])
    p.op("pe", lambda e: e.transpose(b["psT"][0:1, 0:128], b["mxb"][:],
                                     b["ident"][:]),
         reads=["mxb", "ident"], writes=["psT"])
    p.op("dve", lambda e: e.reduce_max(mr[0:1, 16:17], b["psT"][0:1, 0:nq],
                                       axis=AX),
         reads=["psT"], writes=["mrow"])
    p.op("dve", lambda e: e.reduce_max(mr[0:1, 17:18], b["psT"][0:1, 8:16],
                                       axis=AX),
         reads=["psT"], writes=["mrow"])
    p.op("dve", lambda e: e.tensor_tensor(mr[0:1, 18:19], mr[0:1, 16:17],
                                          mr[0:1, 17:18], ALU.mult),
         reads=["mrow"], writes=["mrow"])
    p.op("act", lambda e: e.activation(mr[0:1, 19:20], mr[0:1, 18:19],
                                       AF.Ln, bias=1e-30),
         reads=["mrow", "konst"], writes=["mrow"])
    p.op("act", lambda e: e.activation(mr[0:1, 20:21], mr[0:1, 19:20],
                                       AF.Exp, scale=0.5),
         reads=["mrow"], writes=["mrow"])
    p.op("dve", lambda e: e.tensor_scalar(mr[0:1, 21:22], mr[0:1, 20:21],
                                          -1.0, CEXP, ALU.mult, ALU.add),
         reads=["mrow"], writes=["mrow"])
    p.op("dve", lambda e: e.tensor_copy(b["cmsrc"][0:1, 0:1],
                                        mr[0:1, 21:22]),
         reads=["mrow"], writes=["cmsrc"])
    p.op("pe", lambda e: e.matmul(b["psD"][:, 200:201], b["onesq"][:, 0:128],
                                  b["cmsrc"][:, 0:1], start=True,
                                  stop=True),
         reads=["cmsrc", "onesq"], writes=["psD"])
    p.op("dve", lambda e: e.tensor_copy(b["cm"][:], b["psD"][:, 200:201]),
         reads=["psD"], writes=["cm"])


def valid_kts(layer, q0, qw, slope):
    if layer == 1:
        return list(range(NT))
    kts = []
    for kt in range(NT):
        ktb = kt * 128
        if ktb >= q0 + qw:
            continue
        if slope * (q0 - ktb - 127) >= SKIP_GAP:
            continue
        kts.append(kt)
    return kts


def emit_attn_head(p, b, layer, h, Tq, oT_cols):
    slope = SLOPES[h] if layer == 0 else 0.0
    nch = Tq // QW
    psn = {0: "psA", 1: "psB"}
    ptn = {0: "pt0", 1: "pt1"}
    for cp in range(nch // 2):
        q0s = [(2 * cp) * QW, (2 * cp + 1) * QW]
        ktss = [valid_kts(layer, q0, QW, slope) for q0 in q0s]
        L = max(len(k) for k in ktss)

        def s_mm(s, n):
            kt = ktss[s][n]
            key = psn[s] + str(n % 2)
            p.op("pe", lambda e, kt=kt, s=s, n=n: e.matmul(
                b[psn[s]][:, (n % 2) * 512:(n % 2) * 512 + 512],
                b["kb128"][:, kt * 128:(kt + 1) * 128],
                b["qb128"][:, q0s[s]:q0s[s] + QW],
                start=True, stop=True),
                reads=["kb", "qb"], writes=[key])

        def do_exp(s, n):
            kt = ktss[s][n]
            ktb = kt * 128
            pskey = psn[s] + str(n % 2)
            pb = ptn[s]
            ptkey = pb + str(n % 2)
            ptv = b[pb][:, (n % 2) * 512:(n % 2) * 512 + 512]
            p.op("act", lambda e, pskey=pskey, ptv=ptv, s=s, n=n:
                 e.activation(ptv, b[psn[s]][:, (n % 2) * 512:
                                             (n % 2) * 512 + 512],
                              AF.Exp, bias=b["cm"][:]),
                 reads=[pskey, "cm"], writes=[ptkey])
            if layer == 0:
                off = GOFF + q0s[s] - ktb
                p.op("dve", lambda e, ptv=ptv, off=off: e.tensor_tensor(
                    ptv, ptv, b["gt"][:, off:off + QW], ALU.mult),
                    reads=[ptkey, "gt"], writes=[ptkey])

        def do_av(s, n):
            kt = ktss[s][n]
            ptkey = ptn[s] + str(n % 2)
            first, last = (n == 0), (n == len(ktss[s]) - 1)
            p.op("pe", lambda e, kt=kt, s=s, n=n, first=first, last=last:
                 e.matmul(
                     b["psC"][0:65, s * 512:s * 512 + 512],
                     b["vb"][:, kt * 260 + (h % 4) * 65:
                             kt * 260 + (h % 4) * 65 + 65],
                     b[ptn[s]][:, (n % 2) * 512:(n % 2) * 512 + 512],
                     start=first, stop=last),
                 reads=[ptkey, "vb"], writes=["psC%d" % s])

        for n in range(L + 1):
            for s in range(2):
                if n < len(ktss[s]):
                    s_mm(s, n)
                if 1 <= n <= len(ktss[s]):
                    do_av(s, n - 1)
                if n < len(ktss[s]):
                    do_exp(s, n)
        ch, sub = h // 2, (h % 2) * 64
        for s in range(2):
            rbv = b["rbA"] if s == 0 else b["rbB"]
            rsk = "rs%d" % s
            p.op("dve", lambda e, s=s: e.tensor_copy(
                b["rsum"][64:65, s * 512:(s + 1) * 512],
                b["psC"][64:65, s * 512:s * 512 + 512]),
                reads=["psC%d" % s], writes=[rsk])
            p.op("pe", lambda e, s=s: e.matmul(
                b["psD"][:, 0:512], b["onesq"][:, 0:128],
                b["rsum"][:, s * 512:(s + 1) * 512], start=True, stop=True),
                reads=[rsk, "onesq"], writes=["psD"])
            p.op("dve", lambda e, rbv=rbv: e.reciprocal(
                rbv[0:64, 0:512], b["psD"][0:64, 0:512]),
                reads=["psD"], writes=["rb%d" % s])
            obk = "ob%d" % s
            p.op("dve", lambda e, s=s, obk=obk, rbv=rbv: e.tensor_tensor(
                b[obk][0:64, 0:QW], b["psC"][0:64, s * 512:s * 512 + 512],
                rbv[0:64, 0:512], ALU.mult),
                reads=["psC%d" % s, "rb%d" % s], writes=[obk])
            dst = b["oT"][sub:sub + 64,
                          ch * oT_cols + q0s[s]: ch * oT_cols + q0s[s] + QW]
            p.op("sync", lambda e, dst=dst, obk=obk: e.dma_start(
                out=dst, in_=b[obk][0:64, 0:QW]),
                reads=[obk], writes=["oT"], dma=True)


def _attn_layer(p, b, layer, t, Tq, oT_cols, q_src_key):
    wqk = t["wqk0"] if layer == 0 else t["wqk1"]
    wv = t["wv0"] if layer == 0 else t["wv1"]
    has_bias = (layer == 0)
    bv = t["bv0"] if layer == 0 else None
    for hg in range(4):
        emit_v_group(p, b, hg, wv, bv)
        for hh in range(4):
            h = hg * 4 + hh
            if layer == 0:
                p.op("pool", lambda e, h=h: e.dma_start(
                    out=b["gt"], in_=t["gtab"][h * 128:(h + 1) * 128, :]),
                    writes=["gt"], dma=True)
            emit_qk_head(p, b, h, wqk, has_bias, Tq, q_src_key=q_src_key)
            emit_head_M(p, b, Tq)
            emit_attn_head(p, b, layer, h, Tq, oT_cols)


def _top2_softmax(p, b, sigmoid):
    st = b["stats"]
    p.op("dve", lambda e: e.tensor_copy(b["gbuf"][:, 0:8], b["psD"][:, 0:8]),
         reads=["psD"], writes=["gbuf"])
    p.op("dve", lambda e: e.reduce_max(st[:, 8:9], b["gbuf"][:, 0:8],
                                       axis=AX),
         reads=["gbuf"], writes=["m1"])
    p.op("dve", lambda e: e.tensor_scalar(b["eq1"][:, 0:8], b["gbuf"][:, 0:8],
                                          st[:, 8:9], None, ALU.is_equal),
         reads=["gbuf", "m1"], writes=["eq1"])
    p.op("dve", lambda e: e.scalar_tensor_tensor(
        b["eq2"][:, 0:8], b["eq1"][:, 0:8], -1e30, b["gbuf"][:, 0:8],
        ALU.mult, ALU.add), reads=["eq1", "gbuf"], writes=["eq2"])
    p.op("dve", lambda e: e.reduce_max(st[:, 9:10], b["eq2"][:, 0:8],
                                       axis=AX),
         reads=["eq2"], writes=["m2"])
    p.op("dve", lambda e: e.tensor_scalar(b["eq2"][:, 0:8], b["gbuf"][:, 0:8],
                                          st[:, 9:10], None, ALU.is_equal),
         reads=["gbuf", "m2"], writes=["eq2"])
    if not sigmoid:
        p.op("dve", lambda e: e.tensor_sub(st[:, 10:11], st[:, 9:10],
                                           st[:, 8:9]),
             reads=["m1", "m2"], writes=["d21"])
        p.op("act", lambda e: e.activation(st[:, 11:12], st[:, 10:11],
                                           AF.Exp),
             reads=["d21"], writes=["e21"])
        p.op("dve", lambda e: e.tensor_scalar_add(st[:, 12:13],
                                                  st[:, 11:12], 1.0),
             reads=["e21"], writes=["s21"])
        p.op("dve", lambda e: e.reciprocal(st[:, 13:14], st[:, 12:13]),
             reads=["s21"], writes=["wA"])
        p.op("dve", lambda e: e.tensor_tensor(st[:, 14:15], st[:, 11:12],
                                              st[:, 13:14], ALU.mult),
             reads=["e21", "wA"], writes=["wB"])
        wa, wb_ = 13, 14
    else:
        for k, src, kk in ((10, 8, "m1"), (11, 9, "m2")):
            p.op("act", lambda e, k=k, src=src: e.activation(
                st[:, k:k + 1], st[:, src:src + 1], AF.Exp, scale=-1.0),
                reads=[kk], writes=["sg%d" % k])
            p.op("dve", lambda e, k=k: e.tensor_scalar_add(
                st[:, k:k + 1], st[:, k:k + 1], 1.0),
                reads=["sg%d" % k], writes=["sg%d" % k])
            p.op("dve", lambda e, k=k: e.reciprocal(st[:, k + 2:k + 3],
                                                    st[:, k:k + 1]),
                 reads=["sg%d" % k], writes=["v%d" % k])
        p.op("dve", lambda e: e.tensor_add(st[:, 14:15], st[:, 12:13],
                                           st[:, 13:14]),
             reads=["v10", "v11"], writes=["vs"])
        p.op("dve", lambda e: e.reciprocal(st[:, 15:16], st[:, 14:15]),
             reads=["vs"], writes=["vsr"])
        p.op("dve", lambda e: e.tensor_scalar_mul(st[:, 15:16],
                                                  st[:, 15:16], 1.0 / 3.0),
             reads=["vsr"], writes=["vsr"])
        p.op("dve", lambda e: e.tensor_tensor(st[:, 10:11], st[:, 12:13],
                                              st[:, 15:16], ALU.mult),
             reads=["v10", "vsr"], writes=["wA"])
        p.op("dve", lambda e: e.tensor_tensor(st[:, 11:12], st[:, 13:14],
                                              st[:, 15:16], ALU.mult),
             reads=["v11", "vsr"], writes=["wB"])
        wa, wb_ = 10, 11
    p.op("dve", lambda e, wa=wa: e.tensor_scalar(
        b["cw"][:, 0:8], b["eq1"][:, 0:8], st[:, wa:wa + 1], None, ALU.mult),
        reads=["eq1", "wA"], writes=["cw"])
    p.op("dve", lambda e, wb_=wb_: e.scalar_tensor_tensor(
        b["cw"][:, 0:8], b["eq2"][:, 0:8], st[:, wb_:wb_ + 1],
        b["cw"][:, 0:8], ALU.mult, ALU.add),
        reads=["eq2", "wB", "cw"], writes=["cw"])


def emit_gate0_logits(p, b, i):
    for ch in range(ND):
        p.op("pe", lambda e, ch=ch: e.matmul(
            b["psD"][:, 0:8],
            b["xnT"][:, ch * T + i * 128: ch * T + i * 128 + 128],
            b["wg0b"][:, ch * 8:(ch + 1) * 8],
            start=(ch == 0), stop=(ch == ND - 1)),
            reads=["wg0b", "xnT"], writes=["psD"])


def emit_gate0_rest(p, b, i, psy, pk):
    _top2_softmax(p, b, sigmoid=False)
    for ch in range(ND):
        p.op("pe", lambda e, ch=ch: e.matmul(
            b["psB"][:, 0:128],
            b["xnT"][:, ch * T + i * 128: ch * T + i * 128 + 128],
            b["a0b"][:, ch * 128:(ch + 1) * 128],
            start=(ch == 0), stop=(ch == ND - 1)),
            reads=["a0b", "xnT"], writes=["psB0"])
    cw8 = b["cw"][:, 0:8]
    cwb = bass.AP(cw8.tensor, cw8.offset,
                  [list(d) for d in cw8.ap] + [[0, 16]])
    p.op("dve", lambda e: e.tensor_tensor(
        b["hw0"].rearrange("p (g r) -> p g r", g=8),
        b["psB"][:, 0:128].rearrange("p (g r) -> p g r", g=8), cwb,
        ALU.mult),
        reads=["psB0", "cw"], writes=["hw0"])
    p.op("pe", lambda e: e.transpose(b["psT"][:, 0:128], b["hw0"],
                                     b["ident"][:]),
         reads=["hw0", "ident"], writes=["psT"])
    p.op("dve", lambda e: e.tensor_copy(b["hw0t"], b["psT"][:, 0:128]),
         reads=["psT"], writes=["hw0t"])
    for half in range(2):
        p.op("pe", lambda e, half=half: e.matmul(
            psy[:, half * 512:(half + 1) * 512], b["hw0t"],
            b["b0b"][:, half * 512:(half + 1) * 512],
            start=False, stop=True), reads=["hw0t", "b0b"],
            writes=[pk[half]])


def load_asm0_weights(p, b, t):
    p.op("pool", lambda e: e.memset(b["bpjb128"][:], 0.0), writes=["bpjb"])
    p.op("pool", lambda e: e.dma_start(
        out=b["wob"].rearrange("p (ch c) -> p ch c", ch=ND),
        in_=t["wo0"].rearrange("(ch p) c -> p ch c", p=128)),
        writes=["wob"], dma=True)
    p.op("pool", lambda e: e.dma_start(
        out=b["wg0b"].rearrange("p (ch c) -> p ch c", ch=ND),
        in_=t["wg0"].rearrange("(ch p) c -> p ch c", p=128)),
        writes=["wg0b"], dma=True)
    p.op("pool", lambda e: e.dma_start(
        out=b["a0b"].rearrange("p (ch c) -> p ch c", ch=ND),
        in_=t["a0"].rearrange("(ch p) c -> p ch c", p=128)),
        writes=["a0b"], dma=True)
    p.op("pool", lambda e: e.dma_start(out=b["b0b"], in_=t["b0"]),
         writes=["b0b"], dma=True)
    p.op("pool", lambda e: e.dma_start(out=b["bpjb"][0:1, :], in_=t["bproj"]),
         writes=["bpjb"], dma=True)


def emit_assembly_l0(p, b, x0_dram, ln1, x1_dram, x1_out):
    for i in range(NT):
        psy = b["psC"] if i % 2 == 0 else b["psA"]
        pk = ["psC0", "psC1"] if i % 2 == 0 else ["psA0", "psA1"]
        emit_gate0_logits(p, b, i)
        for half in range(2):
            p.op("pe", lambda e, half=half, psy=psy: e.matmul(
                psy[:, half * 512:(half + 1) * 512], b["onesq"][:, 0:128],
                b["bpjb128"][:, half * 512:(half + 1) * 512],
                start=True, stop=False),
                reads=["bpjb", "onesq"], writes=[pk[half]])
        for ch in range(ND):
            for half in range(2):
                p.op("pe", lambda e, ch=ch, i=i, half=half, psy=psy: e.matmul(
                    psy[:, half * 512:(half + 1) * 512],
                    b["oT"][:, ch * T + i * 128: ch * T + i * 128 + 128],
                    b["wob"][:, ch * 1024 + half * 512:
                             ch * 1024 + (half + 1) * 512],
                    start=False, stop=False), reads=["oT", "wob"],
                    writes=[pk[half]])
        emit_gate0_rest(p, b, i, psy, pk)
        par = i % 2
        xt = b["xt0"] if par == 0 else b["xt1"]
        xk = "xt%d" % par
        p.op("sync", lambda e, i=i, xt=xt: e.dma_start(
            out=xt[:], in_=x0_dram[i * 128:(i + 1) * 128, :]),
            writes=[xk], dma=True)
        p.op("dve", lambda e, xt=xt, psy=psy: e.scalar_tensor_tensor(
            xt[:], psy[:], 1.0, xt[:], ALU.mult, ALU.add),
            reads=pk, writes=[xk])
        if x1_dram is not None:
            p.op("sync", lambda e, i=i, xt=xt: e.dma_start(
                out=x1_dram[i * 128:(i + 1) * 128, :], in_=xt[:]),
                reads=[xk], writes=["x1scr"], dma=True)
        if x1_out is not None:
            p.op("sync", lambda e, i=i, xt=xt: e.dma_start(
                out=x1_out[i * 128:(i + 1) * 128, :], in_=xt[:]),
                reads=[xk], writes=["x1out"], dma=True)
        if ln1:
            _ln_tile(p, b, xt, xk, i, b["xnT"], "xnT", T)


def yacc_slice(b, i):
    return b["xnT"][:, i * 2048:(i + 1) * 2048].bitcast(F32)


def emit_gating1_proj1(p, b, t):
    p.op("pool", lambda e: e.dma_start(
        out=b["wg1b"].rearrange("p (ch c) -> p ch c", ch=ND),
        in_=t["wg1"].rearrange("(ch p) c -> p ch c", p=128)),
        writes=["wg1b"], dma=True)
    p.op("pool", lambda e: e.dma_start(
        out=b["wob"].rearrange("p (ch c) -> p ch c", ch=ND),
        in_=t["wo1"].rearrange("(ch p) c -> p ch c", p=128)),
        writes=["wob"], dma=True)
    sh = b["wcols"].rearrange("p (i e) -> p i e", i=NTO)
    p.op("pool", lambda e: e.memset(sh[:, :, 0:1], 1.0 / 3.0),
         writes=["wcols"])
    for i in range(NTO):
        psy = b["psC"] if i % 2 == 0 else b["psA"]
        pk = ["psC0", "psC1"] if i % 2 == 0 else ["psA0", "psA1"]
        for ch in range(ND):
            p.op("pe", lambda e, ch=ch, i=i: e.matmul(
                b["psD"][:, 0:8],
                b["ownT"][:, ch * TO + i * 128: ch * TO + i * 128 + 128],
                b["wg1b"][:, ch * 8:(ch + 1) * 8],
                start=(ch == 0), stop=(ch == ND - 1)),
                reads=["wg1b", "ownT"], writes=["psD"])
        for ch in range(ND):
            for half in range(2):
                p.op("pe", lambda e, ch=ch, i=i, half=half, psy=psy:
                     e.matmul(
                         psy[:, half * 512:(half + 1) * 512],
                         b["oT"][:, ch * TO + i * 128: ch * TO + i * 128 + 128],
                         b["wob"][:, ch * 1024 + half * 512:
                                  ch * 1024 + (half + 1) * 512],
                         start=(ch == 0), stop=(ch == ND - 1)),
                     reads=["oT", "wob"], writes=[pk[half]])
        _top2_softmax(p, b, sigmoid=True)
        p.op("dve", lambda e, i=i: e.tensor_copy(
            b["wcols"][:, i * NE + 1: i * NE + 9], b["cw"][:, 0:8]),
            reads=["cw"], writes=["wcols"])
        p.op("dve", lambda e, i=i, psy=psy: e.scalar_tensor_tensor(
            yacc_slice(b, i), psy[:], 1.0, yacc_slice(b, i),
            ALU.mult, ALU.add),
            reads=pk + ["yacc"], writes=["yacc"])


def emit_ffn1(p, b, we13_dram, we2_dram):
    dq = []

    def pop_d():
        if dq:
            dq.pop(0)()

    for e_i in range(NE):
        for fq in range(4):
            par = (e_i * 4 + fq) % 2
            w13k, w2k = "w13_%d" % par, "w2_%d" % par
            w13 = b["w13b0"] if par == 0 else b["w13b1"]
            w2 = b["w2b0"] if par == 0 else b["w2b1"]
            p.op("pool", lambda e, e_i=e_i, fq=fq, w13=w13: e.dma_start(
                out=w13.rearrange("p (ch c) -> p ch c", ch=ND),
                in_=we13_dram[e_i][:, fq * 1024:(fq + 1) * 1024].rearrange(
                    "(ch p) c -> p ch c", p=128)),
                writes=[w13k], dma=True)
            p.op("pool", lambda e, e_i=e_i, fq=fq, w2=w2: e.dma_start(
                out=w2.rearrange("p (ch c) -> p ch c", ch=4),
                in_=we2_dram[e_i][fq * 512:(fq + 1) * 512, :].rearrange(
                    "(ch p) c -> p ch c", p=128)),
                writes=[w2k], dma=True)
            for tc in range(4):
                gpar = (e_i * 16 + fq * 4 + tc) % 2
                hw = b["hwT0"] if gpar == 0 else b["hwT1"]
                hwk = "hwT%d" % gpar
                for j in range(4):
                    up = j % 2
                    uk1, uk3 = "psA%d" % up, "psB%d" % up
                    for ch in range(ND):
                        p.op("pe", lambda e, ch=ch, j=j, tc=tc, up=up,
                             w13=w13: e.matmul(
                                 b["psA"][:, up * 512:up * 512 + 512],
                                 w13[:, ch * 1024 + j * 128:
                                     ch * 1024 + j * 128 + 128],
                                 b["ownT"][:, ch * TO + tc * 512:
                                           ch * TO + tc * 512 + 512],
                                 start=(ch == 0), stop=(ch == ND - 1)),
                             reads=[w13k, "ownT"], writes=[uk1])
                    for ch in range(ND):
                        p.op("pe", lambda e, ch=ch, j=j, tc=tc, up=up,
                             w13=w13: e.matmul(
                                 b["psB"][:, up * 512:up * 512 + 512],
                                 w13[:, ch * 1024 + 512 + j * 128:
                                     ch * 1024 + 512 + j * 128 + 128],
                                 b["ownT"][:, ch * TO + tc * 512:
                                           ch * TO + tc * 512 + 512],
                                 start=(ch == 0), stop=(ch == ND - 1)),
                             reads=[w13k, "ownT"], writes=[uk3])
                    p.op("act", lambda e, up=up, j=j: e.activation(
                        b["s1b"][:, up * 512:up * 512 + 512],
                        b["psA"][:, up * 512:up * 512 + 512], AF.Silu),
                        reads=[uk1], writes=["s1b%d" % up])
                    p.op("dve", lambda e, up=up, j=j, hw=hw: e.tensor_tensor(
                        hw[:, j * 512:(j + 1) * 512],
                        b["s1b"][:, up * 512:up * 512 + 512],
                        b["psB"][:, up * 512:up * 512 + 512], ALU.mult),
                        reads=[uk3, "s1b%d" % up], writes=[hwk])
                    pop_d()

                def mk_down(e_i, tc, hw, hwk, w2, w2k):
                    def gen(il):
                        def f():
                            i = tc * 4 + il
                            for j in range(4):
                                for half in range(2):
                                    p.op("pe", lambda e, j=j, half=half,
                                         il=il, hw=hw, w2=w2: e.matmul(
                                             b["psC"][:, half * 512:
                                                      half * 512 + 512],
                                             hw[:, j * 512 + il * 128:
                                                j * 512 + il * 128 + 128],
                                             w2[:, j * 1024 + half * 512:
                                                j * 1024 + half * 512 + 512],
                                             start=(j == 0), stop=(j == 3)),
                                         reads=[hwk, w2k],
                                         writes=["psC%d" % half])
                            wcol = b["wcols"][:, i * NE + e_i:
                                              i * NE + e_i + 1]
                            p.op("dve", lambda e, i=i, wcol=wcol:
                                 e.scalar_tensor_tensor(
                                     yacc_slice(b, i), b["psC"][:], wcol,
                                     yacc_slice(b, i), ALU.mult, ALU.add),
                                 reads=["psC0", "psC1", "wcols", "yacc"],
                                 writes=["yacc"])
                        return f
                    return [gen(il) for il in range(4)]
                dq.extend(mk_down(e_i, tc, hw, hwk, w2, w2k))
    while dq:
        pop_d()


def emit_assembly_l1(p, b, x1own_src, out_dram):
    for i in range(NTO):
        par = i % 2
        xt = b["xt0"] if par == 0 else b["xt1"]
        xk = "xt%d" % par
        if x1own_src[0] == "dram":
            p.op("sync", lambda e, i=i, xt=xt: e.dma_start(
                out=xt[:], in_=x1own_src[1][i * 128:(i + 1) * 128, :]),
                writes=[xk], dma=True)
        else:
            idxs, x1d = x1own_src[1], x1own_src[2]
            p.op("pool", lambda e, i=i, xt=xt: e.indirect_dma_start(
                out=xt[:], out_offset=None, in_=x1d,
                in_offset=IndirectOffsetOnAxis(idxs[:, i:i + 1], 0)),
                reads=["idxX", "x1scr"], writes=[xk], dma=True)
        p.op("dve", lambda e, i=i, xt=xt: e.scalar_tensor_tensor(
            xt[:], yacc_slice(b, i), 1.0, xt[:], ALU.mult, ALU.add),
            reads=["yacc"], writes=[xk])
        p.op("sync", lambda e, i=i, xt=xt: e.dma_start(
            out=out_dram[i * 128:(i + 1) * 128, :], in_=xt[:]),
            reads=[xk], writes=["x2out"], dma=True)


def build_l0(x1_as_output):
    nc = bass.Bass("TRN2", target_bir_lowering=False)
    t = {}
    t["x0"] = nc.dram_tensor("x0", [T, D], F32, kind="ExternalInput").ap()
    t["wqk0"] = nc.dram_tensor("wqk0", [D, H * 128], F32,
                               kind="ExternalInput").ap()
    t["wv0"] = nc.dram_tensor("wv0", [D, H * 64], F32,
                              kind="ExternalInput").ap()
    t["bqk0"] = nc.dram_tensor("bqk0", [128, 16], F32,
                               kind="ExternalInput").ap()
    t["bv0"] = nc.dram_tensor("bv0", [16, 64], F32,
                              kind="ExternalInput").ap()
    t["wo0"] = nc.dram_tensor("wo0", [D, D], F32, kind="ExternalInput").ap()
    t["bproj"] = nc.dram_tensor("bproj", [1, D], F32,
                                kind="ExternalInput").ap()
    t["wg0"] = nc.dram_tensor("wg0", [D, 8], F32, kind="ExternalInput").ap()
    t["a0"] = nc.dram_tensor("a0", [D, 128], F32, kind="ExternalInput").ap()
    t["b0"] = nc.dram_tensor("b0", [128, D], F32, kind="ExternalInput").ap()
    t["gtab"] = nc.dram_tensor("gtab", [H * 128, GW], BF16,
                               kind="ExternalInput").ap()
    t["msel"] = nc.dram_tensor("msel", [128, 1040], BF16,
                               kind="ExternalInput").ap()
    x1_out = nc.dram_tensor("x1", [T, D], F32, kind="ExternalOutput").ap() \
        if x1_as_output else None

    with ExitStack() as st:
        b = alloc_bufs(nc, st, "l0")
        p = Prog(nc, st)
        emit_consts(p, b)
        p.op("sync", lambda e: e.dma_start(out=b["bqk"][:], in_=t["bqk0"]),
             writes=["bqk"], dma=True)
        p.op("sync", lambda e: e.dma_start(out=b["mselb"][:], in_=t["msel"]),
             writes=["mselb"], dma=True)
        emit_lnT(p, b, t["x0"], NT, b["xnT"], "xnT", T)
        p.barrier(["xnb", "sq1k"])
        emit_vb_ones(p, b)
        _attn_layer(p, b, 0, t, T, T, None)
        p.barrier(ATTN_KEYS)
        load_asm0_weights(p, b, t)
        emit_assembly_l0(p, b, t["x0"], ln1=False, x1_dram=None,
                         x1_out=x1_out)
        p.finish()
    return nc


def build_l1():
    nc = bass.Bass("TRN2", target_bir_lowering=False)
    t = {}
    t["x1f"] = nc.dram_tensor("x1f", [T, D], F32, kind="ExternalInput").ap()
    t["x1o"] = nc.dram_tensor("x1o", [TO, D], F32, kind="ExternalInput").ap()
    t["wqk1"] = nc.dram_tensor("wqk1", [D, H * 128], F32,
                               kind="ExternalInput").ap()
    t["wv1"] = nc.dram_tensor("wv1", [D, H * 64], F32,
                              kind="ExternalInput").ap()
    t["wo1"] = nc.dram_tensor("wo1", [D, D], F32, kind="ExternalInput").ap()
    t["wg1"] = nc.dram_tensor("wg1", [D, 8], F32, kind="ExternalInput").ap()
    we13 = [nc.dram_tensor("we13_%d" % i, [D, 2 * FF], F32,
                           kind="ExternalInput").ap() for i in range(NE)]
    we2 = [nc.dram_tensor("we2_%d" % i, [FF, D], F32,
                          kind="ExternalInput").ap() for i in range(NE)]
    t["msel"] = nc.dram_tensor("msel", [128, 1040], BF16,
                               kind="ExternalInput").ap()
    out = nc.dram_tensor("x2", [TO, D], F32, kind="ExternalOutput").ap()

    with ExitStack() as st:
        b = alloc_bufs(nc, st, "l1")
        p = Prog(nc, st)
        emit_consts(p, b)
        p.op("sync", lambda e: e.dma_start(out=b["mselb"][:], in_=t["msel"]),
             writes=["mselb"], dma=True)
        emit_lnT(p, b, t["x1f"], NT, b["xnT"], "xnT", T)
        emit_lnT(p, b, t["x1o"], NTO, b["ownT"], "ownT", TO)
        p.barrier(["xnb", "sq1k"])
        emit_vb_ones(p, b)
        _attn_layer(p, b, 1, t, TO, TO, "ownT")
        p.barrier(ATTN_KEYS)
        p.op("pool", lambda e: e.memset(b["xnT"][:], 0.0),
             reads=["xnT"], writes=["xnT", "yacc"])
        emit_gating1_proj1(p, b, t)
        p.barrier(["wob"])
        emit_ffn1(p, b, we13, we2)
        emit_assembly_l1(p, b, ("dram", t["x1o"]), out)
        p.finish()
    return nc




ASM0_KEYS = ["wob", "a0b", "b0b", "hw0", "hw0t", "bpjb", "zrow", "wg0b",
             "xnb", "sq1k", "cw", "gbuf", "eq1", "eq2"]


def build_fused():
    nc = bass.Bass("TRN2", target_bir_lowering=False)
    t = {}
    t["x0"] = nc.dram_tensor("x0", [T, D], F32, kind="ExternalInput").ap()
    t["wqk0"] = nc.dram_tensor("wqk0", [D, H * 128], F32,
                               kind="ExternalInput").ap()
    t["wv0"] = nc.dram_tensor("wv0", [D, H * 64], F32,
                              kind="ExternalInput").ap()
    t["bqk0"] = nc.dram_tensor("bqk0", [128, 16], F32,
                               kind="ExternalInput").ap()
    t["bv0"] = nc.dram_tensor("bv0", [16, 64], F32,
                              kind="ExternalInput").ap()
    t["wo0"] = nc.dram_tensor("wo0", [D, D], F32, kind="ExternalInput").ap()
    t["bproj"] = nc.dram_tensor("bproj", [1, D], F32,
                                kind="ExternalInput").ap()
    t["wg0"] = nc.dram_tensor("wg0", [D, 8], F32, kind="ExternalInput").ap()
    t["a0"] = nc.dram_tensor("a0", [D, 128], F32, kind="ExternalInput").ap()
    t["b0"] = nc.dram_tensor("b0", [128, D], F32, kind="ExternalInput").ap()
    t["gtab"] = nc.dram_tensor("gtab", [H * 128, GW], BF16,
                               kind="ExternalInput").ap()
    t["wqk1"] = nc.dram_tensor("wqk1", [D, H * 128], F32,
                               kind="ExternalInput").ap()
    t["wv1"] = nc.dram_tensor("wv1", [D, H * 64], F32,
                              kind="ExternalInput").ap()
    t["wo1"] = nc.dram_tensor("wo1", [D, D], F32, kind="ExternalInput").ap()
    t["wg1"] = nc.dram_tensor("wg1", [D, 8], F32, kind="ExternalInput").ap()
    we13 = [nc.dram_tensor("we13_%d" % i, [D, 2 * FF], F32,
                           kind="ExternalInput").ap() for i in range(NE)]
    we2 = [nc.dram_tensor("we2_%d" % i, [FF, D], F32,
                          kind="ExternalInput").ap() for i in range(NE)]
    t["msel"] = nc.dram_tensor("msel", [128, 1040], BF16,
                               kind="ExternalInput").ap()
    idxT = nc.dram_tensor("idxT", [128, 8], I32, kind="ExternalInput").ap()
    idxX = nc.dram_tensor("idxX", [128, 16], I32, kind="ExternalInput").ap()
    x1scr = nc.dram_tensor("x1scr", [T, D], F32, kind="Internal").ap()
    xn1scr = nc.dram_tensor("xn1scr", [D, T], BF16, kind="Internal").ap()
    out = nc.dram_tensor("x2", [TO, D], F32, kind="ExternalOutput").ap()

    with ExitStack() as st:
        b = alloc_bufs(nc, st, "fused")
        p = Prog(nc, st)
        emit_consts(p, b)
        p.op("sync", lambda e: e.dma_start(out=b["bqk"][:], in_=t["bqk0"]),
             writes=["bqk"], dma=True)
        p.op("sync", lambda e: e.dma_start(out=b["mselb"][:], in_=t["msel"]),
             writes=["mselb"], dma=True)
        p.op("sync", lambda e: e.dma_start(out=b["idxT"][:], in_=idxT),
             writes=["idxT"], dma=True)
        p.op("sync", lambda e: e.dma_start(out=b["idxX"][:], in_=idxX),
             writes=["idxX"], dma=True)
        emit_lnT(p, b, t["x0"], NT, b["xnT"], "xnT", T)
        p.barrier(["xnb", "sq1k"])
        emit_vb_ones(p, b)
        _attn_layer(p, b, 0, t, T, T, None)
        p.barrier(ATTN_KEYS)
        load_asm0_weights(p, b, t)
        emit_assembly_l0(p, b, t["x0"], ln1=True, x1_dram=x1scr, x1_out=None)
        dq = ["sync", "pool", "act"]
        for ch in range(ND):
            p.op(dq[ch % 3], lambda e, ch=ch: e.dma_start(
                out=xn1scr[ch * 128:(ch + 1) * 128, :],
                in_=b["xnT"][:, ch * T:(ch + 1) * T]),
                reads=["xnT"], writes=["xn1scr%d" % ch], dma=True)
        p.barrier(ASM0_KEYS + ["oT"])
        xn1flat = bass.AP(xn1scr.tensor, 0, [[1, D * T], [1, 1]])
        for ch in range(ND):
            p.op("pool", lambda e, ch=ch: e.indirect_dma_start(
                out=b["ownT"][:, ch * TO:(ch + 1) * TO], out_offset=None,
                in_=xn1flat,
                in_offset=IndirectOffsetOnAxis(b["idxT"][:, ch:ch + 1], 0)),
                reads=["xn1scr%d" % ch, "idxT"], writes=["ownT"], dma=True)
        emit_vb_ones(p, b)
        _attn_layer(p, b, 1, t, TO, TO, "ownT")
        p.barrier(ATTN_KEYS)
        p.op("pool", lambda e: e.memset(b["xnT"][:], 0.0),
             reads=["xnT"], writes=["xnT", "yacc"])
        emit_gating1_proj1(p, b, t)
        p.barrier(["wob", "cw", "gbuf", "eq1", "eq2"])
        emit_ffn1(p, b, we13, we2)
        emit_assembly_l1(p, b, ("gather", b["idxX"], x1scr), out)
        p.finish()
    return nc



def _prep_shared(w):
    g = {}
    wa = np.asarray(w["L0_attn_w_attn"], np.float32)
    qk = np.empty((D, H * 128), np.float32)
    for h in range(H):
        qk[:, h * 128:h * 128 + 64] = wa[:, D + h * 64:D + (h + 1) * 64]
        qk[:, h * 128 + 64:h * 128 + 128] = wa[:, h * 64:(h + 1) * 64]
    g["wqk0"] = qk
    g["wv0"] = np.ascontiguousarray(wa[:, 2 * D:3 * D])
    ba = np.asarray(w["L0_attn_b_attn"], np.float32)
    bqk = np.empty((128, 16), np.float32)
    for h in range(H):
        bqk[0:64, h] = ba[D + h * 64:D + (h + 1) * 64]
        bqk[64:128, h] = ba[h * 64:(h + 1) * 64]
    g["bqk0"] = bqk
    g["bv0"] = np.ascontiguousarray(ba[2 * D:].reshape(16, 64))
    g["wo0"] = np.ascontiguousarray(np.asarray(w["L0_attn_w_proj"],
                                               np.float32))
    g["bproj"] = np.ascontiguousarray(
        np.asarray(w["L0_attn_b_proj"], np.float32)[None, :])
    g["wg0"] = np.ascontiguousarray(np.asarray(w["L0_ffn_Wg"], np.float32))
    g["a0"] = np.ascontiguousarray(
        np.asarray(w["L0_ffn_A"], np.float32).transpose(1, 0, 2).reshape(
            D, 128))
    g["b0"] = np.ascontiguousarray(
        np.asarray(w["L0_ffn_B"], np.float32).reshape(128, D))
    wq = np.asarray(w["L1_attn_W_q"], np.float32)
    wk2 = np.asarray(w["L1_attn_W_k"], np.float32)
    wv2 = np.asarray(w["L1_attn_W_v"], np.float32)
    qk1 = np.empty((D, H * 128), np.float32)
    for h in range(H):
        qk1[:, h * 128:h * 128 + 64] = wk2[h]
        qk1[:, h * 128 + 64:h * 128 + 128] = wq[h]
    g["wqk1"] = qk1
    g["wv1"] = np.ascontiguousarray(wv2.transpose(1, 0, 2).reshape(D, H * 64))
    g["wo1"] = np.ascontiguousarray(np.asarray(w["L1_attn_W_o"], np.float32))
    g["wg1"] = np.ascontiguousarray(np.asarray(w["L1_ffn_Wg"], np.float32))
    s1 = np.asarray(w["L1_ffn_Ws1"], np.float32)
    s3 = np.asarray(w["L1_ffn_Ws3"], np.float32)
    s2 = np.asarray(w["L1_ffn_Ws2"], np.float32)
    e1 = np.asarray(w["L1_ffn_We1"], np.float32)
    e3 = np.asarray(w["L1_ffn_We3"], np.float32)
    e2 = np.asarray(w["L1_ffn_We2"], np.float32)

    def inter(a, bm):
        out = np.empty((D, 2 * FF), np.float32)
        for fq in range(4):
            out[:, fq * 1024:fq * 1024 + 512] = a[:, fq * 512:(fq + 1) * 512]
            out[:, fq * 1024 + 512:(fq + 1) * 1024] = \
                bm[:, fq * 512:(fq + 1) * 512]
        return out
    for i in range(NE):
        if i == 0:
            g["we13_%d" % i] = inter(s1, s3)
            g["we2_%d" % i] = np.ascontiguousarray(s2)
        else:
            g["we13_%d" % i] = inter(e1[i - 1], e3[i - 1])
            g["we2_%d" % i] = np.ascontiguousarray(e2[i - 1])
    r = (np.arange(128, dtype=np.float64)[:, None]
         - (np.arange(GW, dtype=np.float64)[None, :] - GOFF))
    gtab = np.zeros((H * 128, GW), dtype=ml_dtypes.bfloat16)
    with np.errstate(under="ignore"):
        for h in range(H):
            gh = np.where(r <= 0, np.exp(SLOPES[h] * r), 0.0)
            gtab[h * 128:(h + 1) * 128] = gh.astype(ml_dtypes.bfloat16)
    g["gtab"] = gtab
    msel = np.zeros((128, 1040), dtype=ml_dtypes.bfloat16)
    for s in range(16):
        if s < 8:
            msel[64:128, s * 65 + s] = 1
        else:
            msel[0:64, s * 65 + s] = 1
    g["msel"] = msel
    return g


FUSED = True


def kernel(**inputs):
    g = _prep_shared(inputs)
    x0 = np.ascontiguousarray(np.asarray(inputs["L0_attn_x"], np.float32))
    if FUSED:
        return kernel_fused(inputs, g, x0)

    nc0 = build_l0(x1_as_output=True)
    l0_keys = ["wqk0", "wv0", "bqk0", "bv0", "wo0", "bproj", "wg0", "a0",
               "b0", "gtab", "msel"]
    in_maps0 = []
    for c in range(8):
        m = {"x0": x0[c // 2]}
        for k in l0_keys:
            m[k] = g[k]
        in_maps0.append(m)
    r0 = run_bass_kernel_spmd(nc0, in_maps0, core_ids=list(range(8)))
    x1 = [np.asarray(r0.results[2 * bb]["x1"]) for bb in range(B)]

    nc1 = build_l1()
    l1_keys = (["wqk1", "wv1", "wo1", "wg1", "msel"] +
               ["we13_%d" % i for i in range(NE)] +
               ["we2_%d" % i for i in range(NE)])
    in_maps1 = []
    for c in range(8):
        bb, hf = c // 2, c % 2
        m = {"x1f": x1[bb],
             "x1o": np.ascontiguousarray(x1[bb][hf * TO:(hf + 1) * TO])}
        for k in l1_keys:
            m[k] = g[k]
        in_maps1.append(m)
    r1 = run_bass_kernel_spmd(nc1, in_maps1, core_ids=list(range(8)))

    out = np.empty((B, T, D), np.float32)
    for c in range(8):
        bb, hf = c // 2, c % 2
        out[bb, hf * TO:(hf + 1) * TO] = np.asarray(r1.results[c]["x2"])
    return out


def kernel_fused(inputs, g, x0):
    nc = build_fused()
    keys = (["wqk0", "wv0", "bqk0", "bv0", "wo0", "bproj", "wg0", "a0", "b0",
             "gtab", "msel", "wqk1", "wv1", "wo1", "wg1"] +
            ["we13_%d" % i for i in range(NE)] +
            ["we2_%d" % i for i in range(NE)])
    pidx = np.arange(128)
    in_maps = []
    for c in range(8):
        bb, hf = c // 2, c % 2
        own0 = hf * TO
        idxT_np = np.empty((128, 8), np.int32)
        for ch in range(ND):
            idxT_np[:, ch] = (ch * 128 + pidx) * T + own0
        idxX_np = np.empty((128, 16), np.int32)
        for i in range(NTO):
            idxX_np[:, i] = own0 + i * 128 + pidx
        m = {"x0": x0[bb], "idxT": idxT_np, "idxX": idxX_np}
        for k in keys:
            m[k] = g[k]
        in_maps.append(m)
    r = run_bass_kernel_spmd(nc, in_maps, core_ids=list(range(8)))
    out = np.empty((B, T, D), np.float32)
    for c in range(8):
        bb, hf = c // 2, c % 2
        out[bb, hf * TO:(hf + 1) * TO] = np.asarray(r.results[c]["x2"])
    return out
